# version 10 (final)
# baseline (speedup 1.0000x reference)
import numpy as np
import concourse.bass as bass
import concourse.mybir as mybir
from concourse.bass_utils import run_bass_kernel_spmd

F32 = mybir.dt.float32
BF16 = mybir.dt.bfloat16
AF = mybir.ActivationFunctionType
ALU = mybir.AluOpType
AX = mybir.AxisListType

T = 4096
D = 1024
C = 512
NCH = T // C
J = D // 128
TT = C // 128

TWO_PI = 6.283185307179586
PI = 3.141592653589793
MAGIC = 1.5 * (2.0 ** 23)
EPS_LN = 1e-5
CPOOL_VALS = [0.0, 1.0, EPS_LN, -0.5, -1.0, PI / 2, float(np.exp(-0.001)), 7.0,
              0.3, -0.3, 0.5, 2.0, 1e-6, 1e-20, 0.1, 3.0]

DEBUG_TAPS = []
PHASES = 99
P1E_LIMIT = "full"
SEQ_P1C = False
SEQ_P1D = False
SEQ_P1E = False
T_STRONG = True
SERIALIZE = False
ZERO_CARRY = False

ENGINES = ("t", "v", "s", "g", "q")


class Buf:
    __slots__ = ("name", "writer", "readers")
    def __init__(self, name):
        self.name = name
        self.writer = None
        self.readers = []


class Node:
    __slots__ = ("idx", "engine", "fn", "is_dma", "deps", "marked", "sem_val",
                 "waits", "tag", "strong", "echo_val")
    def __init__(self, idx, engine, fn, is_dma, tag, strong=False):
        self.echo_val = None
        self.idx = idx
        self.engine = engine
        self.fn = fn
        self.is_dma = is_dma
        self.deps = set()
        self.marked = False
        self.sem_val = None
        self.waits = []
        self.tag = tag
        self.strong = strong


class KB:
    def __init__(self, nc):
        self.nc = nc
        self.nodes = []
        self.bufs = {}
        self.tag = ""
        self.strong = False
        self.seq = False

    def buf(self, name):
        b = self.bufs.get(name)
        if b is None:
            b = Buf(name)
            self.bufs[name] = b
        return b

    def op(self, engine, fn, reads=(), writes=(), is_dma=False):
        n = Node(len(self.nodes), engine, fn, is_dma, self.tag,
                 strong=self.strong or self.seq)
        if self.seq and self.nodes:
            n.deps.add(self.nodes[-1])
        self._apply_barrier(n)
        for b in reads:
            if b.writer is not None:
                n.deps.add(b.writer)
        for b in writes:
            if b.writer is not None:
                n.deps.add(b.writer)
            for r in b.readers:
                n.deps.add(r)
        for b in writes:
            b.writer = n
            b.readers = []
        for b in reads:
            b.readers.append(n)
        self.nodes.append(n)
        return n

    def dma(self, queue_engine, dst_ap, src_ap, reads=(), writes=(), slow_ok=False):
        def fn(eng):
            if slow_ok:
                return eng.dma_start(dst_ap, src_ap, allow_slow_non_contiguous=True)
            return eng.dma_start(dst_ap, src_ap)
        return self.op(queue_engine, fn, reads=reads, writes=writes, is_dma=True)

    def barrier(self):
        deps = set()
        for b in self.bufs.values():
            if b.writer is not None:
                deps.add(b.writer)
            deps.update(b.readers)
        bnodes = []
        for e in ENGINES:
            n = Node(len(self.nodes), e, (lambda eng: eng.nop()), False,
                     self.tag + ":barrier", strong=True)
            n.deps = set(deps)
            self.nodes.append(n)
            bnodes.append(n)
        for b in self.bufs.values():
            b.readers = []
            b.writer = None
        self._pending_barrier = (set(bnodes), {e: False for e in ENGINES})

    _pending_barrier = None

    def _apply_barrier(self, n):
        pb = self._pending_barrier
        if pb is None:
            return
        bset, seen = pb
        if not seen[n.engine]:
            n.deps.update(bset)
            seen[n.engine] = True
            if all(seen.values()):
                self._pending_barrier = None

    def finalize_and_emit(self):
        nc = self.nc
        if SERIALIZE:
            prev = None
            for n in self.nodes:
                if prev is not None:
                    n.deps.add(prev)
                    prev.marked = True
                prev = n
        for n in self.nodes:
            for d in n.deps:
                if d.is_dma:
                    d.marked = True
                elif d.engine != n.engine:
                    d.marked = True
                elif d.strong or n.strong:
                    d.marked = True
                elif d.engine in ("v", "s", "g", "q", "t"):
                    d.marked = True
        sems = {}
        for e in ENGINES:
            sems["c" + e] = nc.alloc_semaphore(f"sem_c{e}")
            sems["d" + e] = nc.alloc_semaphore(f"sem_d{e}")
        counts = {k: 0 for k in sems}
        for n in self.nodes:
            if n.is_dma:
                counts["d" + n.engine] += 16
                n.sem_val = counts["d" + n.engine]
                counts["c" + n.engine] += 1
                n.echo_val = counts["c" + n.engine]
            elif n.marked:
                counts["c" + n.engine] += 1
                n.sem_val = counts["c" + n.engine]
        hwm = {e: {} for e in ENGINES}
        for n in self.nodes:
            needed = {}
            for d in n.deps:
                if d.is_dma:
                    key = "c" + d.engine
                    dval = d.echo_val
                elif d.engine == n.engine:
                    if d.engine == "t" and not (d.strong or n.strong or T_STRONG):
                        continue
                    key = "c" + d.engine
                    dval = d.sem_val
                else:
                    key = "c" + d.engine
                    dval = d.sem_val
                if dval is None:
                    continue
                v = needed.get(key, 0)
                if dval > v:
                    needed[key] = dval
            h = hwm[n.engine]
            for key in sorted(needed):
                val = needed[key]
                if h.get(key, 0) >= val:
                    continue
                n.waits.append((key, val))
                h[key] = val

        block_ctx = nc.Block()
        block = block_ctx.__enter__()

        def make_body(e):
            mynodes = [n for n in self.nodes if n.engine == e]
            def body(eng):
                for n in mynodes:
                    for key, val in n.waits:
                        eng.wait_ge(sems[key], val)
                    try:
                        inst = n.fn(eng)
                    except Exception as ex:
                        raise RuntimeError(f"emit failed node {n.idx} tag={n.tag}: {ex}") from ex
                    if n.sem_val is not None:
                        if n.is_dma:
                            inst.then_inc(sems["d" + n.engine], 16)
                            eng.wait_ge(sems["d" + n.engine], n.sem_val)
                            eng.nop().then_inc(sems["c" + n.engine], 1)
                        else:
                            inst.then_inc(sems["c" + n.engine], 1)
            return body

        block.tensor(make_body("t"))
        block.vector(make_body("v"))
        block.scalar(make_body("s"))
        block.gpsimd(make_body("g"))
        block.sync(make_body("q"))
        block_ctx.__exit__(None, None, None)
        return counts


def host_constants():
    c = {}
    c["IDENT_B"] = np.eye(128, dtype=np.float32)
    c["IDENT_F"] = np.eye(128, dtype=np.float32)
    c["ONES1F"] = np.ones((1, 128), dtype=np.float32)
    c["M1024"] = np.full((128, 128), 1.0 / 1024.0, dtype=np.float32)
    c["TRAMP"] = np.broadcast_to(np.arange(C, dtype=np.float32), (128, C)).copy()
    c["TRAMP4K"] = np.broadcast_to(np.arange(T, dtype=np.float32), (128, T)).copy()
    tau = np.arange(C, dtype=np.float64)
    c["E0P8"] = np.broadcast_to(np.tile(np.exp(0.001 * tau).astype(np.float32), J),
                                (128, J * C)).copy()
    c["ZEROS512"] = np.zeros((128, C), dtype=np.float32)
    c["ONESF"] = np.ones((128, C), dtype=np.float32)
    io64 = np.arange(64, dtype=np.float32)
    c["C_IOTA64"] = np.broadcast_to(io64, (128, 64)).copy()
    c["C64P"] = np.broadcast_to(io64 + 1e6, (128, 64)).copy()
    c["C_CLUST"] = np.broadcast_to((io64 // 8).astype(np.float32), (128, 64)).copy()
    io8 = np.arange(8, dtype=np.float32)
    c["C_IOTA8"] = np.broadcast_to(io8, (128, 8)).copy()
    c["C8P"] = np.broadcast_to(io8 + 1e6, (128, 8)).copy()
    io4 = np.arange(4, dtype=np.float32)
    c["C_IOTA4"] = np.broadcast_to(io4, (128, 4)).copy()
    c["C4P"] = np.broadcast_to(io4 + 1e6, (128, 4)).copy()
    c["C_GID"] = np.broadcast_to(np.repeat(np.arange(4, dtype=np.float32), 2),
                                 (128, 8)).copy()
    t64 = np.arange(T, dtype=np.float64)
    c["E4K"] = np.broadcast_to(np.exp(0.001 * t64).astype(np.float32), (128, T)).copy()
    em = np.zeros((128, 32), dtype=np.float32)
    for cc in range(8):
        for i in range(4):
            tt_ = cc * 512 + i * 128 + np.arange(128, dtype=np.float64)
            em[:, cc * 4 + i] = np.exp(-0.001 * tt_).astype(np.float32)
    c["EMTM"] = em
    gs = np.zeros((64, 8), dtype=np.float32)
    for cc in range(8):
        gs[cc * 8:(cc + 1) * 8, cc] = 1.0 / 8.0
    c["GSEL8"] = gs
    sel = np.zeros((8, 8, 128), dtype=np.float32)
    for e in range(8):
        sel[e, e, :] = 1.0
    c["SEL8"] = sel.reshape(8, 8 * 128)
    d = np.arange(D, dtype=np.float64) / (D - 1)
    scale = (0.001 ** d).astype(np.float32)
    c["SCALE_STRIP"] = scale.reshape(J, 128).T.copy()
    c["CPOOL"] = np.broadcast_to(np.array(CPOOL_VALS, dtype=np.float32), (128, len(CPOOL_VALS))).copy()
    return c

CONST_SPECS = {
    "IDENT_B": ([128, 128], BF16),
    "IDENT_F": ([128, 128], F32),
    "ONES1F": ([1, 128], F32),
    "M1024": ([128, 128], BF16),
    "TRAMP": ([128, C], F32),
    "TRAMP4K": ([128, T], F32),
    "E0P8": ([128, J * C], BF16),
    "ZEROS512": ([128, C], F32),
    "ONESF": ([128, C], F32),
    "C_IOTA64": ([128, 64], F32),
    "C64P": ([128, 64], F32),
    "C_CLUST": ([128, 64], F32),
    "C_IOTA8": ([128, 8], F32),
    "C8P": ([128, 8], F32),
    "C_IOTA4": ([128, 4], F32),
    "C4P": ([128, 4], F32),
    "C_GID": ([128, 8], F32),
    "SCALE_STRIP": ([128, J], F32),
    "GSEL8": ([64, 8], BF16),
    "SEL8": ([8, 8 * 128], F32),
    "CPOOL": ([128, 16], F32),
    "E4K": ([128, T], F32),
    "EMTM": ([128, 32], F32),
}


def build_program():
    nc = bass.Bass("TRN2", target_bir_lowering=False)
    kb = KB(nc)
    IN = {}
    def din(name, shape):
        IN[name] = nc.dram_tensor(name, list(shape), F32, kind="ExternalInput").ap()
    din("x", (T, D))
    din("L0_attn_W_in", (3, D, D)); din("L0_attn_b_in", (3, D))
    din("L0_attn_params_log", (3, D))
    din("L0_attn_W_mid", (2, D, D)); din("L0_attn_b_mid", (2, D))
    din("L0_attn_w1_w", (D, D)); din("L0_attn_w1_b", (D,))
    din("L0_attn_w2_w", (D, D)); din("L0_attn_w2_b", (D,))
    din("L0_attn_ln_g", (D,)); din("L0_attn_ln_b", (D,))
    din("L0_ffn_signatures_raw", (64, D)); din("L0_ffn_knot_values", (8,))
    din("L0_ffn_temperature", (1,))
    din("L0_ffn_compress_w1", (D, 256)); din("L0_ffn_compress_b1", (256,))
    din("L0_ffn_compress_w2", (256, 2)); din("L0_ffn_compress_b2", (2,))
    din("L0_ffn_spline_w1", (64, 2, 16)); din("L0_ffn_spline_b1", (64, 16))
    din("L0_ffn_spline_w2", (64, 16, 1)); din("L0_ffn_spline_b2", (64, 1))
    din("L0_ffn_directions", (64, D))
    din("L0_ffn_norm_gamma", (D,)); din("L0_ffn_norm_beta", (D,))
    din("L0_ffn_output_scale", (1,))
    din("L1_attn_ln_sc_w", (D,)); din("L1_attn_ln_sc_b", (D,))
    din("L1_attn_fc_w", (D, D)); din("L1_attn_fc_b", (D,))
    din("L1_attn_phz_w", (D, D)); din("L1_attn_phz_b", (D,))
    din("L1_attn_lci_re", (D,)); din("L1_attn_lci_im", (D,))
    din("L1_attn_ln_in_w", (D,)); din("L1_attn_ln_in_b", (D,))
    din("L1_attn_ln_ffn_w", (D,)); din("L1_attn_ln_ffn_b", (D,))
    din("L1_attn_fc1_w", (D, 4 * D)); din("L1_attn_fc1_b", (4 * D,))
    din("L1_attn_fc2_w", (4 * D, D)); din("L1_attn_fc2_b", (D,))
    din("L1_ffn_gate_weight", (8, D)); din("L1_ffn_e_score_correction_bias", (8,))
    din("L1_ffn_gate_proj", (8, 512, D)); din("L1_ffn_up_proj", (8, 512, D))
    din("L1_ffn_down_proj", (8, D, 512))

    CONST = {}
    for name, (shape, dt) in CONST_SPECS.items():
        CONST[name] = nc.dram_tensor("K_" + name, list(shape), dt, kind="ExternalInput").ap()
    OUT = nc.dram_tensor("out", [T, D], F32, kind="ExternalOutput").ap()
    SCR = {}
    def scratch(name, shape, dt=F32):
        kind = "ExternalOutput" if name in DEBUG_TAPS else "Internal"
        SCR[name] = nc.dram_tensor("scr_" + name, list(shape), dt, kind=kind).ap()
    scratch("xn0", (T, D))
    scratch("h0re", (J, 128, T), BF16)
    scratch("h0im", (J, 128, T), BF16)
    scratch("o0", (J, 128, T), BF16)
    scratch("s0", (T, D))
    scratch("x1", (T, D))
    scratch("xn1", (T, D))
    scratch("xmid", (T, D))
    scratch("s1", (T, D))
    scratch("xn1cm", (J, 128, T), BF16)
    scratch("xn2cm", (J, 128, T), BF16)
    scratch("xn2cmlo", (J, 128, T), BF16)
    scratch("xn3cm", (J, 128, T), BF16)
    scratch("phi1", (J, 128, T), F32)
    scratch("y1", (J, 128, T), BF16)
    scratch("h1", (J, 128, T), F32)
    scratch("dbg", (128, 64))
    scratch("dbg2", (128, 2048))
    for nm in ("dbgA", "dbgB", "dbgC", "dbgD", "dbgE"):
        scratch(nm, (128, T))
    scratch("gptc", (8, 128, J, 512), BF16)
    scratch("uptc", (8, 128, J, 512), BF16)
    scratch("dntc", (8, 128, 4, D), BF16)
    return nc, kb, IN, CONST, OUT, SCR


class Ph:
    def __init__(self, st, name):
        self.st = st
        self.nc = st["nc"]
        self.kb = st["kb"]
        self.name = name
        self._guards = []
        self.kb.tag = name
        self._vg_flip = 0

    def sb(self, name, shape, dt):
        g = self.nc.sbuf_tensor(self.name + "_" + name, shape, dt)
        t = g.__enter__()
        self._guards.append(g)
        return t, self.kb.buf(self.name + "_" + name)

    def psb(self, name, shape, dt):
        g = self.nc.psum_tensor(self.name + "_" + name, shape, dt)
        t = g.__enter__()
        self._guards.append(g)
        return t, self.kb.buf(self.name + "_" + name)

    def close(self):
        for g in reversed(self._guards):
            g.__exit__(None, None, None)

    def _mk(self, engine, method, out_ap, args, kwargs, reads, writes):
        def fn(eng):
            return getattr(eng, method)(out_ap, *args, **kwargs)
        return self.kb.op(engine, fn, reads=reads, writes=writes)

    def tt(self, engine, out_ap, a_ap, b_ap, op, rb, wb):
        return self._mk(engine, "tensor_tensor", out_ap, (a_ap, b_ap, op), {}, rb, wb)

    def ts(self, engine, out_ap, in_ap, s1, s2, op0, op1, rb, wb):
        return self._mk(engine, "tensor_scalar", out_ap, (in_ap, s1, s2, op0, op1), {}, rb, wb)

    def tss(self, engine, out_ap, in_ap, s, op, rb, wb):
        return self._mk(engine, "tensor_scalar", out_ap, (in_ap, s, None, op, ALU.bypass),
                        {}, rb, wb)

    def stt(self, engine, out_ap, in0_ap, scalar, in1_ap, op0, op1, rb, wb):
        return self._mk(engine, "scalar_tensor_tensor", out_ap,
                        (in0_ap, scalar, in1_ap, op0, op1), {}, rb, wb)

    def scan(self, engine, out_ap, d0_ap, d1_ap, init, op0, op1, rb, wb):
        return self._mk(engine, "tensor_tensor_scan", out_ap,
                        (d0_ap, d1_ap, init, op0, op1), {}, rb, wb)

    def act(self, out_ap, in_ap, func, rb, wb, bias=0.0, scale=1.0, accum_out=None):
        kw = dict(bias=bias, scale=scale)
        if accum_out is not None:
            kw["accum_out"] = accum_out
        rb = list(rb)
        if isinstance(bias, float) or isinstance(scale, float):
            rb.append(self.st["KCB"]["CPOOL"])
        return self._mk("s", "activation", out_ap, (in_ap, func), kw, rb, wb)

    def copy(self, engine, out_ap, in_ap, rb, wb):
        return self._mk(engine, "tensor_copy", out_ap, (in_ap,), {}, rb, wb)

    def red(self, out_ap, in_ap, op, rb, wb):
        return self._mk("v", "tensor_reduce", out_ap, (in_ap, AX.X, op), {}, rb, wb)

    def memset(self, engine, ap, val, wb):
        def fn(eng):
            return eng.memset(ap, val)
        return self.kb.op(engine, fn, reads=[], writes=wb)

    def mm(self, out_ap, lhsT_ap, rhs_ap, start, stop, rb, wb):
        def fn(eng):
            return eng.matmul(out_ap, lhsT_ap, rhs_ap, start=start, stop=stop)
        return self.kb.op("t", fn, reads=rb, writes=wb)

    def transpose(self, out_ap, in_ap, ident_ap, rb, wb):
        def fn(eng):
            return eng.transpose(out_ap, in_ap, ident_ap)
        return self.kb.op("t", fn, reads=rb, writes=wb)


def tm_slice(dram_ap, c):
    return dram_ap[c * C:(c + 1) * C, :].rearrange("(i p) d -> p i d", p=128)


def cm_slice(dram_ap, c):
    return dram_ap[:, :, c * C:(c + 1) * C].rearrange("j p t -> p j t")


def load_weight_cm(ph, name_sb, dram_ap, dk, dn, dt=BF16):
    t, b = ph.sb(name_sb, [128, dk, dn], dt)
    src = dram_ap.rearrange("(k p) n -> p k n", p=128)
    ph.kb.dma("g" if dt == BF16 else "q", t[:], src, writes=[b])
    return t, b


def load_strip(ph, name_sb, dram_ap, jn=J):
    t, b = ph.sb(name_sb, [128, jn], F32)
    src = dram_ap.rearrange("(j p) -> p j", p=128)
    ph.kb.dma("q", t[:], src, writes=[b], slow_ok=True)
    return t, b


def ln_stats_tm(ph, x_t, x_b, junk_t, junk_b, pref):
    sum_t, sum_b = ph.sb(pref + "_sum", [128, 8], F32)
    sq_t, sq_b = ph.sb(pref + "_sq", [128, 8], F32)
    mean_t, mean_b = ph.sb(pref + "_mean", [128, 8], F32)
    rstd_t, rstd_b = ph.sb(pref + "_rstd", [128, 8], F32)
    m2_t, m2_b = ph.sb(pref + "_m2", [128, 8], F32)
    for i in range(TT):
        ph.act(junk_t[:, 0:D], x_t[:, i, :], AF.Identity, [x_b], [junk_b, sum_b],
               accum_out=sum_t[:, i:i + 1])
        ph.act(junk_t[:, 0:D], x_t[:, i, :], AF.Square, [x_b], [junk_b, sq_b],
               accum_out=sq_t[:, i:i + 1])
    mean = mean_t[:, 0:TT]
    rstd = rstd_t[:, 0:TT]
    m2 = m2_t[:, 0:TT]
    ms_b = mean_b
    prev_strong = ph.kb.strong
    ph.kb.strong = True
    ph.tss("v", mean, sum_t[:, 0:TT], 1.0 / D, ALU.mult, [sum_b], [mean_b])
    ph.act(m2, mean, AF.Square, [mean_b], [m2_b])
    ph.stt("v", rstd, sq_t[:, 0:TT], 1.0 / D, m2, ALU.mult, ALU.subtract, [sq_b, m2_b], [rstd_b])
    if pref == "st0" and "dbg" in DEBUG_TAPS:
        scr = ph.st["SCR"]
        ph.kb.dma("q", scr["dbg"][:, 16:16 + TT], sq_t[:, 0:TT], reads=[sq_b])
        ph.kb.dma("q", scr["dbg"][:, 24:24 + TT], m2, reads=[m2_b])
        ph.kb.dma("q", scr["dbg"][:, 32:32 + TT], rstd, reads=[rstd_b])
        ph.kb.dma("q", scr["dbg"][:, 40:40 + TT], sum_t[:, 0:TT], reads=[sum_b])
    ph.act(rstd, rstd, AF.Ln, [rstd_b], [rstd_b], bias=EPS_LN)
    ph.act(rstd, rstd, AF.Exp, [rstd_b], [rstd_b], scale=-0.5)
    ph.kb.strong = prev_strong
    if pref == "st0" and "dbg" in DEBUG_TAPS:
        scr = ph.st["SCR"]
        ph.kb.dma("q", scr["dbg"][:, 4:4 + TT], rstd, reads=[rstd_b])
    return mean, rstd, (mean_b, rstd_b)


def phase_p0a1(st):
    nc, kb = st["nc"], st["kb"]
    IN, KC, KCB, SCR, PS, PSB = st["IN"], st["KC"], st["KCB"], st["SCR"], st["PS"], st["PSB"]
    ph_hold = Ph(st, "p0a1h")
    xnc_t, xnc_b = ph_hold.sb("xnc", [128, J, T], BF16)
    ph = ph_hold

    BIN, BINB, PLOG, PLOGB = [], [], [], []
    for e in range(3):
        t, b = load_strip(ph, f"bin{e}", IN["L0_attn_b_in"][e])
        BIN.append(t); BINB.append(b)
        t, b = load_strip(ph, f"plog{e}", IN["L0_attn_params_log"][e])
        PLOG.append(t); PLOGB.append(b)
    nu_t, nu_b = ph.sb("nu", [128, J], F32)
    th_t, th_b = ph.sb("theta", [128, J], F32)
    ga_t, ga_b = ph.sb("gamma", [128, J], F32)
    rho_t, rho_b = ph.sb("rho", [128, J], F32)
    ph.act(nu_t[:], PLOG[0][:], AF.Exp, [PLOGB[0]], [nu_b])
    ph.act(th_t[:], PLOG[1][:], AF.Exp, [PLOGB[1]], [th_b])
    ph.act(ga_t[:], PLOG[2][:], AF.Exp, [PLOGB[2]], [ga_b])
    ph.act(rho_t[:], nu_t[:], AF.Exp, [nu_b], [rho_b], scale=-1.0)
    gbr_t, gbr_b = ph.sb("gbr", [128, J], F32)
    gbi_t, gbi_b = ph.sb("gbi", [128, J], F32)
    ph.tt("v", gbr_t[:], ga_t[:], BIN[0][:], ALU.mult, [ga_b, BINB[0]], [gbr_b])
    ph.tt("v", gbi_t[:], ga_t[:], BIN[1][:], ALU.mult, [ga_b, BINB[1]], [gbi_b])

    ph = Ph(st, "p0a1a")
    xt_t, xt_b = ph.sb("xt", [128, TT, D], F32)
    junk_t, junk_b = ph.sb("junk", [128, D], F32)
    xnb_t, xnb_b = ph.sb("xnb", [128, TT, D], BF16)
    ptr0, ptr0b = ph.psb("ptr0", [128, 512], BF16)
    ptr1, ptr1b = ph.psb("ptr1", [128, 512], BF16)
    PTR, PTRB = [ptr0, ptr1], [ptr0b, ptr1b]
    for c in range(NCH):
        kb.dma("q", xt_t[:], tm_slice(IN["x"], c), writes=[xt_b])
        mean, rstd, (mean_b, rstd_b) = ln_stats_tm(ph, xt_t, xt_b, junk_t, junk_b, f"st{c}")
        for i in range(TT):
            ph.ts("v", xt_t[:, i, :], xt_t[:, i, :], mean[:, i:i + 1], rstd[:, i:i + 1],
                  ALU.subtract, ALU.mult, [mean_b, rstd_b, xt_b], [xt_b])
        kb.dma("q", tm_slice(SCR["xn0"], c), xt_t[:], reads=[xt_b])
        ph.act(xnb_t[:].rearrange("p i d -> p (i d)"), xt_t[:].rearrange("p i d -> p (i d)"),
               AF.Identity, [xt_b], [xnb_b])
        for j in range(J):
            bank, bankb = PTR[j % 2], PTRB[j % 2]
            for i in range(TT):
                ph.transpose(bank[:, i * 128:(i + 1) * 128], xnb_t[:, i, j * 128:(j + 1) * 128],
                             KC["IDENT_B"][:], [xnb_b, KCB["IDENT_B"]], [bankb])
            ph.copy("v", xnc_t[:, j, c * C:(c + 1) * C], bank[:], [bankb], [xnc_b])

    ph.close()
    kb.barrier()
    ph = Ph(st, "p0a1b")
    tramp4k_t, tramp4k_b = load_const_local(ph, "TRAMP4K")
    WS, WSB = [], []
    for d in range(2):
        row_t, row_b = [], []
        for e in range(3):
            t, b = ph.sb(f"ws{d}_{e}", [128, J, 128], BF16)
            row_t.append(t); row_b.append(b)
        WS.append(row_t); WSB.append(row_b)
    tcg_t, tcg_b = ph.sb("tcg", [128, T], BF16)
    tsg_t, tsg_b = ph.sb("tsg", [128, T], BF16)
    ab_t, ab_b = ph.sb("abuf", [128, T], BF16)
    bb_t, bb_b = ph.sb("bbuf", [128, T], BF16)
    wre_t, wre_b = ph.sb("wre", [128, T], BF16)
    wim_t, wim_b = ph.sb("wim", [128, T], BF16)
    gre_t, gre_b = ab_t, ab_b
    gim_t, gim_b = bb_t, bb_b
    ob2_t, ob2_b = ph.sb("ob2", [128, C], BF16)
    pvB_t, pvB_b = ph.sb("pvB", [128, T], F32)
    pgB_t, pgB_b = ph.sb("pgB", [128, T], F32)
    rhog_t, rhog_b = ph.sb("rhog", [128, T], F32)
    tta_t, tta_b = pvB_t, pvB_b
    ttk_t, ttk_b = pgB_t, pgB_b
    hre_t, hre_b = wre_t, wre_b
    him_t, him_b = wim_t, wim_b

    for m in range(J):
        d = m % 2
        for e in range(3):
            kb.dma("g", WS[d][e][:], IN["L0_attn_W_in"][e][:, m * 128:(m + 1) * 128]
                   .rearrange("(k p) n -> p k n", p=128), writes=[WSB[d][e]])
        ph.ts("v", tta_t[:], tramp4k_t[:], th_t[:, m:m + 1], None, ALU.mult, ALU.bypass,
              [tramp4k_b, th_b], [tta_b])
        ph.ts("v", ttk_t[:], tta_t[:], 1.0 / TWO_PI, MAGIC, ALU.mult, ALU.add, [tta_b], [ttk_b])
        ph.ts("v", ttk_t[:], ttk_t[:], MAGIC, TWO_PI, ALU.subtract, ALU.mult, [ttk_b], [ttk_b])
        ph.tt("v", tta_t[:], tta_t[:], ttk_t[:], ALU.subtract, [tta_b, ttk_b], [tta_b])
        ph.act(tsg_t[:], tta_t[:], AF.Sin, [tta_b], [tsg_b])
        ph.act(tta_t[:], tta_t[:], AF.Abs, [tta_b], [tta_b])
        ph.act(tcg_t[:], tta_t[:], AF.Sin, [tta_b], [tcg_b], scale=-1.0, bias=PI / 2)
        ph.ts("v", rhog_t[:], tramp4k_t[:], 0.0, rho_t[:, m:m + 1], ALU.mult, ALU.add,
              [tramp4k_b, rho_b], [rhog_b])
        for c in range(NCH):
            pr, prb = PS[0 + (c % 2) * 3], PSB[0 + (c % 2) * 3]
            pi_, pib = PS[1 + (c % 2) * 3], PSB[1 + (c % 2) * 3]
            po, pob = PS[2 + (c % 2) * 3], PSB[2 + (c % 2) * 3]
            for k in range(J):
                ph.mm(pr[:], WS[d][0][:, k, :], xnc_t[:, k, c * C:(c + 1) * C],
                      k == 0, k == J - 1, [WSB[d][0], xnc_b], [prb])
            for k in range(J):
                ph.mm(pi_[:], WS[d][1][:, k, :], xnc_t[:, k, c * C:(c + 1) * C],
                      k == 0, k == J - 1, [WSB[d][1], xnc_b], [pib])
            for k in range(J):
                ph.mm(po[:], WS[d][2][:, k, :], xnc_t[:, k, c * C:(c + 1) * C],
                      k == 0, k == J - 1, [WSB[d][2], xnc_b], [pob])
            sl = slice(c * C, (c + 1) * C)
            ph.act(ab_t[:, sl], pr[:], AF.Identity, [prb, ga_b, gbr_b], [ab_b],
                   scale=ga_t[:, m:m + 1], bias=gbr_t[:, m:m + 1])
            ph.act(bb_t[:, sl], pi_[:], AF.Identity, [pib, ga_b, gbi_b], [bb_b],
                   scale=ga_t[:, m:m + 1], bias=gbi_t[:, m:m + 1])
            ph.act(ob2_t[:], po[:], AF.Identity, [pob, BINB[2]], [ob2_b],
                   bias=BIN[2][:, m:m + 1])
            kb.dma("q", SCR["o0"][m][:, c * C:(c + 1) * C], ob2_t[:], reads=[ob2_b])
        if "dbgA" in DEBUG_TAPS and m == 0:
            kb.dma("g", SCR["dbgA"], tcg_t[:], reads=[tcg_b])
            kb.dma("g", SCR["dbgB"], tsg_t[:], reads=[tsg_b])
            kb.dma("q", SCR["dbgC"], rhog_t[:], reads=[rhog_b])
            kb.dma("g", SCR["dbgD"], ab_t[:], reads=[ab_b])
            kb.dma("g", SCR["dbgE"], bb_t[:], reads=[bb_b])
        ph.tt("v", pvB_t[:], tcg_t[:], ab_t[:], ALU.mult, [tcg_b, ab_b], [pvB_b])
        ph.tt("g", pgB_t[:], tsg_t[:], bb_t[:], ALU.mult, [tsg_b, bb_b], [pgB_b])
        ph.tt("v", wre_t[:], pvB_t[:], pgB_t[:], ALU.add, [pvB_b, pgB_b], [wre_b])
        ph.tt("g", pgB_t[:], tcg_t[:], bb_t[:], ALU.mult, [tcg_b, bb_b], [pgB_b])
        ph.tt("v", pvB_t[:], tsg_t[:], ab_t[:], ALU.mult, [tsg_b, ab_b], [pvB_b])
        ph.tt("v", wim_t[:], pgB_t[:], pvB_t[:], ALU.subtract, [pgB_b, pvB_b], [wim_b])
        ph.scan("v", gre_t[:], rhog_t[:], wre_t[:], 0.0, ALU.mult, ALU.add,
                [rhog_b, wre_b], [gre_b])
        ph.scan("v", gim_t[:], rhog_t[:], wim_t[:], 0.0, ALU.mult, ALU.add,
                [rhog_b, wim_b], [gim_b])
        ph.tt("v", pvB_t[:], tcg_t[:], gre_t[:], ALU.mult, [tcg_b, gre_b], [pvB_b])
        ph.tt("v", pgB_t[:], tsg_t[:], gim_t[:], ALU.mult, [tsg_b, gim_b], [pgB_b])
        ph.tt("v", hre_t[:], pvB_t[:], pgB_t[:], ALU.subtract, [pvB_b, pgB_b], [hre_b])
        ph.tt("v", pgB_t[:], tcg_t[:], gim_t[:], ALU.mult, [tcg_b, gim_b], [pgB_b])
        ph.tt("v", pvB_t[:], tsg_t[:], gre_t[:], ALU.mult, [tsg_b, gre_b], [pvB_b])
        ph.tt("v", him_t[:], pgB_t[:], pvB_t[:], ALU.add, [pgB_b, pvB_b], [him_b])
        kb.dma("q", SCR["h0re"][m], hre_t[:], reads=[hre_b])
        kb.dma("q", SCR["h0im"][m], him_t[:], reads=[him_b])
    ph.close()
    ph_hold.close()




def load_const_local(ph, name):
    shape, dt = CONST_SPECS[name]
    t, b = ph.sb("kc_" + name, shape, dt)
    ph.kb.dma("q", t[:], ph.st["CONST"][name], writes=[b])
    return t, b


def replicate_row(ph, name, dram_vec, n, PS, PSB):
    KC, KCB = ph.st["KC"], ph.st["KCB"]
    row_t, row_b = ph.sb(name + "_row", [1, n], F32)
    ph.kb.dma("q", row_t[:], dram_vec, writes=[row_b])
    rep_t, rep_b = ph.sb(name + "_rep", [128, n], F32)
    nhalf = (n + 511) // 512
    for h in range(nhalf):
        lo, hi = h * 512, min(n, (h + 1) * 512)
        bank, bankb = PS[5], PSB[5]
        def mk(eng, out_ap=bank[:, 0:hi - lo], l=row_t[0:1, lo:hi]):
            return eng.matmul(out_ap, KC["ONES1F"][:], l, start=True, stop=True)
        ph.kb.op("t", mk, reads=[row_b, KCB["ONES1F"]], writes=[bankb])
        ph.act(rep_t[:, lo:hi], bank[:, 0:hi - lo], AF.Identity, [bankb], [rep_b])
    return rep_t, rep_b


def phase_p0a2(st):
    nc, kb = st["nc"], st["kb"]
    IN, KC, KCB, SCR, PS, PSB = st["IN"], st["KC"], st["KCB"], st["SCR"], st["PS"], st["PSB"]
    ph = Ph(st, "p0a2")

    WM0, WM0B = load_weight_cm(ph, "wm0", IN["L0_attn_W_mid"][0], J, D)
    WM1, WM1B = load_weight_cm(ph, "wm1", IN["L0_attn_W_mid"][1], J, D)
    W1, W1B = load_weight_cm(ph, "w1", IN["L0_attn_w1_w"], J, D)
    W2, W2B = load_weight_cm(ph, "w2", IN["L0_attn_w2_w"], J, D)
    ph.ts("v", WM1[:].rearrange("p k n -> p (k n)"), WM1[:].rearrange("p k n -> p (k n)"),
          -1.0, None, ALU.mult, ALU.bypass, [WM1B], [WM1B])
    bm0_t, bm0_b = load_strip(ph, "bm0", IN["L0_attn_b_mid"][0])
    bm1_t, bm1_b = load_strip(ph, "bm1", IN["L0_attn_b_mid"][1])
    badd_t, badd_b = ph.sb("badd", [128, J], F32)
    ph.stt("v", badd_t[:], bm1_t[:], -1.0, bm0_t[:], ALU.mult, ALU.add, [bm0_b, bm1_b], [badd_b])
    w1b_t, w1b_b = load_strip(ph, "w1b", IN["L0_attn_w1_b"])
    w2b_t, w2b_b = load_strip(ph, "w2b", IN["L0_attn_w2_b"])
    lng_t, lng_b = replicate_row(ph, "lng", IN["L0_attn_ln_g"], D, PS, PSB)
    lnb_t, lnb_b = replicate_row(ph, "lnb", IN["L0_attn_ln_b"], D, PS, PSB)

    hre_t, hre_b = ph.sb("hre", [128, J, C], BF16)
    him_t, him_b = ph.sb("him", [128, J, C], BF16)
    ocm_t, ocm_b = ph.sb("ocm", [128, J, C], BF16)
    out_t, out_b = ph.sb("out", [128, J, C], F32)
    outb_t, outb_b = ph.sb("outb", [128, J, C], BF16)
    y1_t, y1_b = ph.sb("y1", [128, J, C], BF16)
    y2_t, y2_b = ph.sb("y2", [128, J, C], F32)
    y2t_t, y2t_b = ph.sb("y2t", [128, TT, D], F32)
    s0_t, s0_b = ph.sb("s0", [128, TT, D], F32)
    junk_t, junk_b = ph.sb("junk", [128, D], F32)
    ptrf0, ptrf0b = ph.psb("ptrf0", [128, 512], F32)
    ptrf1, ptrf1b = ph.psb("ptrf1", [128, 512], F32)
    PTRF, PTRFB = [ptrf0, ptrf1], [ptrf0b, ptrf1b]

    for c in range(NCH):
        kb.dma("q", hre_t[:], cm_slice(SCR["h0re"], c), writes=[hre_b])
        kb.dma("q", him_t[:], cm_slice(SCR["h0im"], c), writes=[him_b])
        kb.dma("q", ocm_t[:], cm_slice(SCR["o0"], c), writes=[ocm_b])
        for m in range(J):
            p, pb = PS[m % 2], PSB[m % 2]
            for k in range(J):
                ph.mm(p[:], WM0[:, k, m * 128:(m + 1) * 128], hre_t[:, k, :],
                      k == 0, False, [WM0B, hre_b], [pb])
            for k in range(J):
                ph.mm(p[:], WM1[:, k, m * 128:(m + 1) * 128], him_t[:, k, :],
                      False, k == J - 1, [WM1B, him_b], [pb])
            ph.stt("v", out_t[:, m, :], p[:], badd_t[:, m:m + 1], ocm_t[:, m, :],
                   ALU.add, ALU.add, [pb, badd_b, ocm_b], [out_b])
        ph.act(outb_t[:].rearrange("p k n -> p (k n)"), out_t[:].rearrange("p k n -> p (k n)"),
               AF.Identity, [out_b], [outb_b])
        for m in range(J):
            p, pb = PS[2 + m % 2], PSB[2 + m % 2]
            for k in range(J):
                ph.mm(p[:], W1[:, k, m * 128:(m + 1) * 128], outb_t[:, k, :],
                      k == 0, k == J - 1, [W1B, outb_b], [pb])
            ph.act(y1_t[:, m, :], p[:], AF.Gelu, [pb, w1b_b], [y1_b],
                   bias=w1b_t[:, m:m + 1])
        for m in range(J):
            p, pb = PS[4 + m % 2], PSB[4 + m % 2]
            for k in range(J):
                ph.mm(p[:], W2[:, k, m * 128:(m + 1) * 128], y1_t[:, k, :],
                      k == 0, k == J - 1, [W2B, y1_b], [pb])
            ph.stt("v", y2_t[:, m, :], p[:], w2b_t[:, m:m + 1], out_t[:, m, :],
                   ALU.add, ALU.add, [pb, w2b_b, out_b], [y2_b])
        for i in range(TT):
            for half in range(2):
                bank, bankb = PTRF[half], PTRFB[half]
                for mm_ in range(4):
                    m = half * 4 + mm_
                    ph.transpose(bank[:, mm_ * 128:(mm_ + 1) * 128],
                                 y2_t[:, m, i * 128:(i + 1) * 128],
                                 KC["IDENT_F"][:], [y2_b, KCB["IDENT_F"]], [bankb])
                ph.act(y2t_t[:, i, half * 512:(half + 1) * 512], bank[:], AF.Identity,
                       [bankb], [y2t_b])
        mean, rstd, (mean_b, rstd_b) = ln_stats_tm(ph, y2t_t, y2t_b, junk_t, junk_b, f"a2st{c}")
        for i in range(TT):
            ph.ts("v", y2t_t[:, i, :], y2t_t[:, i, :], mean[:, i:i + 1], rstd[:, i:i + 1],
                  ALU.subtract, ALU.mult, [mean_b, rstd_b, y2t_b], [y2t_b])
            tmpf = out_t[:].rearrange("p k n -> p (k n)")
            ph.tt("v", tmpf[:, i * D:(i + 1) * D], y2t_t[:, i, :], lng_t[:], ALU.mult,
                  [y2t_b, lng_b], [out_b])
            ph.tt("v", s0_t[:, i, :], tmpf[:, i * D:(i + 1) * D], lnb_t[:], ALU.add,
                  [out_b, lnb_b], [s0_b])
        kb.dma("q", tm_slice(SCR["s0"], c), s0_t[:], reads=[s0_b])
    ph.close()


def phase_p0b(st):
    nc, kb = st["nc"], st["kb"]
    IN, KC, KCB, SCR, PS, PSB = st["IN"], st["KC"], st["KCB"], st["SCR"], st["PS"], st["PSB"]
    ph = Ph(st, "p0b")

    sr_t, sr_b = ph.sb("sr", [64, D], F32)
    kb.dma("q", sr_t[:], IN["L0_ffn_signatures_raw"], writes=[sr_b])
    q1_t, q1_b = ph.sb("q1", [64, D], F32)
    q2_t, q2_b = ph.sb("q2", [64, D], F32)
    qb_t, qb_b = ph.sb("qb", [64, D], BF16)
    ph.ts("v", q1_t[:], sr_t[:], 0.3, None, ALU.is_gt, ALU.bypass, [sr_b], [q1_b])
    ph.ts("v", q2_t[:], sr_t[:], -0.3, None, ALU.is_lt, ALU.bypass, [sr_b], [q2_b])
    ph.tt("v", qb_t[:], q1_t[:], q2_t[:], ALU.subtract, [q1_b, q2_b], [qb_b])
    cs_t, cs_b = ph.sb("cs", [8, D], BF16)
    csp_t, csp_b = ph.sb("csp", [8, 512], F32)
    csn_t, csn_b = ph.sb("csn", [8, 512], F32)
    for half in range(2):
        bank, bankb = PS[5], PSB[5]
        def mkc(eng, o=bank[0:8, 0:512], r=qb_t[:, half * 512:(half + 1) * 512]):
            return eng.matmul(o, KC["GSEL8"][:], r, start=True, stop=True)
        kb.op("t", mkc, reads=[qb_b, KCB["GSEL8"]], writes=[bankb])
        ph.ts("v", csp_t[:], bank[0:8, 0:512], 0.0, None, ALU.is_gt, ALU.bypass,
              [bankb], [csp_b])
        ph.ts("v", csn_t[:], bank[0:8, 0:512], 0.0, None, ALU.is_lt, ALU.bypass,
              [bankb], [csn_b])
        ph.tt("v", cs_t[:, half * 512:(half + 1) * 512], csp_t[:], csn_t[:],
              ALU.subtract, [csp_b, csn_b], [cs_b])
    sig_t, sig_b = ph.sb("sigcat", [128, J, 72], BF16)
    ptb0, ptb0b = ph.psb("ptb0", [128, 512], BF16)
    for k in range(J):
        ph.transpose(ptb0[0:128, 0:8], cs_t[:, k * 128:(k + 1) * 128],
                     KC["IDENT_B"][0:8, 0:8], [cs_b, KCB["IDENT_B"]], [ptb0b])
        ph.copy("v", sig_t[:, k, 0:8], ptb0[0:128, 0:8], [ptb0b], [sig_b])
        ph.transpose(ptb0[0:128, 8:72], qb_t[:, k * 128:(k + 1) * 128],
                     KC["IDENT_B"][0:64, 0:64], [qb_b, KCB["IDENT_B"]], [ptb0b])
        ph.copy("v", sig_t[:, k, 8:72], ptb0[0:128, 8:72], [ptb0b], [sig_b])
    spl_t, spl_b = ph.sb("spl", [64, 72], BF16)
    w1flat = IN["L0_ffn_spline_w1"].rearrange("t i g -> t (i g)")
    kb.dma("g", spl_t[:, 0:32], w1flat, writes=[spl_b])
    kb.dma("g", spl_t[:, 32:48], IN["L0_ffn_spline_b1"], writes=[spl_b])
    kb.dma("g", spl_t[:, 48:64], IN["L0_ffn_spline_w2"].rearrange("t g o -> t (g o)"),
           writes=[spl_b])
    kb.dma("g", spl_t[:, 64:65], IN["L0_ffn_spline_b2"], writes=[spl_b])
    dirs_t, dirs_b = ph.sb("dirs", [64, D], BF16)
    kb.dma("g", dirs_t[:], IN["L0_ffn_directions"], writes=[dirs_b])
    CW1, CW1B = load_weight_cm(ph, "cw1", IN["L0_ffn_compress_w1"], J, 256)
    CW2, CW2B = load_weight_cm(ph, "cw2", IN["L0_ffn_compress_w2"], 2, 2)
    cb1_t, cb1_b = load_strip(ph, "cb1", IN["L0_ffn_compress_b1"], jn=2)
    row_t, row_b = ph.sb("repsrow", [1, 16], F32)
    kn_t, kn_b = ph.sb("knots", [1, 8], F32)
    kb.dma("q", kn_t[:], IN["L0_ffn_knot_values"], writes=[kn_b])
    kb.dma("q", row_t[0:1, 0:1], IN["L0_ffn_temperature"], writes=[row_b])
    kb.dma("q", row_t[0:1, 1:2], IN["L0_ffn_output_scale"], writes=[row_b])
    for tb in range(4):
        kb.dma("q", row_t[0:1, 8 + 2 * tb:10 + 2 * tb],
               IN["L0_ffn_compress_b2"], writes=[row_b])
    kb.strong = True
    ph.ts("v", row_t[0:1, 0:1], row_t[0:1, 0:1], 1e-6, None, ALU.add, ALU.bypass, [row_b], [row_b])
    def mkrec(eng, o=row_t[0:1, 0:1], i=row_t[0:1, 0:1]):
        return eng.reciprocal(o, i)
    kb.op("v", mkrec, reads=[row_b], writes=[row_b])
    ph.ts("v", row_t[0:1, 2:3], kn_t[0:1, 0:1], 1.0, None, ALU.mult, ALU.bypass, [kn_b], [row_b])
    ph.tt("v", row_t[0:1, 3:10], kn_t[0:1, 1:8], kn_t[0:1, 0:7], ALU.subtract, [kn_b], [row_b])
    kb.strong = False
    reps_t, reps_b = ph.sb("reps", [128, 16], F32)
    bank5, bank5b = PS[5], PSB[5]
    def mkrep(eng, o=bank5[:, 0:16], r=row_t[0:1, 0:16]):
        return eng.matmul(o, KC["ONES1F"][:], r, start=True, stop=True)
    kb.op("t", mkrep, reads=[row_b, KCB["ONES1F"]], writes=[bank5b])
    ph.act(reps_t[:], bank5[:, 0:16], AF.Identity, [bank5b], [reps_b])
    g0g_t, g0g_b = replicate_row(ph, "g0g", IN["L0_ffn_norm_gamma"], D, PS, PSB)
    g0b_t, g0b_b = replicate_row(ph, "g0b", IN["L0_ffn_norm_beta"], D, PS, PSB)

    xn_t, xn_b = ph.sb("xn", [128, TT, D], F32)
    xnb_t, xnb_b = ph.sb("xnb", [128, TT, D], BF16)
    xnc_t, xnc_b = ph.sb("xnc", [128, J, C], BF16)
    xt_t, xt_b = ph.sb("xt", [128, TT, D], F32)
    s0_t, s0_b = ph.sb("s0", [128, TT, D], F32)
    yb_t, yb_b = ph.sb("yb", [128, TT, D], F32)
    tmp_t, tmp_b = ph.sb("tmp", [128, TT, D], F32)
    junk_t, junk_b = ph.sb("junk", [128, D], F32)
    F_t, F_b = ph.sb("fgate", [128, 4 * 72], F32)
    VA_t, VA_b = ph.sb("va", [128, 4 * 72], F32)
    VB_t, VB_b = ph.sb("vb", [128, 4 * 72], F32)
    CL_t, CL_b = ph.sb("cl", [128, 4 * 72], F32)
    hcm_t, hcm_b = ph.sb("hcm", [128, 2, C], BF16)
    compp_t, compp_b = ph.sb("compp", [128, 8], F32)
    gath_t, gath_b = ph.sb("gath", [128, 4 * 72], F32)
    oh_t, oh_b = ph.sb("ohot", [128, TT, 64], BF16)
    ohs_t, ohs_b = ph.sb("ohs", [128, TT, 64], BF16)
    oht_t, oht_b = ph.sb("oht", [64, C], BF16)
    ohst_t, ohst_b = ph.sb("ohst", [64, C], BF16)
    mags_t, mags_b = ph.sb("mags", [128, TT], F32)
    hh_t, hh_b = ph.sb("hh", [128, TT, 16], F32)
    mgp_t, mgp_b = ph.sb("mgp", [128, TT, 16], F32)
    mxn_t, mxn_b = ph.sb("mxn", [128, TT, 2], F32)
    CA8_t, CA8_b = ph.sb("ca8", [128, TT, 8], F32)
    CB8_t, CB8_b = ph.sb("cb8", [128, TT, 8], F32)
    CA64_t, CA64_b = ph.sb("ca64", [128, TT, 64], F32)
    CB64_t, CB64_b = ph.sb("cb64", [128, TT, 64], F32)
    ptrb, ptrbb = ph.psb("ptrb", [128, 512], BF16)

    for c in range(NCH):
        kb.dma("q", xn_t[:], tm_slice(SCR["xn0"], c), writes=[xn_b])
        kb.dma("q", xt_t[:], tm_slice(IN["x"], c), writes=[xt_b])
        kb.dma("q", s0_t[:], tm_slice(SCR["s0"], c), writes=[s0_b])
        ph.act(xnb_t[:].rearrange("p i d -> p (i d)"), xn_t[:].rearrange("p i d -> p (i d)"),
               AF.Identity, [xn_b], [xnb_b])
        for j in range(J):
            bank, bankb = ptrb, ptrbb
            for i in range(TT):
                ph.transpose(bank[:, i * 128:(i + 1) * 128], xnb_t[:, i, j * 128:(j + 1) * 128],
                             KC["IDENT_B"][:], [xnb_b, KCB["IDENT_B"]], [bankb])
            ph.copy("v", xnc_t[:, j, :], bank[:], [bankb], [xnc_b])
        psc, pscb = PS[0], PSB[0]
        for i in range(TT):
            for k in range(J):
                ph.mm(psc[:, i * 72:(i + 1) * 72], xnc_t[:, k, i * 128:(i + 1) * 128],
                      sig_t[:, k, :], k == 0, k == J - 1, [xnc_b, sig_b], [pscb])
        ph.act(F_t[:], psc[:, 0:288], AF.Sigmoid, [pscb, reps_b], [F_b],
               scale=reps_t[:, 0:1])
        for jj in range(7):
            eng = "v"
            ph.ts(eng, CL_t[:], F_t[:], 7.0, float(-jj), ALU.mult, ALU.add, [F_b], [CL_b])
            ph.ts(eng, CL_t[:], CL_t[:], 1.0, 0.0, ALU.min, ALU.max, [CL_b], [CL_b])
            if jj == 0:
                ph.ts(eng, VA_t[:], CL_t[:], reps_t[:, 3:4], reps_t[:, 2:3],
                      ALU.mult, ALU.add, [CL_b, reps_b], [VA_b])
            elif jj % 2 == 1:
                ph.stt(eng, VB_t[:], CL_t[:], reps_t[:, 3 + jj:4 + jj], VA_t[:],
                       ALU.mult, ALU.add, [CL_b, reps_b, VA_b], [VB_b])
            else:
                ph.stt(eng, VA_t[:], CL_t[:], reps_t[:, 3 + jj:4 + jj], VB_t[:],
                       ALU.mult, ALU.add, [CL_b, reps_b, VB_b], [VA_b])
        VAL_t, VAL_b = VA_t, VA_b
        kb.strong = True
        def cg(i):
            return VAL_t[:, i * 72:i * 72 + 8]
        def tg(i):
            return VAL_t[:, i * 72 + 8:i * 72 + 72]
        for i in range(TT):
            ph.red(mxn_t[:, i, 0:1], cg(i), ALU.max, [VAL_b], [mxn_b])
        for i in range(TT):
            ph.ts("v", CA8_t[:, i, :], cg(i), mxn_t[:, i, 0:1], None, ALU.is_ge,
                  ALU.bypass, [VAL_b, mxn_b], [CA8_b])
        for i in range(TT):
            ph.stt("v", CB8_t[:, i, :], CA8_t[:, i, :], -1e6, KC["C8P"][:], ALU.mult,
                   ALU.add, [CA8_b, KCB["C8P"]], [CB8_b])
        for i in range(TT):
            ph.red(mxn_t[:, i, 1:2], CB8_t[:, i, :], ALU.min, [CB8_b], [mxn_b])
        for i in range(TT):
            ph.ts("v", CA64_t[:, i, :], KC["C_CLUST"][:], mxn_t[:, i, 1:2], None,
                  ALU.is_equal, ALU.bypass, [KCB["C_CLUST"], mxn_b], [CA64_b])
        for i in range(TT):
            ph.stt("v", CB64_t[:, i, :], CA64_t[:, i, :], 1e6, tg(i), ALU.mult, ALU.add,
                   [CA64_b, VAL_b], [CB64_b])
        for i in range(TT):
            ph.red(mxn_t[:, i, 0:1], CB64_t[:, i, :], ALU.max, [CB64_b], [mxn_b])
        for i in range(TT):
            ph.ts("v", CA64_t[:, i, :], CB64_t[:, i, :], mxn_t[:, i, 0:1], None,
                  ALU.is_ge, ALU.bypass, [CB64_b, mxn_b], [CA64_b])
        for i in range(TT):
            ph.stt("v", CB64_t[:, i, :], CA64_t[:, i, :], -1e6, KC["C64P"][:],
                   ALU.mult, ALU.add, [CA64_b, KCB["C64P"]], [CB64_b])
        for i in range(TT):
            ph.red(mxn_t[:, i, 1:2], CB64_t[:, i, :], ALU.min, [CB64_b], [mxn_b])
        for i in range(TT):
            ph.ts("v", oh_t[:, i, :], KC["C_IOTA64"][:], mxn_t[:, i, 1:2], None,
                  ALU.is_equal, ALU.bypass, [KCB["C_IOTA64"], mxn_b], [oh_b])
        kb.strong = False
        if "dbg2" in DEBUG_TAPS:
            for i in range(TT):
                kb.dma("g", SCR["dbg2"][:, c * 256 + i * 64:c * 256 + (i + 1) * 64],
                       oh_t[:, i, :], reads=[oh_b])
        for mh in range(2):
            p, pb = PS[1 + mh], PSB[1 + mh]
            for k in range(J):
                ph.mm(p[:], CW1[:, k, mh * 128:(mh + 1) * 128], xnc_t[:, k, :],
                      k == 0, k == J - 1, [CW1B, xnc_b], [pb])
            ph.act(hcm_t[:, mh, :], p[:], AF.Gelu, [pb, cb1_b], [hcm_b],
                   bias=cb1_t[:, mh:mh + 1])
        pcp, pcpb = PS[3], PSB[3]
        for i in range(TT):
            for mh in range(2):
                ph.mm(pcp[:, i * 2:(i + 1) * 2], hcm_t[:, mh, i * 128:(i + 1) * 128],
                      CW2[:, mh, :], mh == 0, mh == 1, [hcm_b, CW2B], [pcpb])
        ph.tt("v", compp_t[:], pcp[:, 0:8], reps_t[:, 8:16], ALU.add,
              [pcpb, reps_b], [compp_b])
        ph.act(compp_t[:], compp_t[:], AF.Tanh, [compp_b], [compp_b])
        for i in range(TT):
            ph.transpose(ptrb[0:64, i * 128:(i + 1) * 128],
                         oh_t[:, i, :], KC["IDENT_B"][:], [oh_b, KCB["IDENT_B"]], [ptrbb])
        ph.copy("v", oht_t[:], ptrb[0:64, 0:512], [ptrbb], [oht_b])
        pg_, pgb_ = PS[4], PSB[4]
        for i in range(TT):
            ph.mm(pg_[:, i * 72:i * 72 + 72], oht_t[:, i * 128:(i + 1) * 128], spl_t[:],
                  True, True, [oht_b, spl_b], [pgb_])
        ph.act(gath_t[:], pg_[:, 0:288], AF.Identity, [pgb_], [gath_b])
        kb.strong = True
        def g0(i):
            return gath_t[:, i * 72:i * 72 + 72]
        for i in range(TT):
            ph.stt("v", mgp_t[:, i, :], g0(i)[:, 0:16], compp_t[:, 2 * i:2 * i + 1],
                   g0(i)[:, 32:48], ALU.mult, ALU.add, [gath_b, compp_b], [mgp_b])
        for i in range(TT):
            ph.stt("v", hh_t[:, i, :], g0(i)[:, 16:32], compp_t[:, 2 * i + 1:2 * i + 2],
                   mgp_t[:, i, :], ALU.mult, ALU.add, [gath_b, compp_b, mgp_b], [hh_b])
        for i in range(TT):
            ph.ts("v", hh_t[:, i, :], hh_t[:, i, :], 0.0, None, ALU.max, ALU.bypass,
                  [hh_b], [hh_b])
        for i in range(TT):
            ph.tt("v", mgp_t[:, i, :], hh_t[:, i, :], g0(i)[:, 48:64], ALU.mult,
                  [hh_b, gath_b], [mgp_b])
        for i in range(TT):
            ph.act(junk_t[:, 0:16], mgp_t[:, i, :], AF.Identity, [mgp_b], [junk_b, mags_b],
                   accum_out=mags_t[:, i:i + 1])
        for i in range(TT):
            ph.ts("v", mags_t[:, i:i + 1], mags_t[:, i:i + 1], g0(i)[:, 64:65],
                  reps_t[:, 1:2], ALU.add, ALU.mult, [mags_b, gath_b, reps_b], [mags_b])
        for i in range(TT):
            ph.ts("v", ohs_t[:, i, :], oh_t[:, i, :], mags_t[:, i:i + 1], None,
                  ALU.mult, ALU.bypass, [oh_b, mags_b], [ohs_b])
        kb.strong = False
        for i in range(TT):
            ph.transpose(ptrb[0:64, i * 128:(i + 1) * 128], ohs_t[:, i, :],
                         KC["IDENT_B"][:], [ohs_b, KCB["IDENT_B"]], [ptrbb])
        ph.copy("v", ohst_t[:], ptrb[0:64, 0:512], [ptrbb], [ohst_b])
        for i in range(TT):
            p, pb = PS[5 - (i % 2)], PSB[5 - (i % 2)]
            for half in range(2):
                ph.mm(p[:], ohst_t[:, i * 128:(i + 1) * 128],
                      dirs_t[:, half * 512:(half + 1) * 512],
                      True, True, [ohst_b, dirs_b], [pb])
                ph.tt("v", yb_t[:, i, half * 512:(half + 1) * 512], p[:],
                      xn_t[:, i, half * 512:(half + 1) * 512], ALU.add,
                      [pb, xn_b], [yb_b])
        mean, rstd, (mean_b, rstd_b) = ln_stats_tm(ph, yb_t, yb_b, junk_t, junk_b, f"b0st{c}")
        for i in range(TT):
            ph.ts("v", yb_t[:, i, :], yb_t[:, i, :], mean[:, i:i + 1], rstd[:, i:i + 1],
                  ALU.subtract, ALU.mult, [mean_b, rstd_b, yb_b], [yb_b])
            ph.tt("v", tmp_t[:, i, :], yb_t[:, i, :], g0g_t[:], ALU.mult,
                  [yb_b, g0g_b], [tmp_b])
            ph.tt("v", yb_t[:, i, :], tmp_t[:, i, :], g0b_t[:], ALU.add,
                  [tmp_b, g0b_b], [yb_b])
        ph.act(yb_t[:].rearrange("p i d -> p (i d)"), yb_t[:].rearrange("p i d -> p (i d)"),
               AF.Silu, [yb_b], [yb_b])
        ph.tt("v", tmp_t[:].rearrange("p i d -> p (i d)"),
              s0_t[:].rearrange("p i d -> p (i d)"),
              yb_t[:].rearrange("p i d -> p (i d)"), ALU.mult, [s0_b, yb_b], [tmp_b])
        ph.tt("v", s0_t[:].rearrange("p i d -> p (i d)"),
              xt_t[:].rearrange("p i d -> p (i d)"),
              tmp_t[:].rearrange("p i d -> p (i d)"), ALU.add, [xt_b, tmp_b], [s0_b])
        kb.dma("q", tm_slice(SCR["x1"], c), s0_t[:], reads=[s0_b])
        mean, rstd, (mean_b, rstd_b) = ln_stats_tm(ph, s0_t, s0_b, junk_t, junk_b, f"b1st{c}")
        for i in range(TT):
            ph.ts("v", s0_t[:, i, :], s0_t[:, i, :], mean[:, i:i + 1], rstd[:, i:i + 1],
                  ALU.subtract, ALU.mult, [mean_b, rstd_b, s0_b], [s0_b])
        kb.dma("q", tm_slice(SCR["xn1"], c), s0_t[:], reads=[s0_b])
    ph.close()


def affine_ln_tm(ph, x_t, x_b, out_t, out_b, tmp_t, tmp_b, mean, rstd, mean_b, rstd_b,
                 w_t, w_b, b_t, b_b):
    for i in range(TT):
        ph.ts("v", tmp_t[:, i, :], x_t[:, i, :], mean[:, i:i + 1], rstd[:, i:i + 1],
              ALU.subtract, ALU.mult, [mean_b, rstd_b, x_b], [tmp_b])
        ph.tt("v", tmp_t[:, i, :], tmp_t[:, i, :], w_t[:], ALU.mult, [tmp_b, w_b], [tmp_b])
        ph.tt("v", out_t[:, i, :], tmp_t[:, i, :], b_t[:], ALU.add, [tmp_b, b_b], [out_b])


def phase_p1a(st):
    nc, kb = st["nc"], st["kb"]
    IN, KC, KCB, SCR, PS, PSB = st["IN"], st["KC"], st["KCB"], st["SCR"], st["PS"], st["PSB"]
    ph = Ph(st, "p1a")
    lscw_t, lscw_b = replicate_row(ph, "lscw", IN["L1_attn_ln_sc_w"], D, PS, PSB)
    lscb_t, lscb_b = replicate_row(ph, "lscb", IN["L1_attn_ln_sc_b"], D, PS, PSB)
    xt_t, xt_b = ph.sb("xt", [128, TT, D], F32)
    x2_t, x2_b = ph.sb("x2", [128, TT, D], F32)
    xlo_t, xlo_b = ph.sb("xlo", [128, TT, D], F32)
    junk_t, junk_b = ph.sb("junk", [128, D], F32)
    xb_t, xb_b = ph.sb("xb", [128, TT, D], BF16)
    x1b_t, x1b_b = ph.sb("x1b", [128, TT, D], BF16)
    xlob_t, xlob_b = ph.sb("xlob", [128, TT, D], BF16)
    cm2_t, cm2_b = ph.sb("cm2", [128, J, C], BF16)
    cm1_t, cm1_b = ph.sb("cm1", [128, J, C], BF16)
    cml_t, cml_b = ph.sb("cml", [128, J, C], BF16)
    ptr0, ptr0b = ph.psb("ptr0", [128, 512], BF16)
    ptr1, ptr1b = ph.psb("ptr1", [128, 512], BF16)
    for c in range(NCH):
        kb.dma("q", xt_t[:], tm_slice(SCR["xn1"], c), writes=[xt_b])
        mean, rstd, (mean_b, rstd_b) = ln_stats_tm(ph, xt_t, xt_b, junk_t, junk_b, f"p1a{c}")
        affine_ln_tm(ph, xt_t, xt_b, x2_t, x2_b, x2_t, x2_b, mean, rstd, mean_b, rstd_b,
                     lscw_t, lscw_b, lscb_t, lscb_b)
        ph.act(xb_t[:].rearrange("p i d -> p (i d)"), x2_t[:].rearrange("p i d -> p (i d)"),
               AF.Identity, [x2_b], [xb_b])
        ph.act(xlo_t[:].rearrange("p i d -> p (i d)"), xb_t[:].rearrange("p i d -> p (i d)"),
               AF.Identity, [xb_b], [xlo_b])
        ph.tt("v", xlo_t[:].rearrange("p i d -> p (i d)"),
              x2_t[:].rearrange("p i d -> p (i d)"),
              xlo_t[:].rearrange("p i d -> p (i d)"), ALU.subtract, [x2_b, xlo_b], [xlo_b])
        ph.act(xlob_t[:].rearrange("p i d -> p (i d)"), xlo_t[:].rearrange("p i d -> p (i d)"),
               AF.Identity, [xlo_b], [xlob_b])
        ph.act(x1b_t[:].rearrange("p i d -> p (i d)"), xt_t[:].rearrange("p i d -> p (i d)"),
               AF.Identity, [xt_b], [x1b_b])
        for j in range(J):
            for i in range(TT):
                ph.transpose(ptr0[:, i * 128:(i + 1) * 128], xb_t[:, i, j * 128:(j + 1) * 128],
                             KC["IDENT_B"][:], [xb_b, KCB["IDENT_B"]], [ptr0b])
                ph.transpose(ptr1[:, i * 128:(i + 1) * 128], x1b_t[:, i, j * 128:(j + 1) * 128],
                             KC["IDENT_B"][:], [x1b_b, KCB["IDENT_B"]], [ptr1b])
            ph.copy("v", cm2_t[:, j, :], ptr0[:], [ptr0b], [cm2_b])
            ph.copy("v", cm1_t[:, j, :], ptr1[:], [ptr1b], [cm1_b])
            for i in range(TT):
                ph.transpose(ptr0[:, i * 128:(i + 1) * 128], xlob_t[:, i, j * 128:(j + 1) * 128],
                             KC["IDENT_B"][:], [xlob_b, KCB["IDENT_B"]], [ptr0b])
            ph.copy("v", cml_t[:, j, :], ptr0[:], [ptr0b], [cml_b])
        kb.dma("q", cm_slice(SCR["xn2cm"], c), cm2_t[:], reads=[cm2_b])
        kb.dma("q", cm_slice(SCR["xn1cm"], c), cm1_t[:], reads=[cm1_b])
        kb.dma("q", cm_slice(SCR["xn2cmlo"], c), cml_t[:], reads=[cml_b])
    ph.close()


def phase_p1b(st):
    nc, kb = st["nc"], st["kb"]
    IN, KC, KCB, SCR, PS, PSB = st["IN"], st["KC"], st["KCB"], st["SCR"], st["PS"], st["PSB"]
    ph = Ph(st, "p1bA")
    xnc_t, xnc_b = ph.sb("xnc", [128, J, T], BF16)
    xncl_t, xncl_b = ph.sb("xncl", [128, J, T], BF16)
    for j in range(J):
        kb.dma("q", xnc_t[:, j, :], SCR["xn2cm"][j], writes=[xnc_b])
        kb.dma("q", xncl_t[:, j, :], SCR["xn2cmlo"][j], writes=[xncl_b])
    FCW, FCWB = load_weight_cm(ph, "fcw", IN["L1_attn_fc_w"], J, D)
    fcb_t, fcb_b = load_strip(ph, "fcb", IN["L1_attn_fc_b"])
    phb_t, phb_b = load_strip(ph, "phb", IN["L1_attn_phz_b"])
    scb_t, scb_b = ph.sb("scb", [128, J], F32)
    ph.tt("v", scb_t[:], phb_t[:], KC["SCALE_STRIP"][:], ALU.mult,
          [phb_b, KCB["SCALE_STRIP"]], [scb_b])
    ybf_t, ybf_b = ph.sb("ybf", [128, C], BF16)
    phi_t, phi_b = ph.sb("phi", [128, T], F32)
    for m in range(J):
        for c in range(NCH):
            p, pb = PS[c % 2], PSB[c % 2]
            for k in range(J):
                ph.mm(p[:], FCW[:, k, m * 128:(m + 1) * 128], xnc_t[:, k, c * C:(c + 1) * C],
                      k == 0, k == J - 1, [FCWB, xnc_b], [pb])
            ph.act(ybf_t[:], p[:], AF.Silu, [pb, fcb_b], [ybf_b], bias=fcb_t[:, m:m + 1])
            kb.dma("q", SCR["y1"][m][:, c * C:(c + 1) * C], ybf_t[:], reads=[ybf_b])
    phsrc = IN["L1_attn_phz_w"].rearrange("(k p) n -> p k n", p=128)
    PZF, WHI, WLO, WHF = [], [], [], []
    for d in range(2):
        PZF.append(ph.sb(f"pzf{d}", [128, J, 128], F32))
        WHI.append(ph.sb(f"whi{d}", [128, J, 128], BF16))
        WLO.append(ph.sb(f"wlo{d}", [128, J, 128], BF16))
        WHF.append(ph.sb(f"whf{d}", [128, J, 128], F32))
    for m in range(J):
        d = m % 2
        pzf_t, pzf_b = PZF[d]; whi_t, whi_b = WHI[d]
        wlo_t, wlo_b = WLO[d]; whf_t, whf_b = WHF[d]
        kb.dma("q", pzf_t[:], phsrc[:, :, m * 128:(m + 1) * 128], writes=[pzf_b])
        ph.act(whi_t[:].rearrange("p k n -> p (k n)"), pzf_t[:].rearrange("p k n -> p (k n)"),
               AF.Identity, [pzf_b], [whi_b])
        ph.act(whf_t[:].rearrange("p k n -> p (k n)"), whi_t[:].rearrange("p k n -> p (k n)"),
               AF.Identity, [whi_b], [whf_b])
        ph.tt("v", wlo_t[:].rearrange("p k n -> p (k n)"), pzf_t[:].rearrange("p k n -> p (k n)"),
              whf_t[:].rearrange("p k n -> p (k n)"), ALU.subtract, [pzf_b, whf_b], [wlo_b])
        for c in range(NCH):
            p2, p2b = PS[2 + c % 2], PSB[2 + c % 2]
            for k in range(J):
                ph.mm(p2[:], whi_t[:, k, :], xnc_t[:, k, c * C:(c + 1) * C],
                      k == 0, False, [whi_b, xnc_b], [p2b])
            for k in range(J):
                ph.mm(p2[:], whi_t[:, k, :], xncl_t[:, k, c * C:(c + 1) * C],
                      False, False, [whi_b, xncl_b], [p2b])
            for k in range(J):
                ph.mm(p2[:], wlo_t[:, k, :], xnc_t[:, k, c * C:(c + 1) * C],
                      False, k == J - 1, [wlo_b, xnc_b], [p2b])
            ph.act(phi_t[:, c * C:(c + 1) * C], p2[:], AF.Identity, [p2b, scb_b, KCB["SCALE_STRIP"]],
                   [phi_b], scale=KC["SCALE_STRIP"][:, m:m + 1], bias=scb_t[:, m:m + 1])
        kb.dma("q", SCR["phi1"][m], phi_t[:], reads=[phi_b])
    ph.close()
    kb.barrier()
    ph = Ph(st, "p1bB")
    e4k_t, e4k_b = load_const_local(ph, "E4K")
    zb_t, zb_b = ph.sb("zeros", [128, T], F32)
    ph.memset("v", zb_t[:], 0.0, [zb_b])
    lre_t, lre_b = load_strip(ph, "lre", IN["L1_attn_lci_re"])
    lim_t, lim_b = load_strip(ph, "lim", IN["L1_attn_lci_im"])
    A_t, A_b = ph.sb("bufA", [128, T], F32)
    B_t, B_b = ph.sb("bufB", [128, T], F32)
    C_t, C_b = ph.sb("bufC", [128, T], F32)
    D_t, D_b = ph.sb("bufD", [128, T], F32)
    E_t, E_b = ph.sb("bufE", [128, T], F32)
    PH, PHB = [], []
    XB, XBB = [], []
    for d in range(2):
        t, b = ph.sb(f"phin{d}", [128, T], F32); PH.append(t); PHB.append(b)
        t, b = ph.sb(f"xin{d}", [128, T], BF16); XB.append(t); XBB.append(b)
    hbf_t, hbf_b = ph.sb("hbf", [128, T], F32)
    p08_t, p08_b = ph.sb("p08", [128, J], F32)
    kb.dma("q", p08_t[:], SCR["phi1"][:, :, 0:1].rearrange("j p o -> p (j o)"),
           writes=[p08_b], slow_ok=True)
    k8_t, k8_b = ph.sb("k8", [128, J], F32)
    k8b_t, k8b_b = ph.sb("k8b", [128, J], F32)
    r8_t, r8_b = ph.sb("r8", [128, J], F32)
    s8_t, s8_b = ph.sb("s8", [128, J], F32)
    c8_t, c8_b = ph.sb("c8", [128, J], F32)
    la_t, la_b = ph.sb("la8", [128, J], F32)
    lb_t, lb_b = ph.sb("lb8", [128, J], F32)
    u_t, u_b = ph.sb("u8", [128, 4, J], F32)
    kb.strong = True
    ph.ts("v", k8_t[:], p08_t[:], 1.0 / TWO_PI, MAGIC, ALU.mult, ALU.add, [p08_b], [k8_b])
    ph.tss("v", lre_t[:], lre_t[:], float(np.exp(-0.001)), ALU.mult, [lre_b], [lre_b])
    ph.ts("v", k8b_t[:], k8_t[:], MAGIC, TWO_PI, ALU.subtract, ALU.mult, [k8_b], [k8b_b])
    ph.tss("v", lim_t[:], lim_t[:], float(np.exp(-0.001)), ALU.mult, [lim_b], [lim_b])
    ph.tt("v", r8_t[:], p08_t[:], k8b_t[:], ALU.subtract, [p08_b, k8b_b], [r8_b])
    ph.act(s8_t[:], r8_t[:], AF.Sin, [r8_b], [s8_b])
    ph.act(r8_t[:], r8_t[:], AF.Abs, [r8_b], [r8_b])
    ph.act(c8_t[:], r8_t[:], AF.Sin, [r8_b], [c8_b], scale=-1.0, bias=PI / 2)
    ph.tt("v", u_t[:, 0, :], c8_t[:], lre_t[:], ALU.mult, [c8_b, lre_b], [u_b])
    ph.tt("v", u_t[:, 1, :], s8_t[:], lim_t[:], ALU.mult, [s8_b, lim_b], [u_b])
    ph.tt("v", u_t[:, 2, :], s8_t[:], lre_t[:], ALU.mult, [s8_b, lre_b], [u_b])
    ph.tt("v", u_t[:, 3, :], c8_t[:], lim_t[:], ALU.mult, [c8_b, lim_b], [u_b])
    ph.tt("v", la_t[:], u_t[:, 0, :], u_t[:, 1, :], ALU.subtract, [u_b], [la_b])
    ph.tt("v", lb_t[:], u_t[:, 2, :], u_t[:, 3, :], ALU.add, [u_b], [lb_b])
    kb.strong = False
    if "dbgA" in DEBUG_TAPS:
        kb.dma("q", SCR["dbg"][:, 0:8], la_t[:], reads=[la_b])
        kb.dma("q", SCR["dbg"][:, 8:16], lb_t[:], reads=[lb_b])
        kb.dma("q", SCR["dbg"][:, 16:24], p08_t[:], reads=[p08_b])
        kb.dma("q", SCR["dbg"][:, 24:32], s8_t[:], reads=[s8_b])
        kb.dma("q", SCR["dbg"][:, 32:40], c8_t[:], reads=[c8_b])
        kb.dma("q", SCR["dbg"][:, 40:48], lre_t[:], reads=[lre_b])
        kb.dma("q", SCR["dbg"][:, 48:56], lim_t[:], reads=[lim_b])
    for m in range(J):
        d = m % 2
        kb.dma("q", PH[d][:], SCR["phi1"][m], writes=[PHB[d]])
        kb.dma("q", XB[d][:], SCR["xn2cm"][m], writes=[XBB[d]])
        ph.scan("v", B_t[:], zb_t[:], PH[d][:], 0.0, ALU.add, ALU.add,
                [zb_b, PHB[d]], [B_b])
        ph.ts("v", B_t[:], B_t[:], p08_t[:, m:m + 1], None, ALU.subtract, ALU.bypass,
              [B_b, p08_b], [B_b])
        ph.ts("v", A_t[:], B_t[:], 1.0 / TWO_PI, MAGIC, ALU.mult, ALU.add, [B_b], [A_b])
        ph.ts("v", A_t[:], A_t[:], MAGIC, TWO_PI, ALU.subtract, ALU.mult, [A_b], [A_b])
        ph.tt("v", B_t[:], B_t[:], A_t[:], ALU.subtract, [B_b, A_b], [B_b])
        if "dbgA" in DEBUG_TAPS and m == 0:
            kb.dma("q", SCR["dbgE"], B_t[:], reads=[B_b])
        ph.act(A_t[:], B_t[:], AF.Sin, [B_b], [A_b])
        ph.act(B_t[:], B_t[:], AF.Abs, [B_b], [B_b])
        ph.act(C_t[:], B_t[:], AF.Sin, [B_b], [C_b], scale=-1.0, bias=PI / 2)
        if "dbgA" in DEBUG_TAPS and m == 0:
            kb.dma("q", SCR["dbgA"], A_t[:], reads=[A_b])
            kb.dma("q", SCR["dbgB"], C_t[:], reads=[C_b])
        ph.act(D_t[:], XB[d][:], AF.Identity, [XBB[d]], [D_b])
        ph.tt("v", D_t[:], D_t[:], e4k_t[:], ALU.mult, [D_b, e4k_b], [D_b])
        ph.tt("g", E_t[:], D_t[:], C_t[:], ALU.mult, [D_b, C_b], [E_b])
        ph.tt("g", D_t[:], D_t[:], A_t[:], ALU.mult, [D_b, A_b], [D_b])
        ph.scan("v", B_t[:], zb_t[:], E_t[:], 0.0, ALU.add, ALU.add, [zb_b, E_b], [B_b])
        ph.scan("v", E_t[:], zb_t[:], D_t[:], 0.0, ALU.add, ALU.add, [zb_b, D_b], [E_b])
        if "dbgA" in DEBUG_TAPS and m == 0:
            kb.dma("q", SCR["dbgC"], B_t[:], reads=[B_b])
            kb.dma("q", SCR["dbgD"], E_t[:], reads=[E_b])
        ph.ts("v", B_t[:], B_t[:], la_t[:, m:m + 1], None, ALU.add, ALU.bypass,
              [B_b, la_b], [B_b])
        ph.ts("v", E_t[:], E_t[:], lb_t[:, m:m + 1], None, ALU.subtract, ALU.bypass,
              [E_b, lb_b], [E_b])
        ph.tt("v", B_t[:], C_t[:], B_t[:], ALU.mult, [C_b, B_b], [B_b])
        ph.tt("g", E_t[:], A_t[:], E_t[:], ALU.mult, [A_b, E_b], [E_b])
        ph.tt("v", hbf_t[:], B_t[:], E_t[:], ALU.add, [B_b, E_b], [hbf_b])
        kb.dma("q", SCR["h1"][m], hbf_t[:], reads=[hbf_b])
    ph.close()


def phase_p1c(st):
    nc, kb = st["nc"], st["kb"]
    IN, KC, KCB, SCR, PS, PSB = st["IN"], st["KC"], st["KCB"], st["SCR"], st["PS"], st["PSB"]
    ph = Ph(st, "p1c")
    emtm_t, emtm_b = load_const_local(ph, "EMTM")
    lniw_t, lniw_b = replicate_row(ph, "lniw", IN["L1_attn_ln_in_w"], D, PS, PSB)
    lnib_t, lnib_b = replicate_row(ph, "lnib", IN["L1_attn_ln_in_b"], D, PS, PSB)
    lfw_t, lfw_b = replicate_row(ph, "lfw", IN["L1_attn_ln_ffn_w"], D, PS, PSB)
    lfb_t, lfb_b = replicate_row(ph, "lfb", IN["L1_attn_ln_ffn_b"], D, PS, PSB)
    hcm_t, hcm_b = ph.sb("hcm", [128, J, C], F32)
    ycm_t, ycm_b = ph.sb("ycm", [128, J, C], BF16)
    htm_t, htm_b = ph.sb("htm", [128, TT, D], F32)
    ytm_t, ytm_b = ph.sb("ytm", [128, TT, D], F32)
    xn1_t, xn1_b = ph.sb("xn1", [128, TT, D], F32)
    xm_t, xm_b = ph.sb("xm", [128, TT, D], F32)
    x3_t, x3_b = ph.sb("x3", [128, TT, D], F32)
    x3b_t, x3b_b = ph.sb("x3b", [128, TT, D], BF16)
    cm3_t, cm3_b = ph.sb("cm3", [128, J, C], BF16)
    junk_t, junk_b = ph.sb("junk", [128, D], F32)
    ptr0, ptr0b = ph.psb("ptr0", [128, 512], BF16)
    ptr1, ptr1b = ph.psb("ptr1", [128, 512], BF16)
    for c in range(NCH):
        kb.dma("q", hcm_t[:], cm_slice(SCR["h1"], c), writes=[hcm_b])
        kb.dma("q", ycm_t[:], cm_slice(SCR["y1"], c), writes=[ycm_b])
        kb.dma("q", xn1_t[:], tm_slice(SCR["xn1"], c), writes=[xn1_b])
        for j in range(J):
            bank, bankb = (PS[0], PSB[0]) if j % 2 == 0 else (PS[1], PSB[1])
            for i in range(TT):
                ph.transpose(bank[:, i * 128:(i + 1) * 128], hcm_t[:, j, i * 128:(i + 1) * 128],
                             KC["IDENT_F"][:], [hcm_b, KCB["IDENT_F"]], [bankb])
            for i in range(TT):
                ph.act(htm_t[:, i, j * 128:(j + 1) * 128], bank[:, i * 128:(i + 1) * 128],
                       AF.Identity, [bankb, emtm_b], [htm_b],
                       scale=emtm_t[:, c * 4 + i:c * 4 + i + 1])
        for j in range(J):
            bank, bankb = (ptr0, ptr0b) if j % 2 == 0 else (ptr1, ptr1b)
            for i in range(TT):
                ph.transpose(bank[:, i * 128:(i + 1) * 128], ycm_t[:, j, i * 128:(i + 1) * 128],
                             KC["IDENT_B"][:], [ycm_b, KCB["IDENT_B"]], [bankb])
            for i in range(TT):
                ph.act(ytm_t[:, i, j * 128:(j + 1) * 128], bank[:, i * 128:(i + 1) * 128],
                       AF.Identity, [bankb], [ytm_b])
        mean, rstd, (mean_b, rstd_b) = ln_stats_tm(ph, htm_t, htm_b, junk_t, junk_b, f"c1{c}")
        affine_ln_tm(ph, htm_t, htm_b, htm_t, htm_b, x3_t, x3_b, mean, rstd, mean_b, rstd_b,
                     lniw_t, lniw_b, lnib_t, lnib_b)
        ph.tt("v", xm_t[:].rearrange("p i d -> p (i d)"),
              htm_t[:].rearrange("p i d -> p (i d)"),
              ytm_t[:].rearrange("p i d -> p (i d)"), ALU.mult, [htm_b, ytm_b], [xm_b])
        ph.tt("v", xm_t[:].rearrange("p i d -> p (i d)"),
              xm_t[:].rearrange("p i d -> p (i d)"),
              xn1_t[:].rearrange("p i d -> p (i d)"), ALU.add, [xm_b, xn1_b], [xm_b])
        kb.dma("q", tm_slice(SCR["xmid"], c), xm_t[:], reads=[xm_b])
        mean, rstd, (mean_b, rstd_b) = ln_stats_tm(ph, xm_t, xm_b, junk_t, junk_b, f"c2{c}")
        affine_ln_tm(ph, xm_t, xm_b, x3_t, x3_b, x3_t, x3_b, mean, rstd, mean_b, rstd_b,
                     lfw_t, lfw_b, lfb_t, lfb_b)
        ph.act(x3b_t[:].rearrange("p i d -> p (i d)"), x3_t[:].rearrange("p i d -> p (i d)"),
               AF.Identity, [x3_b], [x3b_b])
        for j in range(J):
            bank, bankb = (ptr0, ptr0b) if j % 2 == 0 else (ptr1, ptr1b)
            for i in range(TT):
                ph.transpose(bank[:, i * 128:(i + 1) * 128], x3b_t[:, i, j * 128:(j + 1) * 128],
                             KC["IDENT_B"][:], [x3b_b, KCB["IDENT_B"]], [bankb])
            ph.copy("v", cm3_t[:, j, :], bank[:], [bankb], [cm3_b])
        kb.dma("q", cm_slice(SCR["xn3cm"], c), cm3_t[:], reads=[cm3_b])
    ph.close()


def phase_p1d(st):
    nc, kb = st["nc"], st["kb"]
    IN, KC, KCB, SCR, PS, PSB = st["IN"], st["KC"], st["KCB"], st["SCR"], st["PS"], st["PSB"]
    ph = Ph(st, "p1d")
    F1_t, F1_b = ph.sb("f1w", [128, J, 4 * D], BF16)
    src1 = IN["L1_attn_fc1_w"].rearrange("(k p) n -> p k n", p=128)
    for k in range(J):
        kb.dma("g", F1_t[:, k, :], src1[:, k, :], writes=[F1_b])
    src2 = IN["L1_attn_fc2_w"].rearrange("(k p) n -> p k n", p=128)
    F2S = []
    for d in range(2):
        F2S.append(ph.sb(f"f2s{d}", [128, 32, 128], BF16))
    f1b_t, f1b_b = load_strip(ph, "f1b", IN["L1_attn_fc1_b"], jn=32)
    f2b_t, f2b_b = load_strip(ph, "f2b", IN["L1_attn_fc2_b"])
    X3, X3B = [], []
    for d in range(2):
        t, b = ph.sb(f"x3s{d}", [128, J, C], BF16); X3.append(t); X3B.append(b)
    ff1_t, ff1_b = ph.sb("ff1", [128, 32, C], BF16)
    ffc_t, ffc_b = ph.sb("ffc", [128, C], F32)
    xm_t, xm_b = ph.sb("xm", [128, TT, D], F32)
    s1_t, s1_b = ph.sb("s1", [128, TT, D], F32)
    for tb in range(NCH):
        d = tb % 2
        kb.dma("q", X3[d][:], cm_slice(SCR["xn3cm"], tb), writes=[X3B[d]])
        kb.dma("q", xm_t[:], tm_slice(SCR["xmid"], tb), writes=[xm_b])
        for mm in range(32):
            p, pb = PS[mm % 2], PSB[mm % 2]
            for k in range(J):
                ph.mm(p[:], F1_t[:, k, mm * 128:(mm + 1) * 128], X3[d][:, k, :],
                      k == 0, k == J - 1, [F1_b, X3B[d]], [pb])
            ph.act(ff1_t[:, mm, :], p[:], AF.Silu, [pb, f1b_b], [ff1_b],
                   bias=f1b_t[:, mm:mm + 1])
        for mo in range(J):
            f2_t, f2_b = F2S[mo % 2]
            for k8 in range(4):
                kb.dma("g", f2_t[:, k8 * 8:(k8 + 1) * 8, :],
                       src2[:, k8 * 8:(k8 + 1) * 8, mo * 128:(mo + 1) * 128], writes=[f2_b])
            p, pb = PS[2 + mo % 2], PSB[2 + mo % 2]
            for mm in range(32):
                ph.mm(p[:], f2_t[:, mm, :], ff1_t[:, mm, :],
                      mm == 0, mm == 31, [f2_b, ff1_b], [pb])
            ph.act(ffc_t[:], p[:], AF.Identity, [pb, f2b_b], [ffc_b],
                   bias=f2b_t[:, mo:mo + 1])
            bank, bankb = PS[4 + mo % 2], PSB[4 + mo % 2]
            for i in range(TT):
                ph.transpose(bank[:, i * 128:(i + 1) * 128], ffc_t[:, i * 128:(i + 1) * 128],
                             KC["IDENT_F"][:], [ffc_b, KCB["IDENT_F"]], [bankb])
            for i in range(TT):
                ph.tt("v", s1_t[:, i, mo * 128:(mo + 1) * 128], bank[:, i * 128:(i + 1) * 128],
                      xm_t[:, i, mo * 128:(mo + 1) * 128], ALU.add, [bankb, xm_b], [s1_b])
        kb.dma("q", tm_slice(SCR["s1"], tb), s1_t[:], reads=[s1_b])
    ph.close()


def phase_p1e(st):
    nc, kb = st["nc"], st["kb"]
    IN, KC, KCB, SCR, PS, PSB = st["IN"], st["KC"], st["KCB"], st["SCR"], st["PS"], st["PSB"]
    OUT = st["OUT"]
    pha = Ph(st, "p1ew")
    wl_t, wl_b = pha.sb("wl", [128, 4, D], BF16)
    stg_t, stg_b = pha.sb("stg", [128, J, C], BF16)
    dnl_t, dnl_b = pha.sb("dnl", [128, J, C], BF16)
    dstg_t, dstg_b = pha.sb("dstg", [128, 4, D], BF16)
    ptw, ptwb = pha.psb("ptw", [128, 512], BF16)
    for e in range(8):
        for which, dst in (("L1_ffn_gate_proj", SCR["gptc"]), ("L1_ffn_up_proj", SCR["uptc"])):
            kb.dma("g", wl_t[:], IN[which][e].rearrange("(ic p) d -> p ic d", p=128),
                   writes=[wl_b])
            for ic in range(4):
                for dk in range(J):
                    pha.transpose(ptw[:, (dk % 4) * 128:(dk % 4 + 1) * 128],
                                  wl_t[:, ic, dk * 128:(dk + 1) * 128],
                                  KC["IDENT_B"][:], [wl_b, KCB["IDENT_B"]], [ptwb])
                    pha.copy("v", stg_t[:, dk, ic * 128:(ic + 1) * 128],
                             ptw[:, (dk % 4) * 128:(dk % 4 + 1) * 128], [ptwb], [stg_b])
            kb.dma("q", dst[e], stg_t[:], reads=[stg_b])
        kb.dma("g", dnl_t[:], IN["L1_ffn_down_proj"][e].rearrange("(hk p) i -> p hk i", p=128),
               writes=[dnl_b])
        for hk in range(J):
            for ic in range(4):
                pha.transpose(ptw[:, ic * 128:(ic + 1) * 128],
                              dnl_t[:, hk, ic * 128:(ic + 1) * 128],
                              KC["IDENT_B"][:], [dnl_b, KCB["IDENT_B"]], [ptwb])
                pha.copy("v", dstg_t[:, ic, hk * 128:(hk + 1) * 128],
                         ptw[:, ic * 128:(ic + 1) * 128], [ptwb], [dstg_b])
        kb.dma("q", SCR["dntc"][e], dstg_t[:], reads=[dstg_b])
    pha.close()
    kb.barrier()
    if P1E_LIMIT == "prep":
        return
    ph = Ph(st, "p1e")
    sel8_t, sel8_b = load_const_local(ph, "SEL8")
    gwT_t, gwT_b = ph.sb("gwT", [128, J, 8], BF16)
    gwsrc = IN["L1_ffn_gate_weight"].rearrange("e (k p) -> p k e", p=128)
    for k in range(J):
        kb.dma("g", gwT_t[:, k, :], gwsrc[:, k, :], writes=[gwT_b], slow_ok=True)
    brep_t, brep_b = replicate_row(ph, "brep", IN["L1_ffn_e_score_correction_bias"], 8, PS, PSB)
    DN_t, DN_b = ph.sb("dnw", [128, 8, 4, D], BF16)
    for e in range(8):
        kb.dma("g", DN_t[:, e, :, :], SCR["dntc"][e], writes=[DN_b])
    X_t, X_b = ph.sb("xin", [128, J, C], BF16)
    GP_t, GP_b = ph.sb("gpw", [128, J, C], BF16)
    UP_t, UP_b = ph.sb("upw", [128, J, C], BF16)
    hw_t, hw_b = ph.sb("hw", [128, 8, 4, C], BF16)
    glog_t, glog_b = ph.sb("glog", [128, 32], F32)
    SC_t, SC_b = ph.sb("gsc", [128, 32], F32)
    SFC_t, SFC_b = ph.sb("gsfc", [128, TT, 8], F32)
    GS_t, GS_b = ph.sb("ggs", [128, TT, 4], F32)
    T8_t, T8_b = ph.sb("gt8", [128, TT, 8], F32)
    T8b_t, T8b_b = ph.sb("gt8b", [128, TT, 8], F32)
    OH1_t, OH1_b = ph.sb("goh1", [128, TT, 8], F32)
    OH2_t, OH2_b = ph.sb("goh2", [128, TT, 8], F32)
    SMK_t, SMK_b = ph.sb("gsmk", [128, TT, 4, 2], F32)
    mx_t, mx_b = ph.sb("gmx", [128, TT, 4], F32)
    w12_t, w12_b = ph.sb("gw12", [128, TT, 4], F32)
    CW_t, CW_b = ph.sb("gcw", [128, TT, 8], F32)
    wrow_t, wrow_b = ph.sb("wrow", [8, C], F32)
    tmpf_t, tmpf_b = ph.sb("tmpf", [128, C], F32)
    g1b_t, g1b_b = ph.sb("g1b", [128, C], BF16)
    x1_t, x1_b = ph.sb("x1", [128, TT, D], F32)
    s1_t, s1_b = ph.sb("s1", [128, TT, D], F32)
    ob_t, ob_b = ph.sb("ob", [128, TT, D], F32)
    ptg, ptgb = ph.psb("ptg", [128, 512], BF16)
    for tb in range(NCH):
        kb.dma("q", X_t[:], cm_slice(SCR["xn1cm"], tb), writes=[X_b])
        kb.dma("q", x1_t[:], tm_slice(SCR["x1"], tb), writes=[x1_b])
        kb.dma("q", s1_t[:], tm_slice(SCR["s1"], tb), writes=[s1_b])
        pg, pgb = PS[4], PSB[4]
        for i in range(TT):
            for k in range(J):
                ph.mm(pg[:, i * 8:(i + 1) * 8], X_t[:, k, i * 128:(i + 1) * 128],
                      gwT_t[:, k, :], k == 0, k == J - 1, [X_b, gwT_b], [pgb])
        ph.act(glog_t[:], pg[:, 0:32], AF.Identity, [pgb], [glog_b])
        ph.act(SC_t[:], glog_t[:], AF.Sigmoid, [glog_b], [SC_b])
        kb.strong = True
        for i in range(TT):
            ph.tt("v", SFC_t[:, i, :], SC_t[:, i * 8:(i + 1) * 8], brep_t[:], ALU.add,
                  [SC_b, brep_b], [SFC_b])
        for i in range(TT):
            sfc2 = SFC_t[:, i, :].rearrange("p (g two) -> p g two", two=2)
            ph.tt("v", GS_t[:, i, :], sfc2[:, :, 0:1].rearrange("p g o -> p (g o)"),
                  sfc2[:, :, 1:2].rearrange("p g o -> p (g o)"), ALU.add, [SFC_b], [GS_b])
        for i in range(TT):
            ph.red(mx_t[:, i, 0:1], GS_t[:, i, :], ALU.max, [GS_b], [mx_b])
        for i in range(TT):
            ph.ts("v", T8_t[:, i, 0:4], GS_t[:, i, :], mx_t[:, i, 0:1], None, ALU.is_ge,
                  ALU.bypass, [GS_b, mx_b], [T8_b])
        for i in range(TT):
            ph.stt("v", T8_t[:, i, 4:8], T8_t[:, i, 0:4], -1e9, GS_t[:, i, :],
                   ALU.mult, ALU.add, [T8_b, GS_b], [T8_b])
        for i in range(TT):
            ph.red(mx_t[:, i, 1:2], T8_t[:, i, 4:8], ALU.max, [T8_b], [mx_b])
        for i in range(TT):
            ph.ts("v", GS_t[:, i, :], GS_t[:, i, :], mx_t[:, i, 1:2], None, ALU.is_ge,
                  ALU.bypass, [GS_b, mx_b], [GS_b])
        for i in range(TT):
            ph.copy("v", SMK_t[:, i, :, 0:1].rearrange("p g o -> p (g o)"), GS_t[:, i, :],
                    [GS_b], [SMK_b])
        for i in range(TT):
            ph.copy("v", SMK_t[:, i, :, 1:2].rearrange("p g o -> p (g o)"), GS_t[:, i, :],
                    [GS_b], [SMK_b])
        for i in range(TT):
            ph.tt("v", T8_t[:, i, :], SFC_t[:, i, :],
                  SMK_t[:, i, :, :].rearrange("p g o -> p (g o)"), ALU.mult,
                  [SFC_b, SMK_b], [T8_b])
        for i in range(TT):
            ph.red(mx_t[:, i, 2:3], T8_t[:, i, :], ALU.max, [T8_b], [mx_b])
        for i in range(TT):
            ph.ts("v", T8b_t[:, i, :], T8_t[:, i, :], mx_t[:, i, 2:3], None, ALU.is_ge,
                  ALU.bypass, [T8_b, mx_b], [T8b_b])
        for i in range(TT):
            ph.stt("v", T8b_t[:, i, :], T8b_t[:, i, :], -1e6, KC["C8P"][:], ALU.mult,
                   ALU.add, [T8b_b, KCB["C8P"]], [T8b_b])
        for i in range(TT):
            ph.red(mx_t[:, i, 3:4], T8b_t[:, i, :], ALU.min, [T8b_b], [mx_b])
        for i in range(TT):
            ph.ts("v", OH1_t[:, i, :], KC["C_IOTA8"][:], mx_t[:, i, 3:4], None,
                  ALU.is_equal, ALU.bypass, [KCB["C_IOTA8"], mx_b], [OH1_b])
        for i in range(TT):
            ph.stt("v", T8_t[:, i, :], OH1_t[:, i, :], -1e9, T8_t[:, i, :], ALU.mult,
                   ALU.add, [OH1_b, T8_b], [T8_b])
        for i in range(TT):
            ph.red(mx_t[:, i, 0:1], T8_t[:, i, :], ALU.max, [T8_b], [mx_b])
        for i in range(TT):
            ph.ts("v", T8b_t[:, i, :], T8_t[:, i, :], mx_t[:, i, 0:1], None, ALU.is_ge,
                  ALU.bypass, [T8_b, mx_b], [T8b_b])
        for i in range(TT):
            ph.stt("v", T8b_t[:, i, :], T8b_t[:, i, :], -1e6, KC["C8P"][:], ALU.mult,
                   ALU.add, [T8b_b, KCB["C8P"]], [T8b_b])
        for i in range(TT):
            ph.red(mx_t[:, i, 1:2], T8b_t[:, i, :], ALU.min, [T8b_b], [mx_b])
        for i in range(TT):
            ph.ts("v", OH2_t[:, i, :], KC["C_IOTA8"][:], mx_t[:, i, 1:2], None,
                  ALU.is_equal, ALU.bypass, [KCB["C_IOTA8"], mx_b], [OH2_b])
        for i in range(TT):
            ph.tt("v", OH1_t[:, i, :], OH1_t[:, i, :], OH2_t[:, i, :], ALU.add,
                  [OH1_b, OH2_b], [OH1_b])
        for i in range(TT):
            ph.tt("v", T8b_t[:, i, :], SC_t[:, i * 8:(i + 1) * 8], OH1_t[:, i, :],
                  ALU.mult, [SC_b, OH1_b], [T8b_b])
        for i in range(TT):
            ph.red(w12_t[:, i, 0:1], T8b_t[:, i, :], ALU.add, [T8b_b], [w12_b])
        for i in range(TT):
            ph.ts("v", w12_t[:, i, 1:2], w12_t[:, i, 0:1], 1e-20, None, ALU.add,
                  ALU.bypass, [w12_b], [w12_b])
        for i in range(TT):
            def mkrec(eng, o=w12_t[:, i, 2:3], ii=w12_t[:, i, 1:2]):
                return eng.reciprocal(o, ii)
            kb.op("v", mkrec, reads=[w12_b], writes=[w12_b])
        for i in range(TT):
            ph.ts("v", CW_t[:, i, :], T8b_t[:, i, :], w12_t[:, i, 2:3], None, ALU.mult,
                  ALU.bypass, [T8b_b, w12_b], [CW_b])
        kb.strong = False
        if P1E_LIMIT == "gate":
            if "dbg" in DEBUG_TAPS:
                kb.dma("q", SCR["dbg"][:, tb * 8:(tb + 1) * 8], CW_t[:, 0, :], reads=[CW_b])
            continue
        pw, pwb = PS[5], PSB[5]
        for i in range(TT):
            ph.transpose(pw[0:8, i * 128:(i + 1) * 128], CW_t[:, i, :], KC["IDENT_F"][:],
                         [CW_b, KCB["IDENT_F"]], [pwb])
        ph.act(wrow_t[:], pw[0:8, 0:512], AF.Identity, [pwb], [wrow_b])
        for e in range(8):
            kb.dma("g", GP_t[:], SCR["gptc"][e], writes=[GP_b])
            kb.dma("g", UP_t[:], SCR["uptc"][e], writes=[UP_b])
            prep, prepb = PS[4], PSB[4]
            def mkrep(eng, o=prep[:, 0:512], l=sel8_t[0:8, e * 128:(e + 1) * 128],
                      r=wrow_t[0:8, :]):
                return eng.matmul(o, l, r, start=True, stop=True)
            kb.op("t", mkrep, reads=[sel8_b, wrow_b], writes=[prepb])
            for ic in range(4):
                pg_, pg_b = PS[0 + ic % 2], PSB[0 + ic % 2]
                pu_, pu_b = PS[2 + ic % 2], PSB[2 + ic % 2]
                for k in range(J):
                    ph.mm(pg_[:], GP_t[:, k, ic * 128:(ic + 1) * 128], X_t[:, k, :],
                          k == 0, k == J - 1, [GP_b, X_b], [pg_b])
                for k in range(J):
                    ph.mm(pu_[:], UP_t[:, k, ic * 128:(ic + 1) * 128], X_t[:, k, :],
                          k == 0, k == J - 1, [UP_b, X_b], [pu_b])
                ph.act(tmpf_t[:], pg_[:], AF.Silu, [pg_b], [tmpf_b])
                ph.tt("v", tmpf_t[:], tmpf_t[:], pu_[:], ALU.mult, [tmpf_b, pu_b], [tmpf_b])
                ph.tt("v", hw_t[:, e, ic, :], tmpf_t[:], prep[:, 0:512], ALU.mult,
                      [tmpf_b, prepb], [hw_b])
        if P1E_LIMIT == "experts":
            continue
        for mo in range(J):
            py, pyb = PS[mo % 2], PSB[mo % 2]
            for e in range(8):
                for ic in range(4):
                    ph.mm(py[:], DN_t[:, e, ic, mo * 128:(mo + 1) * 128], hw_t[:, e, ic, :],
                          (e == 0 and ic == 0), (e == 7 and ic == 3), [DN_b, hw_b], [pyb])
            ph.act(g1b_t[:], py[:], AF.Silu, [pyb], [g1b_b])
            for i in range(TT):
                ph.transpose(ptg[:, i * 128:(i + 1) * 128], g1b_t[:, i * 128:(i + 1) * 128],
                             KC["IDENT_B"][:], [g1b_b, KCB["IDENT_B"]], [ptgb])
            for i in range(TT):
                ph.tt("v", ob_t[:, i, mo * 128:(mo + 1) * 128], ptg[:, i * 128:(i + 1) * 128],
                      s1_t[:, i, mo * 128:(mo + 1) * 128], ALU.mult, [ptgb, s1_b], [ob_b])
                ph.tt("v", ob_t[:, i, mo * 128:(mo + 1) * 128],
                      ob_t[:, i, mo * 128:(mo + 1) * 128],
                      x1_t[:, i, mo * 128:(mo + 1) * 128], ALU.add, [ob_b, x1_b], [ob_b])
        kb.dma("q", tm_slice(OUT, tb), ob_t[:], reads=[ob_b])
    ph.close()


def build(n_phases=99):
    nc, kb, IN, CONST, OUT, SCR = build_program()
    A = nc.alloc_sbuf_tensor
    P = nc.alloc_psum_tensor
    LOCAL_CONSTS = {"TRAMP4K", "E0P8", "SEL8", "E4K", "EMTM"}
    sb_const = {}
    for name, (shape, dt) in CONST_SPECS.items():
        if name in LOCAL_CONSTS:
            continue
        sb_const[name] = A("k_" + name, shape, dt)
        kb.dma("q", sb_const[name][:], CONST[name], writes=[kb.buf("k_" + name)])
    st_const = CONST
    KC = sb_const
    KCB = {name: kb.buf("k_" + name) for name in CONST_SPECS}
    for k, v in enumerate(CPOOL_VALS):
        nc.const_aps.register(F32, float(v), sb_const["CPOOL"][:, k:k + 1])
    PS = [P(f"ps{i}", [128, 512], F32) for i in range(6)]
    PSB = [kb.buf(f"ps{i}") for i in range(6)]
    st = dict(nc=nc, kb=kb, IN=IN, KC=KC, KCB=KCB, OUT=OUT, SCR=SCR, A=A, PS=PS, PSB=PSB,
              CONST=CONST)

    phase_p0a1(st)
    if n_phases >= 2:
        kb.barrier(); phase_p0a2(st)
    if n_phases >= 3:
        kb.barrier(); phase_p0b(st)
    if n_phases >= 4:
        kb.barrier(); phase_p1a(st)
    if n_phases >= 5:
        kb.barrier(); phase_p1b(st)
    if n_phases >= 6:
        kb.barrier(); kb.seq = SEQ_P1C; phase_p1c(st); kb.seq = False
    if n_phases >= 7:
        kb.barrier(); kb.seq = SEQ_P1D; phase_p1d(st); kb.seq = False
    if n_phases >= 8:
        kb.barrier(); kb.seq = SEQ_P1E; phase_p1e(st); kb.seq = False
    kb.barrier()
    kb.finalize_and_emit()
    return nc


def kernel(**inputs):
    x_full = inputs["L0_attn_x"]
    consts = host_constants()
    nc = build(PHASES)
    base = {}
    for k, v in inputs.items():
        if k == "L0_attn_x":
            continue
        base[k] = np.ascontiguousarray(v, dtype=np.float32)
    import ml_dtypes
    for name, (shape, dt) in CONST_SPECS.items():
        arr = consts[name]
        if dt == BF16:
            arr = arr.astype(ml_dtypes.bfloat16)
        base["K_" + name] = np.ascontiguousarray(arr)
    in_maps = []
    for i in range(8):
        m = dict(base)
        m["x"] = np.ascontiguousarray(x_full[i], dtype=np.float32)
        in_maps.append(m)
    res = run_bass_kernel_spmd(nc, in_maps, core_ids=list(range(8)))
    global LAST_RESULTS
    LAST_RESULTS = res
    out = np.stack([np.asarray(res.results[i].get("out", np.zeros((T, D), np.float32)))
                    for i in range(8)])
    return out.astype(np.float32)


LAST_RESULTS = None


# version 11
# speedup vs baseline: 1.5450x; 1.5450x over previous
import numpy as np
import concourse.bass as bass
import concourse.mybir as mybir
from concourse.bass_utils import run_bass_kernel_spmd

F32 = mybir.dt.float32
BF16 = mybir.dt.bfloat16
AF = mybir.ActivationFunctionType
ALU = mybir.AluOpType
AX = mybir.AxisListType

T = 4096
D = 1024
C = 512
NCH = T // C
J = D // 128
TT = C // 128

TWO_PI = 6.283185307179586
PI = 3.141592653589793
MAGIC = 1.5 * (2.0 ** 23)
EPS_LN = 1e-5
CPOOL_VALS = [0.0, 1.0, EPS_LN, -0.5, -1.0, PI / 2, float(np.exp(-0.001)), 7.0,
              0.3, -0.3, 0.5, 2.0, 1e-6, 1e-20, 0.1, 3.0]

DEBUG_TAPS = []
PHASES = 99
P1E_LIMIT = "full"
SEQ_P1C = False
SEQ_P1D = False
SEQ_P1E = False
T_STRONG = False
SERIALIZE = False
ZERO_CARRY = False

ENGINES = ("t", "v", "s", "g", "q")


class Buf:
    __slots__ = ("name", "writer", "readers")
    def __init__(self, name):
        self.name = name
        self.writer = None
        self.readers = []


class Node:
    __slots__ = ("idx", "engine", "fn", "is_dma", "deps", "marked", "sem_val",
                 "waits", "tag", "strong", "echo_val")
    def __init__(self, idx, engine, fn, is_dma, tag, strong=False):
        self.echo_val = None
        self.idx = idx
        self.engine = engine
        self.fn = fn
        self.is_dma = is_dma
        self.deps = set()
        self.marked = False
        self.sem_val = None
        self.waits = []
        self.tag = tag
        self.strong = strong


class KB:
    def __init__(self, nc):
        self.nc = nc
        self.nodes = []
        self.bufs = {}
        self.tag = ""
        self.strong = False
        self.seq = False

    def buf(self, name):
        b = self.bufs.get(name)
        if b is None:
            b = Buf(name)
            self.bufs[name] = b
        return b

    def op(self, engine, fn, reads=(), writes=(), is_dma=False):
        n = Node(len(self.nodes), engine, fn, is_dma, self.tag,
                 strong=self.strong or self.seq)
        if self.seq and self.nodes:
            n.deps.add(self.nodes[-1])
        self._apply_barrier(n)
        for b in reads:
            if b.writer is not None:
                n.deps.add(b.writer)
        for b in writes:
            if b.writer is not None:
                n.deps.add(b.writer)
            for r in b.readers:
                n.deps.add(r)
        for b in writes:
            b.writer = n
            b.readers = []
        for b in reads:
            b.readers.append(n)
        self.nodes.append(n)
        return n

    def dma(self, queue_engine, dst_ap, src_ap, reads=(), writes=(), slow_ok=False):
        def fn(eng):
            if slow_ok:
                return eng.dma_start(dst_ap, src_ap, allow_slow_non_contiguous=True)
            return eng.dma_start(dst_ap, src_ap)
        return self.op(queue_engine, fn, reads=reads, writes=writes, is_dma=True)

    def barrier(self):
        deps = set()
        for b in self.bufs.values():
            if b.writer is not None:
                deps.add(b.writer)
            deps.update(b.readers)
        bnodes = []
        for e in ENGINES:
            n = Node(len(self.nodes), e, (lambda eng: eng.nop()), False,
                     self.tag + ":barrier", strong=True)
            n.deps = set(deps)
            self.nodes.append(n)
            bnodes.append(n)
        for b in self.bufs.values():
            b.readers = []
            b.writer = None
        self._pending_barrier = (set(bnodes), {e: False for e in ENGINES})

    _pending_barrier = None

    def _apply_barrier(self, n):
        pb = self._pending_barrier
        if pb is None:
            return
        bset, seen = pb
        if not seen[n.engine]:
            n.deps.update(bset)
            seen[n.engine] = True
            if all(seen.values()):
                self._pending_barrier = None

    def finalize_and_emit(self):
        nc = self.nc
        if SERIALIZE:
            prev = None
            for n in self.nodes:
                if prev is not None:
                    n.deps.add(prev)
                    prev.marked = True
                prev = n
        for n in self.nodes:
            for d in n.deps:
                if d.is_dma:
                    d.marked = True
                elif d.engine != n.engine:
                    d.marked = True
                elif d.strong or n.strong:
                    d.marked = True
                elif d.engine in ("v", "s", "g", "q", "t"):
                    d.marked = True
        sems = {}
        for e in ENGINES:
            sems["c" + e] = nc.alloc_semaphore(f"sem_c{e}")
            sems["d" + e] = nc.alloc_semaphore(f"sem_d{e}")
        counts = {k: 0 for k in sems}
        for n in self.nodes:
            if n.is_dma:
                counts["d" + n.engine] += 16
                n.sem_val = counts["d" + n.engine]
                counts["c" + n.engine] += 1
                n.echo_val = counts["c" + n.engine]
            elif n.marked:
                counts["c" + n.engine] += 1
                n.sem_val = counts["c" + n.engine]
        hwm = {e: {} for e in ENGINES}
        for n in self.nodes:
            needed = {}
            for d in n.deps:
                if d.is_dma:
                    key = "c" + d.engine
                    dval = d.echo_val
                elif d.engine == n.engine:
                    if d.engine == "t" and not (d.strong or n.strong or T_STRONG):
                        continue
                    key = "c" + d.engine
                    dval = d.sem_val
                else:
                    key = "c" + d.engine
                    dval = d.sem_val
                if dval is None:
                    continue
                v = needed.get(key, 0)
                if dval > v:
                    needed[key] = dval
            h = hwm[n.engine]
            for key in sorted(needed):
                val = needed[key]
                if h.get(key, 0) >= val:
                    continue
                n.waits.append((key, val))
                h[key] = val

        block_ctx = nc.Block()
        block = block_ctx.__enter__()

        def make_body(e):
            mynodes = [n for n in self.nodes if n.engine == e]
            def body(eng):
                for n in mynodes:
                    for key, val in n.waits:
                        eng.wait_ge(sems[key], val)
                    try:
                        inst = n.fn(eng)
                    except Exception as ex:
                        raise RuntimeError(f"emit failed node {n.idx} tag={n.tag}: {ex}") from ex
                    if n.sem_val is not None:
                        if n.is_dma:
                            inst.then_inc(sems["d" + n.engine], 16)
                            eng.wait_ge(sems["d" + n.engine], n.sem_val)
                            eng.nop().then_inc(sems["c" + n.engine], 1)
                        else:
                            inst.then_inc(sems["c" + n.engine], 1)
            return body

        block.tensor(make_body("t"))
        block.vector(make_body("v"))
        block.scalar(make_body("s"))
        block.gpsimd(make_body("g"))
        block.sync(make_body("q"))
        block_ctx.__exit__(None, None, None)
        return counts


def host_constants():
    c = {}
    c["IDENT_B"] = np.eye(128, dtype=np.float32)
    c["IDENT_F"] = np.eye(128, dtype=np.float32)
    c["ONES1F"] = np.ones((1, 128), dtype=np.float32)
    c["M1024"] = np.full((128, 128), 1.0 / 1024.0, dtype=np.float32)
    c["TRAMP"] = np.broadcast_to(np.arange(C, dtype=np.float32), (128, C)).copy()
    c["TRAMP4K"] = np.broadcast_to(np.arange(T, dtype=np.float32), (128, T)).copy()
    tau = np.arange(C, dtype=np.float64)
    c["E0P8"] = np.broadcast_to(np.tile(np.exp(0.001 * tau).astype(np.float32), J),
                                (128, J * C)).copy()
    c["ZEROS512"] = np.zeros((128, C), dtype=np.float32)
    c["ONESF"] = np.ones((128, C), dtype=np.float32)
    io64 = np.arange(64, dtype=np.float32)
    c["C_IOTA64"] = np.broadcast_to(io64, (128, 64)).copy()
    c["C64P"] = np.broadcast_to(io64 + 1e6, (128, 64)).copy()
    c["C_CLUST"] = np.broadcast_to((io64 // 8).astype(np.float32), (128, 64)).copy()
    io8 = np.arange(8, dtype=np.float32)
    c["C_IOTA8"] = np.broadcast_to(io8, (128, 8)).copy()
    c["C8P"] = np.broadcast_to(io8 + 1e6, (128, 8)).copy()
    io4 = np.arange(4, dtype=np.float32)
    c["C_IOTA4"] = np.broadcast_to(io4, (128, 4)).copy()
    c["C4P"] = np.broadcast_to(io4 + 1e6, (128, 4)).copy()
    c["C_GID"] = np.broadcast_to(np.repeat(np.arange(4, dtype=np.float32), 2),
                                 (128, 8)).copy()
    t64 = np.arange(T, dtype=np.float64)
    c["E4K"] = np.broadcast_to(np.exp(0.001 * t64).astype(np.float32), (128, T)).copy()
    em = np.zeros((128, 32), dtype=np.float32)
    for cc in range(8):
        for i in range(4):
            tt_ = cc * 512 + i * 128 + np.arange(128, dtype=np.float64)
            em[:, cc * 4 + i] = np.exp(-0.001 * tt_).astype(np.float32)
    c["EMTM"] = em
    gs = np.zeros((64, 8), dtype=np.float32)
    for cc in range(8):
        gs[cc * 8:(cc + 1) * 8, cc] = 1.0 / 8.0
    c["GSEL8"] = gs
    sel = np.zeros((8, 8, 128), dtype=np.float32)
    for e in range(8):
        sel[e, e, :] = 1.0
    c["SEL8"] = sel.reshape(8, 8 * 128)
    d = np.arange(D, dtype=np.float64) / (D - 1)
    scale = (0.001 ** d).astype(np.float32)
    c["SCALE_STRIP"] = scale.reshape(J, 128).T.copy()
    c["CPOOL"] = np.broadcast_to(np.array(CPOOL_VALS, dtype=np.float32), (128, len(CPOOL_VALS))).copy()
    return c

CONST_SPECS = {
    "IDENT_B": ([128, 128], BF16),
    "IDENT_F": ([128, 128], F32),
    "ONES1F": ([1, 128], F32),
    "M1024": ([128, 128], BF16),
    "TRAMP": ([128, C], F32),
    "TRAMP4K": ([128, T], F32),
    "E0P8": ([128, J * C], BF16),
    "ZEROS512": ([128, C], F32),
    "ONESF": ([128, C], F32),
    "C_IOTA64": ([128, 64], F32),
    "C64P": ([128, 64], F32),
    "C_CLUST": ([128, 64], F32),
    "C_IOTA8": ([128, 8], F32),
    "C8P": ([128, 8], F32),
    "C_IOTA4": ([128, 4], F32),
    "C4P": ([128, 4], F32),
    "C_GID": ([128, 8], F32),
    "SCALE_STRIP": ([128, J], F32),
    "GSEL8": ([64, 8], BF16),
    "SEL8": ([8, 8 * 128], F32),
    "CPOOL": ([128, 16], F32),
    "E4K": ([128, T], F32),
    "EMTM": ([128, 32], F32),
}


def build_program():
    nc = bass.Bass("TRN2", target_bir_lowering=False)
    kb = KB(nc)
    IN = {}
    def din(name, shape):
        IN[name] = nc.dram_tensor(name, list(shape), F32, kind="ExternalInput").ap()
    din("x", (T, D))
    din("L0_attn_W_in", (3, D, D)); din("L0_attn_b_in", (3, D))
    din("L0_attn_params_log", (3, D))
    din("L0_attn_W_mid", (2, D, D)); din("L0_attn_b_mid", (2, D))
    din("L0_attn_w1_w", (D, D)); din("L0_attn_w1_b", (D,))
    din("L0_attn_w2_w", (D, D)); din("L0_attn_w2_b", (D,))
    din("L0_attn_ln_g", (D,)); din("L0_attn_ln_b", (D,))
    din("L0_ffn_signatures_raw", (64, D)); din("L0_ffn_knot_values", (8,))
    din("L0_ffn_temperature", (1,))
    din("L0_ffn_compress_w1", (D, 256)); din("L0_ffn_compress_b1", (256,))
    din("L0_ffn_compress_w2", (256, 2)); din("L0_ffn_compress_b2", (2,))
    din("L0_ffn_spline_w1", (64, 2, 16)); din("L0_ffn_spline_b1", (64, 16))
    din("L0_ffn_spline_w2", (64, 16, 1)); din("L0_ffn_spline_b2", (64, 1))
    din("L0_ffn_directions", (64, D))
    din("L0_ffn_norm_gamma", (D,)); din("L0_ffn_norm_beta", (D,))
    din("L0_ffn_output_scale", (1,))
    din("L1_attn_ln_sc_w", (D,)); din("L1_attn_ln_sc_b", (D,))
    din("L1_attn_fc_w", (D, D)); din("L1_attn_fc_b", (D,))
    din("L1_attn_phz_w", (D, D)); din("L1_attn_phz_b", (D,))
    din("L1_attn_lci_re", (D,)); din("L1_attn_lci_im", (D,))
    din("L1_attn_ln_in_w", (D,)); din("L1_attn_ln_in_b", (D,))
    din("L1_attn_ln_ffn_w", (D,)); din("L1_attn_ln_ffn_b", (D,))
    din("L1_attn_fc1_w", (D, 4 * D)); din("L1_attn_fc1_b", (4 * D,))
    din("L1_attn_fc2_w", (4 * D, D)); din("L1_attn_fc2_b", (D,))
    din("L1_ffn_gate_weight", (8, D)); din("L1_ffn_e_score_correction_bias", (8,))
    din("L1_ffn_gate_proj", (8, 512, D)); din("L1_ffn_up_proj", (8, 512, D))
    din("L1_ffn_down_proj", (8, D, 512))

    CONST = {}
    for name, (shape, dt) in CONST_SPECS.items():
        CONST[name] = nc.dram_tensor("K_" + name, list(shape), dt, kind="ExternalInput").ap()
    OUT = nc.dram_tensor("out", [T, D], F32, kind="ExternalOutput").ap()
    SCR = {}
    def scratch(name, shape, dt=F32):
        kind = "ExternalOutput" if name in DEBUG_TAPS else "Internal"
        SCR[name] = nc.dram_tensor("scr_" + name, list(shape), dt, kind=kind).ap()
    scratch("xn0", (T, D))
    scratch("h0re", (J, 128, T), BF16)
    scratch("h0im", (J, 128, T), BF16)
    scratch("o0", (J, 128, T), BF16)
    scratch("s0", (T, D))
    scratch("x1", (T, D))
    scratch("xn1", (T, D))
    scratch("xmid", (T, D))
    scratch("s1", (T, D))
    scratch("xn1cm", (J, 128, T), BF16)
    scratch("xn2cm", (J, 128, T), BF16)
    scratch("xn2cmlo", (J, 128, T), BF16)
    scratch("xn3cm", (J, 128, T), BF16)
    scratch("phi1", (J, 128, T), F32)
    scratch("y1", (J, 128, T), BF16)
    scratch("h1", (J, 128, T), F32)
    scratch("dbg", (128, 64))
    scratch("dbg2", (128, 2048))
    for nm in ("dbgA", "dbgB", "dbgC", "dbgD", "dbgE"):
        scratch(nm, (128, T))
    scratch("gptc", (8, 128, J, 512), BF16)
    scratch("uptc", (8, 128, J, 512), BF16)
    scratch("dntc", (8, 128, 4, D), BF16)
    return nc, kb, IN, CONST, OUT, SCR


class Ph:
    def __init__(self, st, name):
        self.st = st
        self.nc = st["nc"]
        self.kb = st["kb"]
        self.name = name
        self._guards = []
        self.kb.tag = name
        self._vg_flip = 0

    def sb(self, name, shape, dt):
        g = self.nc.sbuf_tensor(self.name + "_" + name, shape, dt)
        t = g.__enter__()
        self._guards.append(g)
        return t, self.kb.buf(self.name + "_" + name)

    def psb(self, name, shape, dt):
        g = self.nc.psum_tensor(self.name + "_" + name, shape, dt)
        t = g.__enter__()
        self._guards.append(g)
        return t, self.kb.buf(self.name + "_" + name)

    def close(self):
        for g in reversed(self._guards):
            g.__exit__(None, None, None)

    def _mk(self, engine, method, out_ap, args, kwargs, reads, writes):
        def fn(eng):
            return getattr(eng, method)(out_ap, *args, **kwargs)
        return self.kb.op(engine, fn, reads=reads, writes=writes)

    def tt(self, engine, out_ap, a_ap, b_ap, op, rb, wb):
        return self._mk(engine, "tensor_tensor", out_ap, (a_ap, b_ap, op), {}, rb, wb)

    def ts(self, engine, out_ap, in_ap, s1, s2, op0, op1, rb, wb):
        return self._mk(engine, "tensor_scalar", out_ap, (in_ap, s1, s2, op0, op1), {}, rb, wb)

    def tss(self, engine, out_ap, in_ap, s, op, rb, wb):
        return self._mk(engine, "tensor_scalar", out_ap, (in_ap, s, None, op, ALU.bypass),
                        {}, rb, wb)

    def stt(self, engine, out_ap, in0_ap, scalar, in1_ap, op0, op1, rb, wb):
        return self._mk(engine, "scalar_tensor_tensor", out_ap,
                        (in0_ap, scalar, in1_ap, op0, op1), {}, rb, wb)

    def scan(self, engine, out_ap, d0_ap, d1_ap, init, op0, op1, rb, wb):
        return self._mk(engine, "tensor_tensor_scan", out_ap,
                        (d0_ap, d1_ap, init, op0, op1), {}, rb, wb)

    def act(self, out_ap, in_ap, func, rb, wb, bias=0.0, scale=1.0, accum_out=None):
        kw = dict(bias=bias, scale=scale)
        if accum_out is not None:
            kw["accum_out"] = accum_out
        rb = list(rb)
        if isinstance(bias, float) or isinstance(scale, float):
            rb.append(self.st["KCB"]["CPOOL"])
        return self._mk("s", "activation", out_ap, (in_ap, func), kw, rb, wb)

    def copy(self, engine, out_ap, in_ap, rb, wb):
        return self._mk(engine, "tensor_copy", out_ap, (in_ap,), {}, rb, wb)

    def red(self, out_ap, in_ap, op, rb, wb):
        return self._mk("v", "tensor_reduce", out_ap, (in_ap, AX.X, op), {}, rb, wb)

    def memset(self, engine, ap, val, wb):
        def fn(eng):
            return eng.memset(ap, val)
        return self.kb.op(engine, fn, reads=[], writes=wb)

    def mm(self, out_ap, lhsT_ap, rhs_ap, start, stop, rb, wb):
        def fn(eng):
            return eng.matmul(out_ap, lhsT_ap, rhs_ap, start=start, stop=stop)
        return self.kb.op("t", fn, reads=rb, writes=wb)

    def transpose(self, out_ap, in_ap, ident_ap, rb, wb):
        def fn(eng):
            return eng.transpose(out_ap, in_ap, ident_ap)
        return self.kb.op("t", fn, reads=rb, writes=wb)


def tm_slice(dram_ap, c):
    return dram_ap[c * C:(c + 1) * C, :].rearrange("(i p) d -> p i d", p=128)


def cm_slice(dram_ap, c):
    return dram_ap[:, :, c * C:(c + 1) * C].rearrange("j p t -> p j t")


def load_weight_cm(ph, name_sb, dram_ap, dk, dn, dt=BF16):
    t, b = ph.sb(name_sb, [128, dk, dn], dt)
    src = dram_ap.rearrange("(k p) n -> p k n", p=128)
    ph.kb.dma("g" if dt == BF16 else "q", t[:], src, writes=[b])
    return t, b


def load_strip(ph, name_sb, dram_ap, jn=J):
    t, b = ph.sb(name_sb, [128, jn], F32)
    src = dram_ap.rearrange("(j p) -> p j", p=128)
    ph.kb.dma("q", t[:], src, writes=[b], slow_ok=True)
    return t, b


def ln_stats_tm(ph, x_t, x_b, junk_t, junk_b, pref):
    sum_t, sum_b = ph.sb(pref + "_sum", [128, 8], F32)
    sq_t, sq_b = ph.sb(pref + "_sq", [128, 8], F32)
    mean_t, mean_b = ph.sb(pref + "_mean", [128, 8], F32)
    rstd_t, rstd_b = ph.sb(pref + "_rstd", [128, 8], F32)
    m2_t, m2_b = ph.sb(pref + "_m2", [128, 8], F32)
    for i in range(TT):
        ph.act(junk_t[:, 0:D], x_t[:, i, :], AF.Identity, [x_b], [junk_b, sum_b],
               accum_out=sum_t[:, i:i + 1])
        ph.act(junk_t[:, 0:D], x_t[:, i, :], AF.Square, [x_b], [junk_b, sq_b],
               accum_out=sq_t[:, i:i + 1])
    mean = mean_t[:, 0:TT]
    rstd = rstd_t[:, 0:TT]
    m2 = m2_t[:, 0:TT]
    ms_b = mean_b
    prev_strong = ph.kb.strong
    ph.kb.strong = True
    ph.tss("v", mean, sum_t[:, 0:TT], 1.0 / D, ALU.mult, [sum_b], [mean_b])
    ph.act(m2, mean, AF.Square, [mean_b], [m2_b])
    ph.stt("v", rstd, sq_t[:, 0:TT], 1.0 / D, m2, ALU.mult, ALU.subtract, [sq_b, m2_b], [rstd_b])
    if pref == "st0" and "dbg" in DEBUG_TAPS:
        scr = ph.st["SCR"]
        ph.kb.dma("q", scr["dbg"][:, 16:16 + TT], sq_t[:, 0:TT], reads=[sq_b])
        ph.kb.dma("q", scr["dbg"][:, 24:24 + TT], m2, reads=[m2_b])
        ph.kb.dma("q", scr["dbg"][:, 32:32 + TT], rstd, reads=[rstd_b])
        ph.kb.dma("q", scr["dbg"][:, 40:40 + TT], sum_t[:, 0:TT], reads=[sum_b])
    ph.act(rstd, rstd, AF.Ln, [rstd_b], [rstd_b], bias=EPS_LN)
    ph.act(rstd, rstd, AF.Exp, [rstd_b], [rstd_b], scale=-0.5)
    ph.kb.strong = prev_strong
    if pref == "st0" and "dbg" in DEBUG_TAPS:
        scr = ph.st["SCR"]
        ph.kb.dma("q", scr["dbg"][:, 4:4 + TT], rstd, reads=[rstd_b])
    return mean, rstd, (mean_b, rstd_b)


def phase_p0a1(st):
    nc, kb = st["nc"], st["kb"]
    IN, KC, KCB, SCR, PS, PSB = st["IN"], st["KC"], st["KCB"], st["SCR"], st["PS"], st["PSB"]
    ph_hold = Ph(st, "p0a1h")
    xnc_t, xnc_b = ph_hold.sb("xnc", [128, J, T], BF16)
    ph = ph_hold

    BIN, BINB, PLOG, PLOGB = [], [], [], []
    for e in range(3):
        t, b = load_strip(ph, f"bin{e}", IN["L0_attn_b_in"][e])
        BIN.append(t); BINB.append(b)
        t, b = load_strip(ph, f"plog{e}", IN["L0_attn_params_log"][e])
        PLOG.append(t); PLOGB.append(b)
    nu_t, nu_b = ph.sb("nu", [128, J], F32)
    th_t, th_b = ph.sb("theta", [128, J], F32)
    ga_t, ga_b = ph.sb("gamma", [128, J], F32)
    rho_t, rho_b = ph.sb("rho", [128, J], F32)
    ph.act(nu_t[:], PLOG[0][:], AF.Exp, [PLOGB[0]], [nu_b])
    ph.act(th_t[:], PLOG[1][:], AF.Exp, [PLOGB[1]], [th_b])
    ph.act(ga_t[:], PLOG[2][:], AF.Exp, [PLOGB[2]], [ga_b])
    ph.act(rho_t[:], nu_t[:], AF.Exp, [nu_b], [rho_b], scale=-1.0)
    gbr_t, gbr_b = ph.sb("gbr", [128, J], F32)
    gbi_t, gbi_b = ph.sb("gbi", [128, J], F32)
    ph.tt("v", gbr_t[:], ga_t[:], BIN[0][:], ALU.mult, [ga_b, BINB[0]], [gbr_b])
    ph.tt("v", gbi_t[:], ga_t[:], BIN[1][:], ALU.mult, [ga_b, BINB[1]], [gbi_b])

    ph = Ph(st, "p0a1a")
    xt_t, xt_b = ph.sb("xt", [128, TT, D], F32)
    junk_t, junk_b = ph.sb("junk", [128, D], F32)
    xnb_t, xnb_b = ph.sb("xnb", [128, TT, D], BF16)
    ptr0, ptr0b = ph.psb("ptr0", [128, 512], BF16)
    ptr1, ptr1b = ph.psb("ptr1", [128, 512], BF16)
    PTR, PTRB = [ptr0, ptr1], [ptr0b, ptr1b]
    for c in range(NCH):
        kb.dma("q", xt_t[:], tm_slice(IN["x"], c), writes=[xt_b])
        mean, rstd, (mean_b, rstd_b) = ln_stats_tm(ph, xt_t, xt_b, junk_t, junk_b, f"st{c}")
        for i in range(TT):
            ph.ts("v", xt_t[:, i, :], xt_t[:, i, :], mean[:, i:i + 1], rstd[:, i:i + 1],
                  ALU.subtract, ALU.mult, [mean_b, rstd_b, xt_b], [xt_b])
        kb.dma("q", tm_slice(SCR["xn0"], c), xt_t[:], reads=[xt_b])
        ph.act(xnb_t[:].rearrange("p i d -> p (i d)"), xt_t[:].rearrange("p i d -> p (i d)"),
               AF.Identity, [xt_b], [xnb_b])
        for j in range(J):
            bank, bankb = PTR[j % 2], PTRB[j % 2]
            for i in range(TT):
                ph.transpose(bank[:, i * 128:(i + 1) * 128], xnb_t[:, i, j * 128:(j + 1) * 128],
                             KC["IDENT_B"][:], [xnb_b, KCB["IDENT_B"]], [bankb])
            ph.copy("v", xnc_t[:, j, c * C:(c + 1) * C], bank[:], [bankb], [xnc_b])

    ph.close()
    kb.barrier()
    ph = Ph(st, "p0a1b")
    tramp4k_t, tramp4k_b = load_const_local(ph, "TRAMP4K")
    WS, WSB = [], []
    for d in range(2):
        row_t, row_b = [], []
        for e in range(3):
            t, b = ph.sb(f"ws{d}_{e}", [128, J, 128], BF16)
            row_t.append(t); row_b.append(b)
        WS.append(row_t); WSB.append(row_b)
    tcg_t, tcg_b = ph.sb("tcg", [128, T], BF16)
    tsg_t, tsg_b = ph.sb("tsg", [128, T], BF16)
    ab_t, ab_b = ph.sb("abuf", [128, T], BF16)
    bb_t, bb_b = ph.sb("bbuf", [128, T], BF16)
    wre_t, wre_b = ph.sb("wre", [128, T], BF16)
    wim_t, wim_b = ph.sb("wim", [128, T], BF16)
    gre_t, gre_b = ab_t, ab_b
    gim_t, gim_b = bb_t, bb_b
    ob2_t, ob2_b = ph.sb("ob2", [128, C], BF16)
    pvB_t, pvB_b = ph.sb("pvB", [128, T], F32)
    pgB_t, pgB_b = ph.sb("pgB", [128, T], F32)
    rhog_t, rhog_b = ph.sb("rhog", [128, T], F32)
    tta_t, tta_b = pvB_t, pvB_b
    ttk_t, ttk_b = pgB_t, pgB_b
    hre_t, hre_b = wre_t, wre_b
    him_t, him_b = wim_t, wim_b

    for m in range(J):
        d = m % 2
        for e in range(3):
            kb.dma("g", WS[d][e][:], IN["L0_attn_W_in"][e][:, m * 128:(m + 1) * 128]
                   .rearrange("(k p) n -> p k n", p=128), writes=[WSB[d][e]])
        ph.ts("v", tta_t[:], tramp4k_t[:], th_t[:, m:m + 1], None, ALU.mult, ALU.bypass,
              [tramp4k_b, th_b], [tta_b])
        ph.ts("v", ttk_t[:], tta_t[:], 1.0 / TWO_PI, MAGIC, ALU.mult, ALU.add, [tta_b], [ttk_b])
        ph.ts("v", ttk_t[:], ttk_t[:], MAGIC, TWO_PI, ALU.subtract, ALU.mult, [ttk_b], [ttk_b])
        ph.tt("v", tta_t[:], tta_t[:], ttk_t[:], ALU.subtract, [tta_b, ttk_b], [tta_b])
        ph.act(tsg_t[:], tta_t[:], AF.Sin, [tta_b], [tsg_b])
        ph.act(tta_t[:], tta_t[:], AF.Abs, [tta_b], [tta_b])
        ph.act(tcg_t[:], tta_t[:], AF.Sin, [tta_b], [tcg_b], scale=-1.0, bias=PI / 2)
        ph.ts("v", rhog_t[:], tramp4k_t[:], 0.0, rho_t[:, m:m + 1], ALU.mult, ALU.add,
              [tramp4k_b, rho_b], [rhog_b])
        for c in range(NCH):
            pr, prb = PS[0 + (c % 2) * 3], PSB[0 + (c % 2) * 3]
            pi_, pib = PS[1 + (c % 2) * 3], PSB[1 + (c % 2) * 3]
            po, pob = PS[2 + (c % 2) * 3], PSB[2 + (c % 2) * 3]
            for k in range(J):
                ph.mm(pr[:], WS[d][0][:, k, :], xnc_t[:, k, c * C:(c + 1) * C],
                      k == 0, k == J - 1, [WSB[d][0], xnc_b], [prb])
            for k in range(J):
                ph.mm(pi_[:], WS[d][1][:, k, :], xnc_t[:, k, c * C:(c + 1) * C],
                      k == 0, k == J - 1, [WSB[d][1], xnc_b], [pib])
            for k in range(J):
                ph.mm(po[:], WS[d][2][:, k, :], xnc_t[:, k, c * C:(c + 1) * C],
                      k == 0, k == J - 1, [WSB[d][2], xnc_b], [pob])
            sl = slice(c * C, (c + 1) * C)
            ph.act(ab_t[:, sl], pr[:], AF.Identity, [prb, ga_b, gbr_b], [ab_b],
                   scale=ga_t[:, m:m + 1], bias=gbr_t[:, m:m + 1])
            ph.act(bb_t[:, sl], pi_[:], AF.Identity, [pib, ga_b, gbi_b], [bb_b],
                   scale=ga_t[:, m:m + 1], bias=gbi_t[:, m:m + 1])
            ph.act(ob2_t[:], po[:], AF.Identity, [pob, BINB[2]], [ob2_b],
                   bias=BIN[2][:, m:m + 1])
            kb.dma("q", SCR["o0"][m][:, c * C:(c + 1) * C], ob2_t[:], reads=[ob2_b])
        if "dbgA" in DEBUG_TAPS and m == 0:
            kb.dma("g", SCR["dbgA"], tcg_t[:], reads=[tcg_b])
            kb.dma("g", SCR["dbgB"], tsg_t[:], reads=[tsg_b])
            kb.dma("q", SCR["dbgC"], rhog_t[:], reads=[rhog_b])
            kb.dma("g", SCR["dbgD"], ab_t[:], reads=[ab_b])
            kb.dma("g", SCR["dbgE"], bb_t[:], reads=[bb_b])
        ph.tt("v", pvB_t[:], tcg_t[:], ab_t[:], ALU.mult, [tcg_b, ab_b], [pvB_b])
        ph.tt("g", pgB_t[:], tsg_t[:], bb_t[:], ALU.mult, [tsg_b, bb_b], [pgB_b])
        ph.tt("v", wre_t[:], pvB_t[:], pgB_t[:], ALU.add, [pvB_b, pgB_b], [wre_b])
        ph.tt("g", pgB_t[:], tcg_t[:], bb_t[:], ALU.mult, [tcg_b, bb_b], [pgB_b])
        ph.tt("v", pvB_t[:], tsg_t[:], ab_t[:], ALU.mult, [tsg_b, ab_b], [pvB_b])
        ph.tt("v", wim_t[:], pgB_t[:], pvB_t[:], ALU.subtract, [pgB_b, pvB_b], [wim_b])
        ph.scan("v", gre_t[:], rhog_t[:], wre_t[:], 0.0, ALU.mult, ALU.add,
                [rhog_b, wre_b], [gre_b])
        ph.scan("v", gim_t[:], rhog_t[:], wim_t[:], 0.0, ALU.mult, ALU.add,
                [rhog_b, wim_b], [gim_b])
        ph.tt("v", pvB_t[:], tcg_t[:], gre_t[:], ALU.mult, [tcg_b, gre_b], [pvB_b])
        ph.tt("v", pgB_t[:], tsg_t[:], gim_t[:], ALU.mult, [tsg_b, gim_b], [pgB_b])
        ph.tt("v", hre_t[:], pvB_t[:], pgB_t[:], ALU.subtract, [pvB_b, pgB_b], [hre_b])
        ph.tt("v", pgB_t[:], tcg_t[:], gim_t[:], ALU.mult, [tcg_b, gim_b], [pgB_b])
        ph.tt("v", pvB_t[:], tsg_t[:], gre_t[:], ALU.mult, [tsg_b, gre_b], [pvB_b])
        ph.tt("v", him_t[:], pgB_t[:], pvB_t[:], ALU.add, [pgB_b, pvB_b], [him_b])
        kb.dma("q", SCR["h0re"][m], hre_t[:], reads=[hre_b])
        kb.dma("q", SCR["h0im"][m], him_t[:], reads=[him_b])
    ph.close()
    ph_hold.close()




def load_const_local(ph, name):
    shape, dt = CONST_SPECS[name]
    t, b = ph.sb("kc_" + name, shape, dt)
    ph.kb.dma("q", t[:], ph.st["CONST"][name], writes=[b])
    return t, b


def replicate_row(ph, name, dram_vec, n, PS, PSB):
    KC, KCB = ph.st["KC"], ph.st["KCB"]
    row_t, row_b = ph.sb(name + "_row", [1, n], F32)
    ph.kb.dma("q", row_t[:], dram_vec, writes=[row_b])
    rep_t, rep_b = ph.sb(name + "_rep", [128, n], F32)
    nhalf = (n + 511) // 512
    for h in range(nhalf):
        lo, hi = h * 512, min(n, (h + 1) * 512)
        bank, bankb = PS[5], PSB[5]
        def mk(eng, out_ap=bank[:, 0:hi - lo], l=row_t[0:1, lo:hi]):
            return eng.matmul(out_ap, KC["ONES1F"][:], l, start=True, stop=True)
        ph.kb.op("t", mk, reads=[row_b, KCB["ONES1F"]], writes=[bankb])
        ph.act(rep_t[:, lo:hi], bank[:, 0:hi - lo], AF.Identity, [bankb], [rep_b])
    return rep_t, rep_b


def phase_p0a2(st):
    nc, kb = st["nc"], st["kb"]
    IN, KC, KCB, SCR, PS, PSB = st["IN"], st["KC"], st["KCB"], st["SCR"], st["PS"], st["PSB"]
    ph = Ph(st, "p0a2")

    WM0, WM0B = load_weight_cm(ph, "wm0", IN["L0_attn_W_mid"][0], J, D)
    WM1, WM1B = load_weight_cm(ph, "wm1", IN["L0_attn_W_mid"][1], J, D)
    W1, W1B = load_weight_cm(ph, "w1", IN["L0_attn_w1_w"], J, D)
    W2, W2B = load_weight_cm(ph, "w2", IN["L0_attn_w2_w"], J, D)
    ph.ts("v", WM1[:].rearrange("p k n -> p (k n)"), WM1[:].rearrange("p k n -> p (k n)"),
          -1.0, None, ALU.mult, ALU.bypass, [WM1B], [WM1B])
    bm0_t, bm0_b = load_strip(ph, "bm0", IN["L0_attn_b_mid"][0])
    bm1_t, bm1_b = load_strip(ph, "bm1", IN["L0_attn_b_mid"][1])
    badd_t, badd_b = ph.sb("badd", [128, J], F32)
    ph.stt("v", badd_t[:], bm1_t[:], -1.0, bm0_t[:], ALU.mult, ALU.add, [bm0_b, bm1_b], [badd_b])
    w1b_t, w1b_b = load_strip(ph, "w1b", IN["L0_attn_w1_b"])
    w2b_t, w2b_b = load_strip(ph, "w2b", IN["L0_attn_w2_b"])
    lng_t, lng_b = replicate_row(ph, "lng", IN["L0_attn_ln_g"], D, PS, PSB)
    lnb_t, lnb_b = replicate_row(ph, "lnb", IN["L0_attn_ln_b"], D, PS, PSB)

    hre_t, hre_b = ph.sb("hre", [128, J, C], BF16)
    him_t, him_b = ph.sb("him", [128, J, C], BF16)
    ocm_t, ocm_b = ph.sb("ocm", [128, J, C], BF16)
    out_t, out_b = ph.sb("out", [128, J, C], F32)
    outb_t, outb_b = ph.sb("outb", [128, J, C], BF16)
    y1_t, y1_b = ph.sb("y1", [128, J, C], BF16)
    y2_t, y2_b = ph.sb("y2", [128, J, C], F32)
    y2t_t, y2t_b = ph.sb("y2t", [128, TT, D], F32)
    s0_t, s0_b = ph.sb("s0", [128, TT, D], F32)
    junk_t, junk_b = ph.sb("junk", [128, D], F32)
    ptrf0, ptrf0b = ph.psb("ptrf0", [128, 512], F32)
    ptrf1, ptrf1b = ph.psb("ptrf1", [128, 512], F32)
    PTRF, PTRFB = [ptrf0, ptrf1], [ptrf0b, ptrf1b]

    for c in range(NCH):
        kb.dma("q", hre_t[:], cm_slice(SCR["h0re"], c), writes=[hre_b])
        kb.dma("q", him_t[:], cm_slice(SCR["h0im"], c), writes=[him_b])
        kb.dma("q", ocm_t[:], cm_slice(SCR["o0"], c), writes=[ocm_b])
        for m in range(J):
            p, pb = PS[m % 2], PSB[m % 2]
            for k in range(J):
                ph.mm(p[:], WM0[:, k, m * 128:(m + 1) * 128], hre_t[:, k, :],
                      k == 0, False, [WM0B, hre_b], [pb])
            for k in range(J):
                ph.mm(p[:], WM1[:, k, m * 128:(m + 1) * 128], him_t[:, k, :],
                      False, k == J - 1, [WM1B, him_b], [pb])
            ph.stt("v", out_t[:, m, :], p[:], badd_t[:, m:m + 1], ocm_t[:, m, :],
                   ALU.add, ALU.add, [pb, badd_b, ocm_b], [out_b])
        ph.act(outb_t[:].rearrange("p k n -> p (k n)"), out_t[:].rearrange("p k n -> p (k n)"),
               AF.Identity, [out_b], [outb_b])
        for m in range(J):
            p, pb = PS[2 + m % 2], PSB[2 + m % 2]
            for k in range(J):
                ph.mm(p[:], W1[:, k, m * 128:(m + 1) * 128], outb_t[:, k, :],
                      k == 0, k == J - 1, [W1B, outb_b], [pb])
            ph.act(y1_t[:, m, :], p[:], AF.Gelu, [pb, w1b_b], [y1_b],
                   bias=w1b_t[:, m:m + 1])
        for m in range(J):
            p, pb = PS[4 + m % 2], PSB[4 + m % 2]
            for k in range(J):
                ph.mm(p[:], W2[:, k, m * 128:(m + 1) * 128], y1_t[:, k, :],
                      k == 0, k == J - 1, [W2B, y1_b], [pb])
            ph.stt("v", y2_t[:, m, :], p[:], w2b_t[:, m:m + 1], out_t[:, m, :],
                   ALU.add, ALU.add, [pb, w2b_b, out_b], [y2_b])
        for i in range(TT):
            for half in range(2):
                bank, bankb = PTRF[half], PTRFB[half]
                for mm_ in range(4):
                    m = half * 4 + mm_
                    ph.transpose(bank[:, mm_ * 128:(mm_ + 1) * 128],
                                 y2_t[:, m, i * 128:(i + 1) * 128],
                                 KC["IDENT_F"][:], [y2_b, KCB["IDENT_F"]], [bankb])
                ph.act(y2t_t[:, i, half * 512:(half + 1) * 512], bank[:], AF.Identity,
                       [bankb], [y2t_b])
        mean, rstd, (mean_b, rstd_b) = ln_stats_tm(ph, y2t_t, y2t_b, junk_t, junk_b, f"a2st{c}")
        for i in range(TT):
            ph.ts("v", y2t_t[:, i, :], y2t_t[:, i, :], mean[:, i:i + 1], rstd[:, i:i + 1],
                  ALU.subtract, ALU.mult, [mean_b, rstd_b, y2t_b], [y2t_b])
            tmpf = out_t[:].rearrange("p k n -> p (k n)")
            ph.tt("v", tmpf[:, i * D:(i + 1) * D], y2t_t[:, i, :], lng_t[:], ALU.mult,
                  [y2t_b, lng_b], [out_b])
            ph.tt("v", s0_t[:, i, :], tmpf[:, i * D:(i + 1) * D], lnb_t[:], ALU.add,
                  [out_b, lnb_b], [s0_b])
        kb.dma("q", tm_slice(SCR["s0"], c), s0_t[:], reads=[s0_b])
    ph.close()


def phase_p0b(st):
    nc, kb = st["nc"], st["kb"]
    IN, KC, KCB, SCR, PS, PSB = st["IN"], st["KC"], st["KCB"], st["SCR"], st["PS"], st["PSB"]
    ph = Ph(st, "p0b")

    sr_t, sr_b = ph.sb("sr", [64, D], F32)
    kb.dma("q", sr_t[:], IN["L0_ffn_signatures_raw"], writes=[sr_b])
    q1_t, q1_b = ph.sb("q1", [64, D], F32)
    q2_t, q2_b = ph.sb("q2", [64, D], F32)
    qb_t, qb_b = ph.sb("qb", [64, D], BF16)
    ph.ts("v", q1_t[:], sr_t[:], 0.3, None, ALU.is_gt, ALU.bypass, [sr_b], [q1_b])
    ph.ts("v", q2_t[:], sr_t[:], -0.3, None, ALU.is_lt, ALU.bypass, [sr_b], [q2_b])
    ph.tt("v", qb_t[:], q1_t[:], q2_t[:], ALU.subtract, [q1_b, q2_b], [qb_b])
    cs_t, cs_b = ph.sb("cs", [8, D], BF16)
    csp_t, csp_b = ph.sb("csp", [8, 512], F32)
    csn_t, csn_b = ph.sb("csn", [8, 512], F32)
    for half in range(2):
        bank, bankb = PS[5], PSB[5]
        def mkc(eng, o=bank[0:8, 0:512], r=qb_t[:, half * 512:(half + 1) * 512]):
            return eng.matmul(o, KC["GSEL8"][:], r, start=True, stop=True)
        kb.op("t", mkc, reads=[qb_b, KCB["GSEL8"]], writes=[bankb])
        ph.ts("v", csp_t[:], bank[0:8, 0:512], 0.0, None, ALU.is_gt, ALU.bypass,
              [bankb], [csp_b])
        ph.ts("v", csn_t[:], bank[0:8, 0:512], 0.0, None, ALU.is_lt, ALU.bypass,
              [bankb], [csn_b])
        ph.tt("v", cs_t[:, half * 512:(half + 1) * 512], csp_t[:], csn_t[:],
              ALU.subtract, [csp_b, csn_b], [cs_b])
    sig_t, sig_b = ph.sb("sigcat", [128, J, 72], BF16)
    ptb0, ptb0b = ph.psb("ptb0", [128, 512], BF16)
    for k in range(J):
        ph.transpose(ptb0[0:128, 0:8], cs_t[:, k * 128:(k + 1) * 128],
                     KC["IDENT_B"][0:8, 0:8], [cs_b, KCB["IDENT_B"]], [ptb0b])
        ph.copy("v", sig_t[:, k, 0:8], ptb0[0:128, 0:8], [ptb0b], [sig_b])
        ph.transpose(ptb0[0:128, 8:72], qb_t[:, k * 128:(k + 1) * 128],
                     KC["IDENT_B"][0:64, 0:64], [qb_b, KCB["IDENT_B"]], [ptb0b])
        ph.copy("v", sig_t[:, k, 8:72], ptb0[0:128, 8:72], [ptb0b], [sig_b])
    spl_t, spl_b = ph.sb("spl", [64, 72], BF16)
    w1flat = IN["L0_ffn_spline_w1"].rearrange("t i g -> t (i g)")
    kb.dma("g", spl_t[:, 0:32], w1flat, writes=[spl_b])
    kb.dma("g", spl_t[:, 32:48], IN["L0_ffn_spline_b1"], writes=[spl_b])
    kb.dma("g", spl_t[:, 48:64], IN["L0_ffn_spline_w2"].rearrange("t g o -> t (g o)"),
           writes=[spl_b])
    kb.dma("g", spl_t[:, 64:65], IN["L0_ffn_spline_b2"], writes=[spl_b])
    dirs_t, dirs_b = ph.sb("dirs", [64, D], BF16)
    kb.dma("g", dirs_t[:], IN["L0_ffn_directions"], writes=[dirs_b])
    CW1, CW1B = load_weight_cm(ph, "cw1", IN["L0_ffn_compress_w1"], J, 256)
    CW2, CW2B = load_weight_cm(ph, "cw2", IN["L0_ffn_compress_w2"], 2, 2)
    cb1_t, cb1_b = load_strip(ph, "cb1", IN["L0_ffn_compress_b1"], jn=2)
    row_t, row_b = ph.sb("repsrow", [1, 16], F32)
    kn_t, kn_b = ph.sb("knots", [1, 8], F32)
    kb.dma("q", kn_t[:], IN["L0_ffn_knot_values"], writes=[kn_b])
    kb.dma("q", row_t[0:1, 0:1], IN["L0_ffn_temperature"], writes=[row_b])
    kb.dma("q", row_t[0:1, 1:2], IN["L0_ffn_output_scale"], writes=[row_b])
    for tb in range(4):
        kb.dma("q", row_t[0:1, 8 + 2 * tb:10 + 2 * tb],
               IN["L0_ffn_compress_b2"], writes=[row_b])
    kb.strong = True
    ph.ts("v", row_t[0:1, 0:1], row_t[0:1, 0:1], 1e-6, None, ALU.add, ALU.bypass, [row_b], [row_b])
    def mkrec(eng, o=row_t[0:1, 0:1], i=row_t[0:1, 0:1]):
        return eng.reciprocal(o, i)
    kb.op("v", mkrec, reads=[row_b], writes=[row_b])
    ph.ts("v", row_t[0:1, 2:3], kn_t[0:1, 0:1], 1.0, None, ALU.mult, ALU.bypass, [kn_b], [row_b])
    ph.tt("v", row_t[0:1, 3:10], kn_t[0:1, 1:8], kn_t[0:1, 0:7], ALU.subtract, [kn_b], [row_b])
    kb.strong = False
    reps_t, reps_b = ph.sb("reps", [128, 16], F32)
    bank5, bank5b = PS[5], PSB[5]
    def mkrep(eng, o=bank5[:, 0:16], r=row_t[0:1, 0:16]):
        return eng.matmul(o, KC["ONES1F"][:], r, start=True, stop=True)
    kb.op("t", mkrep, reads=[row_b, KCB["ONES1F"]], writes=[bank5b])
    ph.act(reps_t[:], bank5[:, 0:16], AF.Identity, [bank5b], [reps_b])
    g0g_t, g0g_b = replicate_row(ph, "g0g", IN["L0_ffn_norm_gamma"], D, PS, PSB)
    g0b_t, g0b_b = replicate_row(ph, "g0b", IN["L0_ffn_norm_beta"], D, PS, PSB)

    xn_t, xn_b = ph.sb("xn", [128, TT, D], F32)
    xnb_t, xnb_b = ph.sb("xnb", [128, TT, D], BF16)
    xnc_t, xnc_b = ph.sb("xnc", [128, J, C], BF16)
    xt_t, xt_b = ph.sb("xt", [128, TT, D], F32)
    s0_t, s0_b = ph.sb("s0", [128, TT, D], F32)
    yb_t, yb_b = ph.sb("yb", [128, TT, D], F32)
    tmp_t, tmp_b = ph.sb("tmp", [128, TT, D], F32)
    junk_t, junk_b = ph.sb("junk", [128, D], F32)
    F_t, F_b = ph.sb("fgate", [128, 4 * 72], F32)
    VA_t, VA_b = ph.sb("va", [128, 4 * 72], F32)
    VB_t, VB_b = ph.sb("vb", [128, 4 * 72], F32)
    CL_t, CL_b = ph.sb("cl", [128, 4 * 72], F32)
    hcm_t, hcm_b = ph.sb("hcm", [128, 2, C], BF16)
    compp_t, compp_b = ph.sb("compp", [128, 8], F32)
    gath_t, gath_b = ph.sb("gath", [128, 4 * 72], F32)
    oh_t, oh_b = ph.sb("ohot", [128, TT, 64], BF16)
    ohs_t, ohs_b = ph.sb("ohs", [128, TT, 64], BF16)
    oht_t, oht_b = ph.sb("oht", [64, C], BF16)
    ohst_t, ohst_b = ph.sb("ohst", [64, C], BF16)
    mags_t, mags_b = ph.sb("mags", [128, TT], F32)
    hh_t, hh_b = ph.sb("hh", [128, TT, 16], F32)
    mgp_t, mgp_b = ph.sb("mgp", [128, TT, 16], F32)
    mxn_t, mxn_b = ph.sb("mxn", [128, TT, 2], F32)
    CA8_t, CA8_b = ph.sb("ca8", [128, TT, 8], F32)
    CB8_t, CB8_b = ph.sb("cb8", [128, TT, 8], F32)
    CA64_t, CA64_b = ph.sb("ca64", [128, TT, 64], F32)
    CB64_t, CB64_b = ph.sb("cb64", [128, TT, 64], F32)
    ptrb, ptrbb = ph.psb("ptrb", [128, 512], BF16)

    for c in range(NCH):
        kb.dma("q", xn_t[:], tm_slice(SCR["xn0"], c), writes=[xn_b])
        kb.dma("q", xt_t[:], tm_slice(IN["x"], c), writes=[xt_b])
        kb.dma("q", s0_t[:], tm_slice(SCR["s0"], c), writes=[s0_b])
        ph.act(xnb_t[:].rearrange("p i d -> p (i d)"), xn_t[:].rearrange("p i d -> p (i d)"),
               AF.Identity, [xn_b], [xnb_b])
        for j in range(J):
            bank, bankb = ptrb, ptrbb
            for i in range(TT):
                ph.transpose(bank[:, i * 128:(i + 1) * 128], xnb_t[:, i, j * 128:(j + 1) * 128],
                             KC["IDENT_B"][:], [xnb_b, KCB["IDENT_B"]], [bankb])
            ph.copy("v", xnc_t[:, j, :], bank[:], [bankb], [xnc_b])
        psc, pscb = PS[0], PSB[0]
        for i in range(TT):
            for k in range(J):
                ph.mm(psc[:, i * 72:(i + 1) * 72], xnc_t[:, k, i * 128:(i + 1) * 128],
                      sig_t[:, k, :], k == 0, k == J - 1, [xnc_b, sig_b], [pscb])
        ph.act(F_t[:], psc[:, 0:288], AF.Sigmoid, [pscb, reps_b], [F_b],
               scale=reps_t[:, 0:1])
        for jj in range(7):
            eng = "v"
            ph.ts(eng, CL_t[:], F_t[:], 7.0, float(-jj), ALU.mult, ALU.add, [F_b], [CL_b])
            ph.ts(eng, CL_t[:], CL_t[:], 1.0, 0.0, ALU.min, ALU.max, [CL_b], [CL_b])
            if jj == 0:
                ph.ts(eng, VA_t[:], CL_t[:], reps_t[:, 3:4], reps_t[:, 2:3],
                      ALU.mult, ALU.add, [CL_b, reps_b], [VA_b])
            elif jj % 2 == 1:
                ph.stt(eng, VB_t[:], CL_t[:], reps_t[:, 3 + jj:4 + jj], VA_t[:],
                       ALU.mult, ALU.add, [CL_b, reps_b, VA_b], [VB_b])
            else:
                ph.stt(eng, VA_t[:], CL_t[:], reps_t[:, 3 + jj:4 + jj], VB_t[:],
                       ALU.mult, ALU.add, [CL_b, reps_b, VB_b], [VA_b])
        VAL_t, VAL_b = VA_t, VA_b
        kb.strong = True
        def cg(i):
            return VAL_t[:, i * 72:i * 72 + 8]
        def tg(i):
            return VAL_t[:, i * 72 + 8:i * 72 + 72]
        for i in range(TT):
            ph.red(mxn_t[:, i, 0:1], cg(i), ALU.max, [VAL_b], [mxn_b])
        for i in range(TT):
            ph.ts("v", CA8_t[:, i, :], cg(i), mxn_t[:, i, 0:1], None, ALU.is_ge,
                  ALU.bypass, [VAL_b, mxn_b], [CA8_b])
        for i in range(TT):
            ph.stt("v", CB8_t[:, i, :], CA8_t[:, i, :], -1e6, KC["C8P"][:], ALU.mult,
                   ALU.add, [CA8_b, KCB["C8P"]], [CB8_b])
        for i in range(TT):
            ph.red(mxn_t[:, i, 1:2], CB8_t[:, i, :], ALU.min, [CB8_b], [mxn_b])
        for i in range(TT):
            ph.ts("v", CA64_t[:, i, :], KC["C_CLUST"][:], mxn_t[:, i, 1:2], None,
                  ALU.is_equal, ALU.bypass, [KCB["C_CLUST"], mxn_b], [CA64_b])
        for i in range(TT):
            ph.stt("v", CB64_t[:, i, :], CA64_t[:, i, :], 1e6, tg(i), ALU.mult, ALU.add,
                   [CA64_b, VAL_b], [CB64_b])
        for i in range(TT):
            ph.red(mxn_t[:, i, 0:1], CB64_t[:, i, :], ALU.max, [CB64_b], [mxn_b])
        for i in range(TT):
            ph.ts("v", CA64_t[:, i, :], CB64_t[:, i, :], mxn_t[:, i, 0:1], None,
                  ALU.is_ge, ALU.bypass, [CB64_b, mxn_b], [CA64_b])
        for i in range(TT):
            ph.stt("v", CB64_t[:, i, :], CA64_t[:, i, :], -1e6, KC["C64P"][:],
                   ALU.mult, ALU.add, [CA64_b, KCB["C64P"]], [CB64_b])
        for i in range(TT):
            ph.red(mxn_t[:, i, 1:2], CB64_t[:, i, :], ALU.min, [CB64_b], [mxn_b])
        for i in range(TT):
            ph.ts("v", oh_t[:, i, :], KC["C_IOTA64"][:], mxn_t[:, i, 1:2], None,
                  ALU.is_equal, ALU.bypass, [KCB["C_IOTA64"], mxn_b], [oh_b])
        kb.strong = False
        if "dbg2" in DEBUG_TAPS:
            for i in range(TT):
                kb.dma("g", SCR["dbg2"][:, c * 256 + i * 64:c * 256 + (i + 1) * 64],
                       oh_t[:, i, :], reads=[oh_b])
        for mh in range(2):
            p, pb = PS[1 + mh], PSB[1 + mh]
            for k in range(J):
                ph.mm(p[:], CW1[:, k, mh * 128:(mh + 1) * 128], xnc_t[:, k, :],
                      k == 0, k == J - 1, [CW1B, xnc_b], [pb])
            ph.act(hcm_t[:, mh, :], p[:], AF.Gelu, [pb, cb1_b], [hcm_b],
                   bias=cb1_t[:, mh:mh + 1])
        pcp, pcpb = PS[3], PSB[3]
        for i in range(TT):
            for mh in range(2):
                ph.mm(pcp[:, i * 2:(i + 1) * 2], hcm_t[:, mh, i * 128:(i + 1) * 128],
                      CW2[:, mh, :], mh == 0, mh == 1, [hcm_b, CW2B], [pcpb])
        ph.tt("v", compp_t[:], pcp[:, 0:8], reps_t[:, 8:16], ALU.add,
              [pcpb, reps_b], [compp_b])
        ph.act(compp_t[:], compp_t[:], AF.Tanh, [compp_b], [compp_b])
        for i in range(TT):
            ph.transpose(ptrb[0:64, i * 128:(i + 1) * 128],
                         oh_t[:, i, :], KC["IDENT_B"][:], [oh_b, KCB["IDENT_B"]], [ptrbb])
        ph.copy("v", oht_t[:], ptrb[0:64, 0:512], [ptrbb], [oht_b])
        pg_, pgb_ = PS[4], PSB[4]
        for i in range(TT):
            ph.mm(pg_[:, i * 72:i * 72 + 72], oht_t[:, i * 128:(i + 1) * 128], spl_t[:],
                  True, True, [oht_b, spl_b], [pgb_])
        ph.act(gath_t[:], pg_[:, 0:288], AF.Identity, [pgb_], [gath_b])
        kb.strong = True
        def g0(i):
            return gath_t[:, i * 72:i * 72 + 72]
        for i in range(TT):
            ph.stt("v", mgp_t[:, i, :], g0(i)[:, 0:16], compp_t[:, 2 * i:2 * i + 1],
                   g0(i)[:, 32:48], ALU.mult, ALU.add, [gath_b, compp_b], [mgp_b])
        for i in range(TT):
            ph.stt("v", hh_t[:, i, :], g0(i)[:, 16:32], compp_t[:, 2 * i + 1:2 * i + 2],
                   mgp_t[:, i, :], ALU.mult, ALU.add, [gath_b, compp_b, mgp_b], [hh_b])
        for i in range(TT):
            ph.ts("v", hh_t[:, i, :], hh_t[:, i, :], 0.0, None, ALU.max, ALU.bypass,
                  [hh_b], [hh_b])
        for i in range(TT):
            ph.tt("v", mgp_t[:, i, :], hh_t[:, i, :], g0(i)[:, 48:64], ALU.mult,
                  [hh_b, gath_b], [mgp_b])
        for i in range(TT):
            ph.act(junk_t[:, 0:16], mgp_t[:, i, :], AF.Identity, [mgp_b], [junk_b, mags_b],
                   accum_out=mags_t[:, i:i + 1])
        for i in range(TT):
            ph.ts("v", mags_t[:, i:i + 1], mags_t[:, i:i + 1], g0(i)[:, 64:65],
                  reps_t[:, 1:2], ALU.add, ALU.mult, [mags_b, gath_b, reps_b], [mags_b])
        for i in range(TT):
            ph.ts("v", ohs_t[:, i, :], oh_t[:, i, :], mags_t[:, i:i + 1], None,
                  ALU.mult, ALU.bypass, [oh_b, mags_b], [ohs_b])
        kb.strong = False
        for i in range(TT):
            ph.transpose(ptrb[0:64, i * 128:(i + 1) * 128], ohs_t[:, i, :],
                         KC["IDENT_B"][:], [ohs_b, KCB["IDENT_B"]], [ptrbb])
        ph.copy("v", ohst_t[:], ptrb[0:64, 0:512], [ptrbb], [ohst_b])
        for i in range(TT):
            p, pb = PS[5 - (i % 2)], PSB[5 - (i % 2)]
            for half in range(2):
                ph.mm(p[:], ohst_t[:, i * 128:(i + 1) * 128],
                      dirs_t[:, half * 512:(half + 1) * 512],
                      True, True, [ohst_b, dirs_b], [pb])
                ph.tt("v", yb_t[:, i, half * 512:(half + 1) * 512], p[:],
                      xn_t[:, i, half * 512:(half + 1) * 512], ALU.add,
                      [pb, xn_b], [yb_b])
        mean, rstd, (mean_b, rstd_b) = ln_stats_tm(ph, yb_t, yb_b, junk_t, junk_b, f"b0st{c}")
        for i in range(TT):
            ph.ts("v", yb_t[:, i, :], yb_t[:, i, :], mean[:, i:i + 1], rstd[:, i:i + 1],
                  ALU.subtract, ALU.mult, [mean_b, rstd_b, yb_b], [yb_b])
            ph.tt("v", tmp_t[:, i, :], yb_t[:, i, :], g0g_t[:], ALU.mult,
                  [yb_b, g0g_b], [tmp_b])
            ph.tt("v", yb_t[:, i, :], tmp_t[:, i, :], g0b_t[:], ALU.add,
                  [tmp_b, g0b_b], [yb_b])
        ph.act(yb_t[:].rearrange("p i d -> p (i d)"), yb_t[:].rearrange("p i d -> p (i d)"),
               AF.Silu, [yb_b], [yb_b])
        ph.tt("v", tmp_t[:].rearrange("p i d -> p (i d)"),
              s0_t[:].rearrange("p i d -> p (i d)"),
              yb_t[:].rearrange("p i d -> p (i d)"), ALU.mult, [s0_b, yb_b], [tmp_b])
        ph.tt("v", s0_t[:].rearrange("p i d -> p (i d)"),
              xt_t[:].rearrange("p i d -> p (i d)"),
              tmp_t[:].rearrange("p i d -> p (i d)"), ALU.add, [xt_b, tmp_b], [s0_b])
        kb.dma("q", tm_slice(SCR["x1"], c), s0_t[:], reads=[s0_b])
        mean, rstd, (mean_b, rstd_b) = ln_stats_tm(ph, s0_t, s0_b, junk_t, junk_b, f"b1st{c}")
        for i in range(TT):
            ph.ts("v", s0_t[:, i, :], s0_t[:, i, :], mean[:, i:i + 1], rstd[:, i:i + 1],
                  ALU.subtract, ALU.mult, [mean_b, rstd_b, s0_b], [s0_b])
        kb.dma("q", tm_slice(SCR["xn1"], c), s0_t[:], reads=[s0_b])
    ph.close()


def affine_ln_tm(ph, x_t, x_b, out_t, out_b, tmp_t, tmp_b, mean, rstd, mean_b, rstd_b,
                 w_t, w_b, b_t, b_b):
    for i in range(TT):
        ph.ts("v", tmp_t[:, i, :], x_t[:, i, :], mean[:, i:i + 1], rstd[:, i:i + 1],
              ALU.subtract, ALU.mult, [mean_b, rstd_b, x_b], [tmp_b])
        ph.tt("v", tmp_t[:, i, :], tmp_t[:, i, :], w_t[:], ALU.mult, [tmp_b, w_b], [tmp_b])
        ph.tt("v", out_t[:, i, :], tmp_t[:, i, :], b_t[:], ALU.add, [tmp_b, b_b], [out_b])


def phase_p1a(st):
    nc, kb = st["nc"], st["kb"]
    IN, KC, KCB, SCR, PS, PSB = st["IN"], st["KC"], st["KCB"], st["SCR"], st["PS"], st["PSB"]
    ph = Ph(st, "p1a")
    lscw_t, lscw_b = replicate_row(ph, "lscw", IN["L1_attn_ln_sc_w"], D, PS, PSB)
    lscb_t, lscb_b = replicate_row(ph, "lscb", IN["L1_attn_ln_sc_b"], D, PS, PSB)
    xt_t, xt_b = ph.sb("xt", [128, TT, D], F32)
    x2_t, x2_b = ph.sb("x2", [128, TT, D], F32)
    xlo_t, xlo_b = ph.sb("xlo", [128, TT, D], F32)
    junk_t, junk_b = ph.sb("junk", [128, D], F32)
    xb_t, xb_b = ph.sb("xb", [128, TT, D], BF16)
    x1b_t, x1b_b = ph.sb("x1b", [128, TT, D], BF16)
    xlob_t, xlob_b = ph.sb("xlob", [128, TT, D], BF16)
    cm2_t, cm2_b = ph.sb("cm2", [128, J, C], BF16)
    cm1_t, cm1_b = ph.sb("cm1", [128, J, C], BF16)
    cml_t, cml_b = ph.sb("cml", [128, J, C], BF16)
    ptr0, ptr0b = ph.psb("ptr0", [128, 512], BF16)
    ptr1, ptr1b = ph.psb("ptr1", [128, 512], BF16)
    for c in range(NCH):
        kb.dma("q", xt_t[:], tm_slice(SCR["xn1"], c), writes=[xt_b])
        mean, rstd, (mean_b, rstd_b) = ln_stats_tm(ph, xt_t, xt_b, junk_t, junk_b, f"p1a{c}")
        affine_ln_tm(ph, xt_t, xt_b, x2_t, x2_b, x2_t, x2_b, mean, rstd, mean_b, rstd_b,
                     lscw_t, lscw_b, lscb_t, lscb_b)
        ph.act(xb_t[:].rearrange("p i d -> p (i d)"), x2_t[:].rearrange("p i d -> p (i d)"),
               AF.Identity, [x2_b], [xb_b])
        ph.act(xlo_t[:].rearrange("p i d -> p (i d)"), xb_t[:].rearrange("p i d -> p (i d)"),
               AF.Identity, [xb_b], [xlo_b])
        ph.tt("v", xlo_t[:].rearrange("p i d -> p (i d)"),
              x2_t[:].rearrange("p i d -> p (i d)"),
              xlo_t[:].rearrange("p i d -> p (i d)"), ALU.subtract, [x2_b, xlo_b], [xlo_b])
        ph.act(xlob_t[:].rearrange("p i d -> p (i d)"), xlo_t[:].rearrange("p i d -> p (i d)"),
               AF.Identity, [xlo_b], [xlob_b])
        ph.act(x1b_t[:].rearrange("p i d -> p (i d)"), xt_t[:].rearrange("p i d -> p (i d)"),
               AF.Identity, [xt_b], [x1b_b])
        for j in range(J):
            for i in range(TT):
                ph.transpose(ptr0[:, i * 128:(i + 1) * 128], xb_t[:, i, j * 128:(j + 1) * 128],
                             KC["IDENT_B"][:], [xb_b, KCB["IDENT_B"]], [ptr0b])
                ph.transpose(ptr1[:, i * 128:(i + 1) * 128], x1b_t[:, i, j * 128:(j + 1) * 128],
                             KC["IDENT_B"][:], [x1b_b, KCB["IDENT_B"]], [ptr1b])
            ph.copy("v", cm2_t[:, j, :], ptr0[:], [ptr0b], [cm2_b])
            ph.copy("v", cm1_t[:, j, :], ptr1[:], [ptr1b], [cm1_b])
            for i in range(TT):
                ph.transpose(ptr0[:, i * 128:(i + 1) * 128], xlob_t[:, i, j * 128:(j + 1) * 128],
                             KC["IDENT_B"][:], [xlob_b, KCB["IDENT_B"]], [ptr0b])
            ph.copy("v", cml_t[:, j, :], ptr0[:], [ptr0b], [cml_b])
        kb.dma("q", cm_slice(SCR["xn2cm"], c), cm2_t[:], reads=[cm2_b])
        kb.dma("q", cm_slice(SCR["xn1cm"], c), cm1_t[:], reads=[cm1_b])
        kb.dma("q", cm_slice(SCR["xn2cmlo"], c), cml_t[:], reads=[cml_b])
    ph.close()


def phase_p1b(st):
    nc, kb = st["nc"], st["kb"]
    IN, KC, KCB, SCR, PS, PSB = st["IN"], st["KC"], st["KCB"], st["SCR"], st["PS"], st["PSB"]
    ph = Ph(st, "p1bA")
    xnc_t, xnc_b = ph.sb("xnc", [128, J, T], BF16)
    xncl_t, xncl_b = ph.sb("xncl", [128, J, T], BF16)
    for j in range(J):
        kb.dma("q", xnc_t[:, j, :], SCR["xn2cm"][j], writes=[xnc_b])
        kb.dma("q", xncl_t[:, j, :], SCR["xn2cmlo"][j], writes=[xncl_b])
    FCW, FCWB = load_weight_cm(ph, "fcw", IN["L1_attn_fc_w"], J, D)
    fcb_t, fcb_b = load_strip(ph, "fcb", IN["L1_attn_fc_b"])
    phb_t, phb_b = load_strip(ph, "phb", IN["L1_attn_phz_b"])
    scb_t, scb_b = ph.sb("scb", [128, J], F32)
    ph.tt("v", scb_t[:], phb_t[:], KC["SCALE_STRIP"][:], ALU.mult,
          [phb_b, KCB["SCALE_STRIP"]], [scb_b])
    ybf_t, ybf_b = ph.sb("ybf", [128, C], BF16)
    phi_t, phi_b = ph.sb("phi", [128, T], F32)
    for m in range(J):
        for c in range(NCH):
            p, pb = PS[c % 2], PSB[c % 2]
            for k in range(J):
                ph.mm(p[:], FCW[:, k, m * 128:(m + 1) * 128], xnc_t[:, k, c * C:(c + 1) * C],
                      k == 0, k == J - 1, [FCWB, xnc_b], [pb])
            ph.act(ybf_t[:], p[:], AF.Silu, [pb, fcb_b], [ybf_b], bias=fcb_t[:, m:m + 1])
            kb.dma("q", SCR["y1"][m][:, c * C:(c + 1) * C], ybf_t[:], reads=[ybf_b])
    phsrc = IN["L1_attn_phz_w"].rearrange("(k p) n -> p k n", p=128)
    PZF, WHI, WLO, WHF = [], [], [], []
    for d in range(2):
        PZF.append(ph.sb(f"pzf{d}", [128, J, 128], F32))
        WHI.append(ph.sb(f"whi{d}", [128, J, 128], BF16))
        WLO.append(ph.sb(f"wlo{d}", [128, J, 128], BF16))
        WHF.append(ph.sb(f"whf{d}", [128, J, 128], F32))
    for m in range(J):
        d = m % 2
        pzf_t, pzf_b = PZF[d]; whi_t, whi_b = WHI[d]
        wlo_t, wlo_b = WLO[d]; whf_t, whf_b = WHF[d]
        kb.dma("q", pzf_t[:], phsrc[:, :, m * 128:(m + 1) * 128], writes=[pzf_b])
        ph.act(whi_t[:].rearrange("p k n -> p (k n)"), pzf_t[:].rearrange("p k n -> p (k n)"),
               AF.Identity, [pzf_b], [whi_b])
        ph.act(whf_t[:].rearrange("p k n -> p (k n)"), whi_t[:].rearrange("p k n -> p (k n)"),
               AF.Identity, [whi_b], [whf_b])
        ph.tt("v", wlo_t[:].rearrange("p k n -> p (k n)"), pzf_t[:].rearrange("p k n -> p (k n)"),
              whf_t[:].rearrange("p k n -> p (k n)"), ALU.subtract, [pzf_b, whf_b], [wlo_b])
        for c in range(NCH):
            p2, p2b = PS[2 + c % 2], PSB[2 + c % 2]
            for k in range(J):
                ph.mm(p2[:], whi_t[:, k, :], xnc_t[:, k, c * C:(c + 1) * C],
                      k == 0, False, [whi_b, xnc_b], [p2b])
            for k in range(J):
                ph.mm(p2[:], whi_t[:, k, :], xncl_t[:, k, c * C:(c + 1) * C],
                      False, False, [whi_b, xncl_b], [p2b])
            for k in range(J):
                ph.mm(p2[:], wlo_t[:, k, :], xnc_t[:, k, c * C:(c + 1) * C],
                      False, k == J - 1, [wlo_b, xnc_b], [p2b])
            ph.act(phi_t[:, c * C:(c + 1) * C], p2[:], AF.Identity, [p2b, scb_b, KCB["SCALE_STRIP"]],
                   [phi_b], scale=KC["SCALE_STRIP"][:, m:m + 1], bias=scb_t[:, m:m + 1])
        kb.dma("q", SCR["phi1"][m], phi_t[:], reads=[phi_b])
    ph.close()
    kb.barrier()
    ph = Ph(st, "p1bB")
    e4k_t, e4k_b = load_const_local(ph, "E4K")
    zb_t, zb_b = ph.sb("zeros", [128, T], F32)
    ph.memset("v", zb_t[:], 0.0, [zb_b])
    lre_t, lre_b = load_strip(ph, "lre", IN["L1_attn_lci_re"])
    lim_t, lim_b = load_strip(ph, "lim", IN["L1_attn_lci_im"])
    A_t, A_b = ph.sb("bufA", [128, T], F32)
    B_t, B_b = ph.sb("bufB", [128, T], F32)
    C_t, C_b = ph.sb("bufC", [128, T], F32)
    D_t, D_b = ph.sb("bufD", [128, T], F32)
    E_t, E_b = ph.sb("bufE", [128, T], F32)
    PH, PHB = [], []
    XB, XBB = [], []
    for d in range(2):
        t, b = ph.sb(f"phin{d}", [128, T], F32); PH.append(t); PHB.append(b)
        t, b = ph.sb(f"xin{d}", [128, T], BF16); XB.append(t); XBB.append(b)
    hbf_t, hbf_b = ph.sb("hbf", [128, T], F32)
    p08_t, p08_b = ph.sb("p08", [128, J], F32)
    kb.dma("q", p08_t[:], SCR["phi1"][:, :, 0:1].rearrange("j p o -> p (j o)"),
           writes=[p08_b], slow_ok=True)
    k8_t, k8_b = ph.sb("k8", [128, J], F32)
    k8b_t, k8b_b = ph.sb("k8b", [128, J], F32)
    r8_t, r8_b = ph.sb("r8", [128, J], F32)
    s8_t, s8_b = ph.sb("s8", [128, J], F32)
    c8_t, c8_b = ph.sb("c8", [128, J], F32)
    la_t, la_b = ph.sb("la8", [128, J], F32)
    lb_t, lb_b = ph.sb("lb8", [128, J], F32)
    u_t, u_b = ph.sb("u8", [128, 4, J], F32)
    kb.strong = True
    ph.ts("v", k8_t[:], p08_t[:], 1.0 / TWO_PI, MAGIC, ALU.mult, ALU.add, [p08_b], [k8_b])
    ph.tss("v", lre_t[:], lre_t[:], float(np.exp(-0.001)), ALU.mult, [lre_b], [lre_b])
    ph.ts("v", k8b_t[:], k8_t[:], MAGIC, TWO_PI, ALU.subtract, ALU.mult, [k8_b], [k8b_b])
    ph.tss("v", lim_t[:], lim_t[:], float(np.exp(-0.001)), ALU.mult, [lim_b], [lim_b])
    ph.tt("v", r8_t[:], p08_t[:], k8b_t[:], ALU.subtract, [p08_b, k8b_b], [r8_b])
    ph.act(s8_t[:], r8_t[:], AF.Sin, [r8_b], [s8_b])
    ph.act(r8_t[:], r8_t[:], AF.Abs, [r8_b], [r8_b])
    ph.act(c8_t[:], r8_t[:], AF.Sin, [r8_b], [c8_b], scale=-1.0, bias=PI / 2)
    ph.tt("v", u_t[:, 0, :], c8_t[:], lre_t[:], ALU.mult, [c8_b, lre_b], [u_b])
    ph.tt("v", u_t[:, 1, :], s8_t[:], lim_t[:], ALU.mult, [s8_b, lim_b], [u_b])
    ph.tt("v", u_t[:, 2, :], s8_t[:], lre_t[:], ALU.mult, [s8_b, lre_b], [u_b])
    ph.tt("v", u_t[:, 3, :], c8_t[:], lim_t[:], ALU.mult, [c8_b, lim_b], [u_b])
    ph.tt("v", la_t[:], u_t[:, 0, :], u_t[:, 1, :], ALU.subtract, [u_b], [la_b])
    ph.tt("v", lb_t[:], u_t[:, 2, :], u_t[:, 3, :], ALU.add, [u_b], [lb_b])
    kb.strong = False
    if "dbgA" in DEBUG_TAPS:
        kb.dma("q", SCR["dbg"][:, 0:8], la_t[:], reads=[la_b])
        kb.dma("q", SCR["dbg"][:, 8:16], lb_t[:], reads=[lb_b])
        kb.dma("q", SCR["dbg"][:, 16:24], p08_t[:], reads=[p08_b])
        kb.dma("q", SCR["dbg"][:, 24:32], s8_t[:], reads=[s8_b])
        kb.dma("q", SCR["dbg"][:, 32:40], c8_t[:], reads=[c8_b])
        kb.dma("q", SCR["dbg"][:, 40:48], lre_t[:], reads=[lre_b])
        kb.dma("q", SCR["dbg"][:, 48:56], lim_t[:], reads=[lim_b])
    for m in range(J):
        d = m % 2
        kb.dma("q", PH[d][:], SCR["phi1"][m], writes=[PHB[d]])
        kb.dma("q", XB[d][:], SCR["xn2cm"][m], writes=[XBB[d]])
        ph.scan("v", B_t[:], zb_t[:], PH[d][:], 0.0, ALU.add, ALU.add,
                [zb_b, PHB[d]], [B_b])
        ph.ts("v", B_t[:], B_t[:], p08_t[:, m:m + 1], None, ALU.subtract, ALU.bypass,
              [B_b, p08_b], [B_b])
        ph.ts("v", A_t[:], B_t[:], 1.0 / TWO_PI, MAGIC, ALU.mult, ALU.add, [B_b], [A_b])
        ph.ts("v", A_t[:], A_t[:], MAGIC, TWO_PI, ALU.subtract, ALU.mult, [A_b], [A_b])
        ph.tt("v", B_t[:], B_t[:], A_t[:], ALU.subtract, [B_b, A_b], [B_b])
        if "dbgA" in DEBUG_TAPS and m == 0:
            kb.dma("q", SCR["dbgE"], B_t[:], reads=[B_b])
        ph.act(A_t[:], B_t[:], AF.Sin, [B_b], [A_b])
        ph.act(B_t[:], B_t[:], AF.Abs, [B_b], [B_b])
        ph.act(C_t[:], B_t[:], AF.Sin, [B_b], [C_b], scale=-1.0, bias=PI / 2)
        if "dbgA" in DEBUG_TAPS and m == 0:
            kb.dma("q", SCR["dbgA"], A_t[:], reads=[A_b])
            kb.dma("q", SCR["dbgB"], C_t[:], reads=[C_b])
        ph.act(D_t[:], XB[d][:], AF.Identity, [XBB[d]], [D_b])
        ph.tt("v", D_t[:], D_t[:], e4k_t[:], ALU.mult, [D_b, e4k_b], [D_b])
        ph.tt("g", E_t[:], D_t[:], C_t[:], ALU.mult, [D_b, C_b], [E_b])
        ph.tt("g", D_t[:], D_t[:], A_t[:], ALU.mult, [D_b, A_b], [D_b])
        ph.scan("v", B_t[:], zb_t[:], E_t[:], 0.0, ALU.add, ALU.add, [zb_b, E_b], [B_b])
        ph.scan("v", E_t[:], zb_t[:], D_t[:], 0.0, ALU.add, ALU.add, [zb_b, D_b], [E_b])
        if "dbgA" in DEBUG_TAPS and m == 0:
            kb.dma("q", SCR["dbgC"], B_t[:], reads=[B_b])
            kb.dma("q", SCR["dbgD"], E_t[:], reads=[E_b])
        ph.ts("v", B_t[:], B_t[:], la_t[:, m:m + 1], None, ALU.add, ALU.bypass,
              [B_b, la_b], [B_b])
        ph.ts("v", E_t[:], E_t[:], lb_t[:, m:m + 1], None, ALU.subtract, ALU.bypass,
              [E_b, lb_b], [E_b])
        ph.tt("v", B_t[:], C_t[:], B_t[:], ALU.mult, [C_b, B_b], [B_b])
        ph.tt("g", E_t[:], A_t[:], E_t[:], ALU.mult, [A_b, E_b], [E_b])
        ph.tt("v", hbf_t[:], B_t[:], E_t[:], ALU.add, [B_b, E_b], [hbf_b])
        kb.dma("q", SCR["h1"][m], hbf_t[:], reads=[hbf_b])
    ph.close()


def phase_p1c(st):
    nc, kb = st["nc"], st["kb"]
    IN, KC, KCB, SCR, PS, PSB = st["IN"], st["KC"], st["KCB"], st["SCR"], st["PS"], st["PSB"]
    ph = Ph(st, "p1c")
    emtm_t, emtm_b = load_const_local(ph, "EMTM")
    lniw_t, lniw_b = replicate_row(ph, "lniw", IN["L1_attn_ln_in_w"], D, PS, PSB)
    lnib_t, lnib_b = replicate_row(ph, "lnib", IN["L1_attn_ln_in_b"], D, PS, PSB)
    lfw_t, lfw_b = replicate_row(ph, "lfw", IN["L1_attn_ln_ffn_w"], D, PS, PSB)
    lfb_t, lfb_b = replicate_row(ph, "lfb", IN["L1_attn_ln_ffn_b"], D, PS, PSB)
    hcm_t, hcm_b = ph.sb("hcm", [128, J, C], F32)
    ycm_t, ycm_b = ph.sb("ycm", [128, J, C], BF16)
    htm_t, htm_b = ph.sb("htm", [128, TT, D], F32)
    ytm_t, ytm_b = ph.sb("ytm", [128, TT, D], F32)
    xn1_t, xn1_b = ph.sb("xn1", [128, TT, D], F32)
    xm_t, xm_b = ph.sb("xm", [128, TT, D], F32)
    x3_t, x3_b = ph.sb("x3", [128, TT, D], F32)
    x3b_t, x3b_b = ph.sb("x3b", [128, TT, D], BF16)
    cm3_t, cm3_b = ph.sb("cm3", [128, J, C], BF16)
    junk_t, junk_b = ph.sb("junk", [128, D], F32)
    ptr0, ptr0b = ph.psb("ptr0", [128, 512], BF16)
    ptr1, ptr1b = ph.psb("ptr1", [128, 512], BF16)
    for c in range(NCH):
        kb.dma("q", hcm_t[:], cm_slice(SCR["h1"], c), writes=[hcm_b])
        kb.dma("q", ycm_t[:], cm_slice(SCR["y1"], c), writes=[ycm_b])
        kb.dma("q", xn1_t[:], tm_slice(SCR["xn1"], c), writes=[xn1_b])
        for j in range(J):
            bank, bankb = (PS[0], PSB[0]) if j % 2 == 0 else (PS[1], PSB[1])
            for i in range(TT):
                ph.transpose(bank[:, i * 128:(i + 1) * 128], hcm_t[:, j, i * 128:(i + 1) * 128],
                             KC["IDENT_F"][:], [hcm_b, KCB["IDENT_F"]], [bankb])
            for i in range(TT):
                ph.act(htm_t[:, i, j * 128:(j + 1) * 128], bank[:, i * 128:(i + 1) * 128],
                       AF.Identity, [bankb, emtm_b], [htm_b],
                       scale=emtm_t[:, c * 4 + i:c * 4 + i + 1])
        for j in range(J):
            bank, bankb = (ptr0, ptr0b) if j % 2 == 0 else (ptr1, ptr1b)
            for i in range(TT):
                ph.transpose(bank[:, i * 128:(i + 1) * 128], ycm_t[:, j, i * 128:(i + 1) * 128],
                             KC["IDENT_B"][:], [ycm_b, KCB["IDENT_B"]], [bankb])
            for i in range(TT):
                ph.act(ytm_t[:, i, j * 128:(j + 1) * 128], bank[:, i * 128:(i + 1) * 128],
                       AF.Identity, [bankb], [ytm_b])
        mean, rstd, (mean_b, rstd_b) = ln_stats_tm(ph, htm_t, htm_b, junk_t, junk_b, f"c1{c}")
        affine_ln_tm(ph, htm_t, htm_b, htm_t, htm_b, x3_t, x3_b, mean, rstd, mean_b, rstd_b,
                     lniw_t, lniw_b, lnib_t, lnib_b)
        ph.tt("v", xm_t[:].rearrange("p i d -> p (i d)"),
              htm_t[:].rearrange("p i d -> p (i d)"),
              ytm_t[:].rearrange("p i d -> p (i d)"), ALU.mult, [htm_b, ytm_b], [xm_b])
        ph.tt("v", xm_t[:].rearrange("p i d -> p (i d)"),
              xm_t[:].rearrange("p i d -> p (i d)"),
              xn1_t[:].rearrange("p i d -> p (i d)"), ALU.add, [xm_b, xn1_b], [xm_b])
        kb.dma("q", tm_slice(SCR["xmid"], c), xm_t[:], reads=[xm_b])
        mean, rstd, (mean_b, rstd_b) = ln_stats_tm(ph, xm_t, xm_b, junk_t, junk_b, f"c2{c}")
        affine_ln_tm(ph, xm_t, xm_b, x3_t, x3_b, x3_t, x3_b, mean, rstd, mean_b, rstd_b,
                     lfw_t, lfw_b, lfb_t, lfb_b)
        ph.act(x3b_t[:].rearrange("p i d -> p (i d)"), x3_t[:].rearrange("p i d -> p (i d)"),
               AF.Identity, [x3_b], [x3b_b])
        for j in range(J):
            bank, bankb = (ptr0, ptr0b) if j % 2 == 0 else (ptr1, ptr1b)
            for i in range(TT):
                ph.transpose(bank[:, i * 128:(i + 1) * 128], x3b_t[:, i, j * 128:(j + 1) * 128],
                             KC["IDENT_B"][:], [x3b_b, KCB["IDENT_B"]], [bankb])
            ph.copy("v", cm3_t[:, j, :], bank[:], [bankb], [cm3_b])
        kb.dma("q", cm_slice(SCR["xn3cm"], c), cm3_t[:], reads=[cm3_b])
    ph.close()


def phase_p1d(st):
    nc, kb = st["nc"], st["kb"]
    IN, KC, KCB, SCR, PS, PSB = st["IN"], st["KC"], st["KCB"], st["SCR"], st["PS"], st["PSB"]
    ph = Ph(st, "p1d")
    F1_t, F1_b = ph.sb("f1w", [128, J, 4 * D], BF16)
    src1 = IN["L1_attn_fc1_w"].rearrange("(k p) n -> p k n", p=128)
    for k in range(J):
        kb.dma("g", F1_t[:, k, :], src1[:, k, :], writes=[F1_b])
    src2 = IN["L1_attn_fc2_w"].rearrange("(k p) n -> p k n", p=128)
    F2S = []
    for d in range(2):
        F2S.append(ph.sb(f"f2s{d}", [128, 32, 128], BF16))
    f1b_t, f1b_b = load_strip(ph, "f1b", IN["L1_attn_fc1_b"], jn=32)
    f2b_t, f2b_b = load_strip(ph, "f2b", IN["L1_attn_fc2_b"])
    X3, X3B = [], []
    for d in range(2):
        t, b = ph.sb(f"x3s{d}", [128, J, C], BF16); X3.append(t); X3B.append(b)
    ff1_t, ff1_b = ph.sb("ff1", [128, 32, C], BF16)
    ffc_t, ffc_b = ph.sb("ffc", [128, C], F32)
    xm_t, xm_b = ph.sb("xm", [128, TT, D], F32)
    s1_t, s1_b = ph.sb("s1", [128, TT, D], F32)
    for tb in range(NCH):
        d = tb % 2
        kb.dma("q", X3[d][:], cm_slice(SCR["xn3cm"], tb), writes=[X3B[d]])
        kb.dma("q", xm_t[:], tm_slice(SCR["xmid"], tb), writes=[xm_b])
        for mm in range(32):
            p, pb = PS[mm % 2], PSB[mm % 2]
            for k in range(J):
                ph.mm(p[:], F1_t[:, k, mm * 128:(mm + 1) * 128], X3[d][:, k, :],
                      k == 0, k == J - 1, [F1_b, X3B[d]], [pb])
            ph.act(ff1_t[:, mm, :], p[:], AF.Silu, [pb, f1b_b], [ff1_b],
                   bias=f1b_t[:, mm:mm + 1])
        for mo in range(J):
            f2_t, f2_b = F2S[mo % 2]
            for k8 in range(4):
                kb.dma("g", f2_t[:, k8 * 8:(k8 + 1) * 8, :],
                       src2[:, k8 * 8:(k8 + 1) * 8, mo * 128:(mo + 1) * 128], writes=[f2_b])
            p, pb = PS[2 + mo % 2], PSB[2 + mo % 2]
            for mm in range(32):
                ph.mm(p[:], f2_t[:, mm, :], ff1_t[:, mm, :],
                      mm == 0, mm == 31, [f2_b, ff1_b], [pb])
            ph.act(ffc_t[:], p[:], AF.Identity, [pb, f2b_b], [ffc_b],
                   bias=f2b_t[:, mo:mo + 1])
            bank, bankb = PS[4 + mo % 2], PSB[4 + mo % 2]
            for i in range(TT):
                ph.transpose(bank[:, i * 128:(i + 1) * 128], ffc_t[:, i * 128:(i + 1) * 128],
                             KC["IDENT_F"][:], [ffc_b, KCB["IDENT_F"]], [bankb])
            for i in range(TT):
                ph.tt("v", s1_t[:, i, mo * 128:(mo + 1) * 128], bank[:, i * 128:(i + 1) * 128],
                      xm_t[:, i, mo * 128:(mo + 1) * 128], ALU.add, [bankb, xm_b], [s1_b])
        kb.dma("q", tm_slice(SCR["s1"], tb), s1_t[:], reads=[s1_b])
    ph.close()


def phase_p1e(st):
    nc, kb = st["nc"], st["kb"]
    IN, KC, KCB, SCR, PS, PSB = st["IN"], st["KC"], st["KCB"], st["SCR"], st["PS"], st["PSB"]
    OUT = st["OUT"]
    pha = Ph(st, "p1ew")
    wl_t, wl_b = pha.sb("wl", [128, 4, D], BF16)
    stg_t, stg_b = pha.sb("stg", [128, J, C], BF16)
    dnl_t, dnl_b = pha.sb("dnl", [128, J, C], BF16)
    dstg_t, dstg_b = pha.sb("dstg", [128, 4, D], BF16)
    ptw, ptwb = pha.psb("ptw", [128, 512], BF16)
    for e in range(8):
        for which, dst in (("L1_ffn_gate_proj", SCR["gptc"]), ("L1_ffn_up_proj", SCR["uptc"])):
            kb.dma("g", wl_t[:], IN[which][e].rearrange("(ic p) d -> p ic d", p=128),
                   writes=[wl_b])
            for ic in range(4):
                for dk in range(J):
                    pha.transpose(ptw[:, (dk % 4) * 128:(dk % 4 + 1) * 128],
                                  wl_t[:, ic, dk * 128:(dk + 1) * 128],
                                  KC["IDENT_B"][:], [wl_b, KCB["IDENT_B"]], [ptwb])
                    pha.copy("v", stg_t[:, dk, ic * 128:(ic + 1) * 128],
                             ptw[:, (dk % 4) * 128:(dk % 4 + 1) * 128], [ptwb], [stg_b])
            kb.dma("q", dst[e], stg_t[:], reads=[stg_b])
        kb.dma("g", dnl_t[:], IN["L1_ffn_down_proj"][e].rearrange("(hk p) i -> p hk i", p=128),
               writes=[dnl_b])
        for hk in range(J):
            for ic in range(4):
                pha.transpose(ptw[:, ic * 128:(ic + 1) * 128],
                              dnl_t[:, hk, ic * 128:(ic + 1) * 128],
                              KC["IDENT_B"][:], [dnl_b, KCB["IDENT_B"]], [ptwb])
                pha.copy("v", dstg_t[:, ic, hk * 128:(hk + 1) * 128],
                         ptw[:, ic * 128:(ic + 1) * 128], [ptwb], [dstg_b])
        kb.dma("q", SCR["dntc"][e], dstg_t[:], reads=[dstg_b])
    pha.close()
    kb.barrier()
    if P1E_LIMIT == "prep":
        return
    ph = Ph(st, "p1e")
    sel8_t, sel8_b = load_const_local(ph, "SEL8")
    gwT_t, gwT_b = ph.sb("gwT", [128, J, 8], BF16)
    gwsrc = IN["L1_ffn_gate_weight"].rearrange("e (k p) -> p k e", p=128)
    for k in range(J):
        kb.dma("g", gwT_t[:, k, :], gwsrc[:, k, :], writes=[gwT_b], slow_ok=True)
    brep_t, brep_b = replicate_row(ph, "brep", IN["L1_ffn_e_score_correction_bias"], 8, PS, PSB)
    DN_t, DN_b = ph.sb("dnw", [128, 8, 4, D], BF16)
    for e in range(8):
        kb.dma("g", DN_t[:, e, :, :], SCR["dntc"][e], writes=[DN_b])
    X_t, X_b = ph.sb("xin", [128, J, C], BF16)
    GP_t, GP_b = ph.sb("gpw", [128, J, C], BF16)
    UP_t, UP_b = ph.sb("upw", [128, J, C], BF16)
    hw_t, hw_b = ph.sb("hw", [128, 8, 4, C], BF16)
    glog_t, glog_b = ph.sb("glog", [128, 32], F32)
    SC_t, SC_b = ph.sb("gsc", [128, 32], F32)
    SFC_t, SFC_b = ph.sb("gsfc", [128, TT, 8], F32)
    GS_t, GS_b = ph.sb("ggs", [128, TT, 4], F32)
    T8_t, T8_b = ph.sb("gt8", [128, TT, 8], F32)
    T8b_t, T8b_b = ph.sb("gt8b", [128, TT, 8], F32)
    OH1_t, OH1_b = ph.sb("goh1", [128, TT, 8], F32)
    OH2_t, OH2_b = ph.sb("goh2", [128, TT, 8], F32)
    SMK_t, SMK_b = ph.sb("gsmk", [128, TT, 4, 2], F32)
    mx_t, mx_b = ph.sb("gmx", [128, TT, 4], F32)
    w12_t, w12_b = ph.sb("gw12", [128, TT, 4], F32)
    CW_t, CW_b = ph.sb("gcw", [128, TT, 8], F32)
    wrow_t, wrow_b = ph.sb("wrow", [8, C], F32)
    tmpf_t, tmpf_b = ph.sb("tmpf", [128, C], F32)
    g1b_t, g1b_b = ph.sb("g1b", [128, C], BF16)
    x1_t, x1_b = ph.sb("x1", [128, TT, D], F32)
    s1_t, s1_b = ph.sb("s1", [128, TT, D], F32)
    ob_t, ob_b = ph.sb("ob", [128, TT, D], F32)
    ptg, ptgb = ph.psb("ptg", [128, 512], BF16)
    for tb in range(NCH):
        kb.dma("q", X_t[:], cm_slice(SCR["xn1cm"], tb), writes=[X_b])
        kb.dma("q", x1_t[:], tm_slice(SCR["x1"], tb), writes=[x1_b])
        kb.dma("q", s1_t[:], tm_slice(SCR["s1"], tb), writes=[s1_b])
        pg, pgb = PS[4], PSB[4]
        for i in range(TT):
            for k in range(J):
                ph.mm(pg[:, i * 8:(i + 1) * 8], X_t[:, k, i * 128:(i + 1) * 128],
                      gwT_t[:, k, :], k == 0, k == J - 1, [X_b, gwT_b], [pgb])
        ph.act(glog_t[:], pg[:, 0:32], AF.Identity, [pgb], [glog_b])
        ph.act(SC_t[:], glog_t[:], AF.Sigmoid, [glog_b], [SC_b])
        kb.strong = True
        for i in range(TT):
            ph.tt("v", SFC_t[:, i, :], SC_t[:, i * 8:(i + 1) * 8], brep_t[:], ALU.add,
                  [SC_b, brep_b], [SFC_b])
        for i in range(TT):
            sfc2 = SFC_t[:, i, :].rearrange("p (g two) -> p g two", two=2)
            ph.tt("v", GS_t[:, i, :], sfc2[:, :, 0:1].rearrange("p g o -> p (g o)"),
                  sfc2[:, :, 1:2].rearrange("p g o -> p (g o)"), ALU.add, [SFC_b], [GS_b])
        for i in range(TT):
            ph.red(mx_t[:, i, 0:1], GS_t[:, i, :], ALU.max, [GS_b], [mx_b])
        for i in range(TT):
            ph.ts("v", T8_t[:, i, 0:4], GS_t[:, i, :], mx_t[:, i, 0:1], None, ALU.is_ge,
                  ALU.bypass, [GS_b, mx_b], [T8_b])
        for i in range(TT):
            ph.stt("v", T8_t[:, i, 4:8], T8_t[:, i, 0:4], -1e9, GS_t[:, i, :],
                   ALU.mult, ALU.add, [T8_b, GS_b], [T8_b])
        for i in range(TT):
            ph.red(mx_t[:, i, 1:2], T8_t[:, i, 4:8], ALU.max, [T8_b], [mx_b])
        for i in range(TT):
            ph.ts("v", GS_t[:, i, :], GS_t[:, i, :], mx_t[:, i, 1:2], None, ALU.is_ge,
                  ALU.bypass, [GS_b, mx_b], [GS_b])
        for i in range(TT):
            ph.copy("v", SMK_t[:, i, :, 0:1].rearrange("p g o -> p (g o)"), GS_t[:, i, :],
                    [GS_b], [SMK_b])
        for i in range(TT):
            ph.copy("v", SMK_t[:, i, :, 1:2].rearrange("p g o -> p (g o)"), GS_t[:, i, :],
                    [GS_b], [SMK_b])
        for i in range(TT):
            ph.tt("v", T8_t[:, i, :], SFC_t[:, i, :],
                  SMK_t[:, i, :, :].rearrange("p g o -> p (g o)"), ALU.mult,
                  [SFC_b, SMK_b], [T8_b])
        for i in range(TT):
            ph.red(mx_t[:, i, 2:3], T8_t[:, i, :], ALU.max, [T8_b], [mx_b])
        for i in range(TT):
            ph.ts("v", T8b_t[:, i, :], T8_t[:, i, :], mx_t[:, i, 2:3], None, ALU.is_ge,
                  ALU.bypass, [T8_b, mx_b], [T8b_b])
        for i in range(TT):
            ph.stt("v", T8b_t[:, i, :], T8b_t[:, i, :], -1e6, KC["C8P"][:], ALU.mult,
                   ALU.add, [T8b_b, KCB["C8P"]], [T8b_b])
        for i in range(TT):
            ph.red(mx_t[:, i, 3:4], T8b_t[:, i, :], ALU.min, [T8b_b], [mx_b])
        for i in range(TT):
            ph.ts("v", OH1_t[:, i, :], KC["C_IOTA8"][:], mx_t[:, i, 3:4], None,
                  ALU.is_equal, ALU.bypass, [KCB["C_IOTA8"], mx_b], [OH1_b])
        for i in range(TT):
            ph.stt("v", T8_t[:, i, :], OH1_t[:, i, :], -1e9, T8_t[:, i, :], ALU.mult,
                   ALU.add, [OH1_b, T8_b], [T8_b])
        for i in range(TT):
            ph.red(mx_t[:, i, 0:1], T8_t[:, i, :], ALU.max, [T8_b], [mx_b])
        for i in range(TT):
            ph.ts("v", T8b_t[:, i, :], T8_t[:, i, :], mx_t[:, i, 0:1], None, ALU.is_ge,
                  ALU.bypass, [T8_b, mx_b], [T8b_b])
        for i in range(TT):
            ph.stt("v", T8b_t[:, i, :], T8b_t[:, i, :], -1e6, KC["C8P"][:], ALU.mult,
                   ALU.add, [T8b_b, KCB["C8P"]], [T8b_b])
        for i in range(TT):
            ph.red(mx_t[:, i, 1:2], T8b_t[:, i, :], ALU.min, [T8b_b], [mx_b])
        for i in range(TT):
            ph.ts("v", OH2_t[:, i, :], KC["C_IOTA8"][:], mx_t[:, i, 1:2], None,
                  ALU.is_equal, ALU.bypass, [KCB["C_IOTA8"], mx_b], [OH2_b])
        for i in range(TT):
            ph.tt("v", OH1_t[:, i, :], OH1_t[:, i, :], OH2_t[:, i, :], ALU.add,
                  [OH1_b, OH2_b], [OH1_b])
        for i in range(TT):
            ph.tt("v", T8b_t[:, i, :], SC_t[:, i * 8:(i + 1) * 8], OH1_t[:, i, :],
                  ALU.mult, [SC_b, OH1_b], [T8b_b])
        for i in range(TT):
            ph.red(w12_t[:, i, 0:1], T8b_t[:, i, :], ALU.add, [T8b_b], [w12_b])
        for i in range(TT):
            ph.ts("v", w12_t[:, i, 1:2], w12_t[:, i, 0:1], 1e-20, None, ALU.add,
                  ALU.bypass, [w12_b], [w12_b])
        for i in range(TT):
            def mkrec(eng, o=w12_t[:, i, 2:3], ii=w12_t[:, i, 1:2]):
                return eng.reciprocal(o, ii)
            kb.op("v", mkrec, reads=[w12_b], writes=[w12_b])
        for i in range(TT):
            ph.ts("v", CW_t[:, i, :], T8b_t[:, i, :], w12_t[:, i, 2:3], None, ALU.mult,
                  ALU.bypass, [T8b_b, w12_b], [CW_b])
        kb.strong = False
        if P1E_LIMIT == "gate":
            if "dbg" in DEBUG_TAPS:
                kb.dma("q", SCR["dbg"][:, tb * 8:(tb + 1) * 8], CW_t[:, 0, :], reads=[CW_b])
            continue
        pw, pwb = PS[5], PSB[5]
        for i in range(TT):
            ph.transpose(pw[0:8, i * 128:(i + 1) * 128], CW_t[:, i, :], KC["IDENT_F"][:],
                         [CW_b, KCB["IDENT_F"]], [pwb])
        ph.act(wrow_t[:], pw[0:8, 0:512], AF.Identity, [pwb], [wrow_b])
        for e in range(8):
            kb.dma("g", GP_t[:], SCR["gptc"][e], writes=[GP_b])
            kb.dma("g", UP_t[:], SCR["uptc"][e], writes=[UP_b])
            prep, prepb = PS[4], PSB[4]
            def mkrep(eng, o=prep[:, 0:512], l=sel8_t[0:8, e * 128:(e + 1) * 128],
                      r=wrow_t[0:8, :]):
                return eng.matmul(o, l, r, start=True, stop=True)
            kb.op("t", mkrep, reads=[sel8_b, wrow_b], writes=[prepb])
            for ic in range(4):
                pg_, pg_b = PS[0 + ic % 2], PSB[0 + ic % 2]
                pu_, pu_b = PS[2 + ic % 2], PSB[2 + ic % 2]
                for k in range(J):
                    ph.mm(pg_[:], GP_t[:, k, ic * 128:(ic + 1) * 128], X_t[:, k, :],
                          k == 0, k == J - 1, [GP_b, X_b], [pg_b])
                for k in range(J):
                    ph.mm(pu_[:], UP_t[:, k, ic * 128:(ic + 1) * 128], X_t[:, k, :],
                          k == 0, k == J - 1, [UP_b, X_b], [pu_b])
                ph.act(tmpf_t[:], pg_[:], AF.Silu, [pg_b], [tmpf_b])
                ph.tt("v", tmpf_t[:], tmpf_t[:], pu_[:], ALU.mult, [tmpf_b, pu_b], [tmpf_b])
                ph.tt("v", hw_t[:, e, ic, :], tmpf_t[:], prep[:, 0:512], ALU.mult,
                      [tmpf_b, prepb], [hw_b])
        if P1E_LIMIT == "experts":
            continue
        for mo in range(J):
            py, pyb = PS[mo % 2], PSB[mo % 2]
            for e in range(8):
                for ic in range(4):
                    ph.mm(py[:], DN_t[:, e, ic, mo * 128:(mo + 1) * 128], hw_t[:, e, ic, :],
                          (e == 0 and ic == 0), (e == 7 and ic == 3), [DN_b, hw_b], [pyb])
            ph.act(g1b_t[:], py[:], AF.Silu, [pyb], [g1b_b])
            for i in range(TT):
                ph.transpose(ptg[:, i * 128:(i + 1) * 128], g1b_t[:, i * 128:(i + 1) * 128],
                             KC["IDENT_B"][:], [g1b_b, KCB["IDENT_B"]], [ptgb])
            for i in range(TT):
                ph.tt("v", ob_t[:, i, mo * 128:(mo + 1) * 128], ptg[:, i * 128:(i + 1) * 128],
                      s1_t[:, i, mo * 128:(mo + 1) * 128], ALU.mult, [ptgb, s1_b], [ob_b])
                ph.tt("v", ob_t[:, i, mo * 128:(mo + 1) * 128],
                      ob_t[:, i, mo * 128:(mo + 1) * 128],
                      x1_t[:, i, mo * 128:(mo + 1) * 128], ALU.add, [ob_b, x1_b], [ob_b])
        kb.dma("q", tm_slice(OUT, tb), ob_t[:], reads=[ob_b])
    ph.close()


def build(n_phases=99):
    nc, kb, IN, CONST, OUT, SCR = build_program()
    A = nc.alloc_sbuf_tensor
    P = nc.alloc_psum_tensor
    LOCAL_CONSTS = {"TRAMP4K", "E0P8", "SEL8", "E4K", "EMTM"}
    sb_const = {}
    for name, (shape, dt) in CONST_SPECS.items():
        if name in LOCAL_CONSTS:
            continue
        sb_const[name] = A("k_" + name, shape, dt)
        kb.dma("q", sb_const[name][:], CONST[name], writes=[kb.buf("k_" + name)])
    st_const = CONST
    KC = sb_const
    KCB = {name: kb.buf("k_" + name) for name in CONST_SPECS}
    for k, v in enumerate(CPOOL_VALS):
        nc.const_aps.register(F32, float(v), sb_const["CPOOL"][:, k:k + 1])
    PS = [P(f"ps{i}", [128, 512], F32) for i in range(6)]
    PSB = [kb.buf(f"ps{i}") for i in range(6)]
    st = dict(nc=nc, kb=kb, IN=IN, KC=KC, KCB=KCB, OUT=OUT, SCR=SCR, A=A, PS=PS, PSB=PSB,
              CONST=CONST)

    phase_p0a1(st)
    if n_phases >= 2:
        kb.barrier(); phase_p0a2(st)
    if n_phases >= 3:
        kb.barrier(); phase_p0b(st)
    if n_phases >= 4:
        kb.barrier(); phase_p1a(st)
    if n_phases >= 5:
        kb.barrier(); phase_p1b(st)
    if n_phases >= 6:
        kb.barrier(); kb.seq = SEQ_P1C; phase_p1c(st); kb.seq = False
    if n_phases >= 7:
        kb.barrier(); kb.seq = SEQ_P1D; phase_p1d(st); kb.seq = False
    if n_phases >= 8:
        kb.barrier(); kb.seq = SEQ_P1E; phase_p1e(st); kb.seq = False
    kb.barrier()
    kb.finalize_and_emit()
    return nc


def kernel(**inputs):
    x_full = inputs["L0_attn_x"]
    consts = host_constants()
    nc = build(PHASES)
    base = {}
    for k, v in inputs.items():
        if k == "L0_attn_x":
            continue
        base[k] = np.ascontiguousarray(v, dtype=np.float32)
    import ml_dtypes
    for name, (shape, dt) in CONST_SPECS.items():
        arr = consts[name]
        if dt == BF16:
            arr = arr.astype(ml_dtypes.bfloat16)
        base["K_" + name] = np.ascontiguousarray(arr)
    in_maps = []
    for i in range(8):
        m = dict(base)
        m["x"] = np.ascontiguousarray(x_full[i], dtype=np.float32)
        in_maps.append(m)
    res = run_bass_kernel_spmd(nc, in_maps, core_ids=list(range(8)))
    global LAST_RESULTS
    LAST_RESULTS = res
    out = np.stack([np.asarray(res.results[i].get("out", np.zeros((T, D), np.float32)))
                    for i in range(8)])
    return out.astype(np.float32)


LAST_RESULTS = None
